# version 03
# baseline (speedup 1.0000x reference)
import numpy as np
import ml_dtypes
import concourse.bass as bass
import concourse.mybir as mybir
from concourse.bass_utils import run_bass_kernel_spmd

F32 = mybir.dt.float32
BF16 = mybir.dt.bfloat16
FP8 = mybir.dt.float8e4
AF = mybir.ActivationFunctionType
ALU = mybir.AluOpType
AX = mybir.AxisListType
BF = ml_dtypes.bfloat16

T = 1024
TF = T + 3
NCH = 8
D = 2048
KT = 16
NH = 32
DS = 64
HD = 64
FIN = 65
TAPE_G = 3072
NRING = 2
EPS_LN = 1e-5
EPS_RMS = 1e-5
EPS_FP8 = 1e-6
MSKNEG = -1e30

def _mm(out, l, r, st, sp):
    return lambda t: t.matmul(out, l, r, start=st, stop=sp)

def _tp(out, in_, idm):
    return lambda t: t.transpose(out, in_, idm)

def _cp(dst, src):
    return lambda e: e.tensor_copy(dst, src)

def _acp(dst, src):
    return lambda s: s.copy(dst, src)

def _act(dst, src, f, bias=0.0, scale=1.0):
    return lambda s: s.activation(dst, src, f, bias=bias, scale=scale)

def _tt(dst, a, b, op):
    return lambda v: v.tensor_tensor(dst, a, b, op)

def _ts(dst, a, s1, s2, op0, op1=ALU.bypass, acc=None):
    if s2 is None:
        return lambda v: v.tensor_scalar(dst, a, s1, None, op0, accum_out=acc)
    return lambda v: v.tensor_scalar(dst, a, s1, s2, op0, op1, accum_out=acc)

def _stt(dst, a, sc, b, op0, op1):
    return lambda v: v.scalar_tensor_tensor(dst, a, sc, b, op0, op1)

def _red(dst, src, op, absval=False):
    return lambda v: v.tensor_reduce(dst, src, AX.X, op,
                                     apply_absolute_value=absval)

def _bn(dst, src):
    return lambda v: v.bn_stats(dst, src)

def _bna(dst, src):
    return lambda v: v.bn_aggr(dst, src)

def _memset(dst, c):
    return lambda v: v.memset(dst, c)

def _recip(dst, src):
    return lambda v: v.reciprocal(dst, src)


class Prog:
    ENGS = ("sync", "scalar", "vector", "tensor", "gpsimd")

    def __init__(self, nc):
        self.nc = nc
        self.steps = {e: [] for e in self.ENGS}
        self.semctx = []
        self.sem = {}
        for k in self.ENGS + ("dsync", "dscal", "dgp"):
            cm = nc.semaphore("sem_" + k)
            self.sem[k] = cm.__enter__()
            self.semctx.append(cm)
        self.cnt = {k: 0 for k in self.sem}
        self.last_wait = {e: {k: 0 for k in self.sem} for e in self.ENGS}

    def wait(self, e, ev):
        if ev is None:
            return
        sem_name, val = ev
        if val <= 0:
            return
        if self.last_wait[e][sem_name] >= val:
            return
        self.last_wait[e][sem_name] = val
        s = self.sem[sem_name]
        self.steps[e].append(lambda eng, s=s, val=val: eng.wait_ge(s, val))

    SELFSYNC = ("vector", "scalar")

    def run(self, e, fn):
        if e in self.SELFSYNC:
            self.inc(e, fn)
        else:
            self.steps[e].append(fn)

    def inc(self, e, fn, n=1):
        if e in self.SELFSYNC:
            self.wait(e, (e, self.cnt[e]))
        self.cnt[e] += n
        s, v = self.sem[e], self.cnt[e]
        self.steps[e].append(lambda eng, fn=fn, s=s, n=n: fn(eng).then_inc(s, n))
        return (e, v)

    def nop_inc(self, e, n=1):
        if e in self.SELFSYNC:
            self.wait(e, (e, self.cnt[e]))
        self.cnt[e] += n
        s, v = self.sem[e], self.cnt[e]
        self.steps[e].append(lambda eng, s=s, n=n: eng.nop().then_inc(s, n))
        return (e, v)

    def get_sem(self, name):
        if name not in self.sem:
            cm = self.nc.semaphore("sem_" + name)
            self.sem[name] = cm.__enter__()
            self.semctx.append(cm)
            self.cnt[name] = 0
            for e in self.ENGS:
                self.last_wait[e][name] = 0
        return self.sem[name]

    def dma(self, q, out, in_, ev=None, sem=None, **kw):
        qs = sem or {"sync": "dsync", "scalar": "dscal", "gpsimd": "dgp"}[q]
        s = self.get_sem(qs)
        if ev is not None:
            self.wait(q, ev)
        self.cnt[qs] += 16
        v = self.cnt[qs]
        def fn(eng, out=out, in_=in_, s=s, kw=kw):
            eng.dma_start(out=out, in_=in_, **kw).then_inc(s, 16)
        self.steps[q].append(fn)
        return (qs, v)

    def ev(self, e):
        return (e, self.cnt.get(e, 0))

    def phase_start(self):
        self.barrier(("scalar", "vector"), ("tensor",))

    def full_barrier(self, waiters=("tensor", "vector", "scalar", "sync")):
        prods = [k for k in self.cnt if self.cnt[k] > 0]
        for w in waiters:
            for p in prods:
                if p != w:
                    self.wait(w, (p, self.cnt[p]))

    def barrier(self, producers, waiters):
        evs = [self.ev(p) for p in producers if p in self.cnt]
        for w in waiters:
            for ev in evs:
                self.wait(w, ev)

    def build(self, block):
        for e in self.ENGS:
            steps = self.steps[e]
            def runner(eng, steps=steps):
                for fn in steps:
                    fn(eng)
            getattr(block, e)(runner)


class Tape:
    def __init__(self):
        self.blocks = []
        self.groups = 0
        self.used = 0
        self.pend = []

    def add(self, blk):
        w = blk.shape[1]
        assert blk.shape[0] == 128 and w <= TAPE_G, blk.shape
        if self.used + w > TAPE_G:
            self._close()
        g, off = self.groups, self.used
        self.pend.append(np.ascontiguousarray(blk, dtype=np.float32))
        self.used += w
        return g, off, w

    def _close(self):
        if self.used < TAPE_G:
            self.pend.append(np.zeros((128, TAPE_G - self.used), np.float32))
        self.blocks.extend(self.pend)
        self.pend = []
        self.groups += 1
        self.used = 0

    def finish(self):
        if self.used or not self.blocks:
            self._close()
        return np.concatenate(self.blocks, axis=1)


class MK:

    def __init__(self, nc, P):
        self.nc = nc
        self.P = P
        self.tape = Tape()
        self.tape_cons = {}
        self.cur_grp = -1
        self.host = {}
        self.t = {}
        self._stat_evac = None
        self.pb6_free = None
        self.pb7_free = None

    def S(self, name, shape, dt=F32, side=None):
        self._scnt = getattr(self, "_scnt", {})
        n = self._scnt.get(name, 0)
        self._scnt[name] = n + 1
        bname = name if n == 0 else "%s_%d" % (name, n)
        cm = (self.nc.sbuf_tensor(bname, list(shape), dt, side=side)
              if side else self.nc.sbuf_tensor(bname, list(shape), dt))
        h = cm.__enter__()
        self.t[name] = h
        self._cms = getattr(self, "_cms", {})
        self._cms[name] = cm
        return h

    def free(self, *names):
        for n in names:
            self._cms[n].__exit__(None, None, None)
            del self._cms[n]
            del self.t[n]

    def tt_tile(self, blk):
        g, off, w = self.tape.add(blk)
        P = self.P
        if g != self.cur_grp:
            cnow = P.cnt.get("tensor", 0)
            if self.cur_grp >= 0:
                self.tape_cons[self.cur_grp] = ("tensor", cnow + 1)
                assert g < NRING or cnow > self._sw_cnt, \
                    "no tensor inc between tape group switches"
            self._sw_cnt = cnow
            self.cur_grp = g
            P.get_sem("tp%d" % (g % NRING)) and None
            P.wait("tensor", ("tp%d" % (g % NRING), 16 * (g // NRING + 1)))
        ring = self.t["RING%d" % (g % NRING)]
        return ring[:, off:off + w]

    def tape_finalize(self, tape_dram_ap):
        P = self.P
        if self.cur_grp >= 0 and self.cur_grp not in self.tape_cons:
            self.tape_cons[self.cur_grp] = P.ev("tensor")
        tape_np = self.tape.finish()
        ngrp = self.tape.groups
        for g in range(ngrp):
            if g >= NRING:
                P.wait("gpsimd", self.tape_cons[g - NRING])
            ring = self.t["RING%d" % (g % NRING)]
            P.dma("gpsimd", ring[:], tape_dram_ap[:, g * TAPE_G:(g + 1) * TAPE_G],
                  sem="tp%d" % (g % NRING))
        return tape_np

    def setup_consts(self, ins, inp):
        P = self.P
        iv, jv = np.meshgrid(np.arange(128), np.arange(128), indexing="ij")
        ones_r = np.ones((1, 128), np.float32)
        cdefs = {
            "c_ones_row": (np.ones((1, 128)), BF16, (1, 128)),
            "c_ones_col": (np.ones((128, 1)), BF16, (128, 1)),
            "c_oz_row": (np.full((1, 128), 1.0 / D), BF16, (1, 128)),
            "c_onesf": (np.ones((128, 128)), BF16, (128, 128)),
            "c_negf": (-np.ones((128, 128)), BF16, (128, 128)),
            "c_ut": ((iv <= jv).astype(np.float32), BF16, (128, 128)),
            "c_sln": (-(iv > jv).astype(np.float32), BF16, (128, 128)),
            "c_id": (np.eye(128), BF16, (128, 128)),
            "c_utx4n": (np.tile(-(iv <= jv).astype(np.float32), (1, 4)), BF16, (128, 512)),
            "c_msk4": (np.tile(np.where(jv >= iv, 0.0, MSKNEG), (1, 4)), BF16, (128, 512)),
        }
        n1w = np.asarray(inp["L0_attn_norm1_w"], np.float32).reshape(16, 128).T
        n2w = np.asarray(inp["L0_attn_norm2_w"], np.float32).reshape(16, 128).T
        zbv = np.asarray(inp["L0_attn_z_bias"], np.float32).reshape(16, 128).T
        cbv = np.asarray(inp["L0_attn_conv_b"], np.float32).reshape(48, 128).T
        dcol = np.broadcast_to(np.asarray(inp["L0_attn_D"], np.float32)[None, :], (128, 32))
        bias = np.asarray(inp["L0_ffn_bias"]).astype(np.float32).reshape(16, 128).T
        cdefs.update({
            "n1w": (n1w, F32, (128, 16)), "n2w": (n2w, F32, (128, 16)),
            "zbv": (zbv, F32, (128, 16)), "cbv": (cbv, F32, (128, 48)),
            "dcol": (dcol, F32, (128, 32)), "biasf": (bias, F32, (128, 16)),
        })
        cvals = [0.0, 1.0, 1e-5, 1.0 / D, -0.5, -1.0, 0.5]
        self.host["cvals"] = np.tile(np.asarray([cvals], np.float32), (128, 1))
        self.S("cvals_s", (128, 8), F32)
        self.const_ev = {}
        for nm, (arr, dt, shape) in cdefs.items():
            npdt = BF if dt == BF16 else np.float32
            self.host[nm] = np.ascontiguousarray(arr, dtype=npdt).reshape(shape)
            self.S(nm + "_s", shape, dt)
            self.t[nm] = self.t[nm + "_s"]
            self.const_ev[nm] = P.dma("sync", self.t[nm][:], ins[nm], sem="cbulk")
        cv_ev = P.dma("sync", self.t["cvals_s"][:, 0:7], ins["cvals"], sem="cbulk")
        for i, v in enumerate(cvals):
            self.nc.const_aps.register(F32, v, self.t["cvals_s"][:, i:i + 1])
        self.S("selc_s", (64, 1), F32)
        self.t["selc"] = self.t["selc_s"]
        self.const_ev["selc"] = P.dma("sync", self.t["selc"][:], ins["selc"], sem="cbulk")
        self.cev = P.ev("cbulk")
        for nm in list(self.const_ev):
            self.const_ev[nm] = self.cev
        P.wait("scalar", self.cev)
        P.wait("vector", self.cev)

    def stats_grp(self, src_fn, dep_fn, c0, w, s1pair, s2row, scratch):
        P = self.P
        Eb = scratch
        pp = self.pp
        want_sum = s1pair is not None
        mm_prev = None
        for k in range(KT):
            srcap = src_fn(k)
            P.wait("scalar", dep_fn(k))
            P.wait("scalar", mm_prev)
            if want_sum:
                P.run("scalar", _acp(Eb[:, 0:w], srcap))
            ev_sq = P.inc("scalar", _act(Eb[:, 512:512 + w], srcap, AF.Square))
            P.wait("tensor", ev_sq)
            P.wait("tensor", self._stat_evac)
            if want_sum:
                P.inc("tensor", _mm(pp[0][0:1, 0:w], self.t["c_ones_col"][:, 0:1],
                                    Eb[:, 0:w], k == 0, k == KT - 1))
            P.inc("tensor", _mm(pp[1][0:1, 0:w], self.t["c_ones_col"][:, 0:1],
                                Eb[:, 512:512 + w], k == 0, k == KT - 1))
            mm_prev = P.ev("tensor")
        P.wait("vector", mm_prev)
        if want_sum:
            P.run("vector", _cp(s1pair[0][0:1, 0:w], pp[0][0:1, 0:w]))
            P.run("vector", _tt(s1pair[1][0:1, 0:w], pp[0][0:1, 0:w],
                                s1pair[0][0:1, 0:w], ALU.subtract))
        ev = P.inc("vector", _cp(s2row[0:1, 0:w], pp[1][0:1, 0:w]))
        self._stat_evac = ev
        return ev

    def row_hilo(self, row, hi, lo, wtot, dep):
        P = self.P
        P.wait("vector", dep)
        P.run("vector", _cp(hi, row))
        return P.inc("vector", _tt(lo, row, hi, ALU.subtract))

    def bcast2(self, lhs_row, hi, lo, psum, w, dep, start=True):
        P = self.P
        P.wait("tensor", dep)
        P.run("tensor", _mm(psum[:, 0:w], lhs_row[0:1, :], hi, start, False))
        return P.inc("tensor", _mm(psum[:, 0:w], lhs_row[0:1, :], lo, False, True))

    def ln_rms(self, x_dram, tag, gate_ev=None):
        P = self.P
        XN, H1T = self.t["XN"], self.t["H1T"]
        RW, RWB = self.t["RW"], self.t["RWB"]
        S1s, XE0 = self.t["S1s"], self.t["XE0"]
        pp = self.pp
        r = lambda i: RW[0:1, i * 512:(i + 1) * 512]
        b = lambda i: RWB[0:1, i * 512:(i + 1) * 512]
        P.phase_start()
        xev = []
        for k in range(KT):
            xev.append(P.dma("sync", XN[:, k * TF:(k + 1) * TF],
                             x_dram[k * 128:(k + 1) * 128, :],
                             ev=(gate_ev if k == 0 else None), sem="xld"))
        gl = [(0, 512), (512, 512), (1024, 3)]
        xall = xev[-1]
        xn_ev = [None] * KT
        prev_v = None
        for (c0, w) in gl:
            sev = self.stats_grp(
                lambda k: XN[:, k * TF + c0:k * TF + c0 + w],
                lambda k: xall, c0, w, (b(0), b(1)), r(0), XE0)
            P.run("vector", _tt(r(1)[0:1, 0:w], b(0)[0:1, 0:w], b(1)[0:1, 0:w],
                                ALU.add))
            P.run("vector", _tt(r(1)[0:1, 0:w], r(1)[0:1, 0:w], r(1)[0:1, 0:w],
                                ALU.mult))
            vev = P.inc("vector", _stt(r(1)[0:1, 0:w], r(1)[0:1, 0:w],
                                       -1.0 / D, r(0)[0:1, 0:w],
                                       ALU.mult, ALU.add))
            P.wait("scalar", vev)
            P.run("scalar", _act(r(1)[0:1, 0:w], r(1)[0:1, 0:w], AF.Ln,
                                 bias=EPS_LN, scale=1.0 / D))
            aev = P.inc("scalar", _act(r(1)[0:1, 0:w], r(1)[0:1, 0:w],
                                       AF.Exp, scale=-0.5))
            hev2 = self.row_hilo(r(1)[0:1, 0:w], b(2)[0:1, 0:w],
                                 b(3)[0:1, 0:w], w, aev)
            P.wait("tensor", hev2)
            P.wait("tensor", prev_v)
            P.run("tensor", _mm(pp[2][:, 0:w], self.t["c_oz_row"][0:1, :],
                                b(0)[0:1, 0:w], True, False))
            P.inc("tensor", _mm(pp[2][:, 0:w], self.t["c_oz_row"][0:1, :],
                                b(1)[0:1, 0:w], False, True))
            P.run("tensor", _mm(pp[3][:, 0:w], self.t["c_ones_row"][0:1, :],
                                b(2)[0:1, 0:w], True, False))
            P.inc("tensor", _mm(pp[3][:, 0:w], self.t["c_ones_row"][0:1, :],
                                b(3)[0:1, 0:w], False, True))
            bev = P.ev("tensor")
            P.wait("vector", bev)
            for k in range(KT):
                xs = XN[:, k * TF + c0:k * TF + c0 + w]
                P.run("vector", _stt(S1s[:, 0:w], xs, 1.0, pp[2][:, 0:w],
                                     ALU.mult, ALU.subtract))
                ev = P.inc("vector", _tt(xs, S1s[:, 0:w], pp[3][:, 0:w],
                                         ALU.mult))
            prev_v = P.ev("vector")
        for k in range(KT):
            xn_ev[k] = prev_v
        h1_ev = [None] * KT
        prev_v2 = None
        for (c0, w) in gl:
            sev1 = self.stats_grp(
                lambda k: XN[:, k * TF + c0:k * TF + c0 + w],
                lambda k: xn_ev[k], c0, w, None, r(0), XE0)
            P.wait("scalar", sev1)
            P.run("scalar", _act(r(1)[0:1, 0:w], r(0)[0:1, 0:w], AF.Ln,
                                 bias=EPS_RMS, scale=1.0 / D))
            aev1 = P.inc("scalar", _act(r(1)[0:1, 0:w], r(1)[0:1, 0:w],
                                        AF.Exp, scale=-0.5))
            hev3 = self.row_hilo(r(1)[0:1, 0:w], b(2)[0:1, 0:w],
                                 b(3)[0:1, 0:w], w, aev1)
            P.wait("tensor", hev3)
            P.wait("tensor", prev_v2)
            P.run("tensor", _mm(pp[3][:, 0:w], self.t["c_ones_row"][0:1, :],
                                b(2)[0:1, 0:w], True, False))
            bev = P.inc("tensor", _mm(pp[3][:, 0:w], self.t["c_ones_row"][0:1, :],
                                      b(3)[0:1, 0:w], False, True))
            P.wait("vector", bev)
            for k in range(KT):
                ev = P.inc("vector", _stt(
                    H1T[:, k * TF + c0:k * TF + c0 + w],
                    XN[:, k * TF + c0:k * TF + c0 + w],
                    self.t["n1w"][:, k:k + 1], pp[3][:, 0:w],
                    ALU.mult, ALU.mult))
            prev_v2 = P.ev("vector")
        for k in range(KT):
            h1_ev[k] = prev_v2
        return xn_ev, h1_ev

    def scan_prepare(self, xfull_dram):
        P = self.P
        self.scan_dmas = []
        XSF = self.t["XSF"]
        for i in range(64):
            for hf in range(2):
                self.scan_dmas.append(
                    (XSF[:, 0:1024],
                     xfull_dram[i * 128:(i + 1) * 128,
                                hf * 1024:(hf + 1) * 1024]))
        self.scan_i = 0
        self.scan_loaded = [P.dma("scalar", *self.scan_dmas[0], sem="scn")]
        P.run("vector", _memset(self.t["STT8"][:, 7:8], 0.0))

    def scan_step(self):
        if getattr(self, "noscan", False):
            return
        if self.scan_i >= 128:
            return
        P = self.P
        i = self.scan_i
        self.scan_i += 1
        hf = i % 2
        slot = self.t["XSF"][:, 0:1024]
        BNO, S8 = self.t["BNO"], self.t["STT8"]
        P.wait("vector", self.scan_loaded[i])
        for j in range(2):
            P.run("vector", _bn(BNO[:, hf * 12 + j * 6:hf * 12 + (j + 1) * 6],
                                slot[:, j * 512:(j + 1) * 512]))
        dst = S8[:, 2 + hf:3 + hf]
        P.run("vector", _red(dst, slot[:], ALU.max))
        ev = P.inc("vector", _red(S8[:, 4 + hf:5 + hf], slot[:], ALU.min))
        if i + 1 < 128:
            self.scan_loaded.append(
                P.dma("scalar", *self.scan_dmas[i + 1], ev=ev, sem="scn"))
        if hf == 0:
            return
        P.run("vector", _bna(S8[:, 0:2], BNO[:, 0:24]))
        P.run("vector", _tt(S8[:, 2:3], S8[:, 2:3], S8[:, 3:4], ALU.max))
        P.run("vector", _tt(S8[:, 4:5], S8[:, 4:5], S8[:, 5:6], ALU.min))
        P.run("vector", _tt(S8[:, 3:4], S8[:, 2:3], S8[:, 0:1], ALU.subtract))
        P.run("vector", _tt(S8[:, 5:6], S8[:, 0:1], S8[:, 4:5], ALU.subtract))
        P.run("vector", _tt(S8[:, 3:4], S8[:, 3:4], S8[:, 5:6], ALU.max))
        P.run("vector", _tt(S8[:, 3:4], S8[:, 3:4], S8[:, 3:4], ALU.mult))
        P.run("vector", _ts(S8[:, 6:7], S8[:, 1:2], EPS_LN, None, ALU.add))
        P.run("vector", _recip(S8[:, 6:7], S8[:, 6:7]))
        P.run("vector", _tt(S8[:, 3:4], S8[:, 3:4], S8[:, 6:7], ALU.mult))
        P.run("vector", _tt(S8[:, 7:8], S8[:, 7:8], S8[:, 3:4], ALU.max))

    def scan_flush(self):
        while self.scan_i < 128:
            self.scan_step()

    def inproj(self, ftiles, h1_ev, WT, conv_diag, scr_dram, scr_base, tag):
        P = self.P
        pp = self.pp
        H1T = self.t["H1T"]
        P.phase_start()
        gl = [(0, 512), (512, 512), (1024, 3)]
        prev_cons = {}
        prev_conv = None
        prev_cvo = None
        prev_xe = [None, None]
        na_done = None
        for fi, f in enumerate(ftiles):
            self.scan_step()
            if getattr(self, "psetlo", False):
                pset = (pp[0], pp[1], pp[2])
            else:
                pset = (pp[4], pp[5], pp[6]) if fi % 2 == 0 else (pp[7], pp[0], pp[1])
            wrows = 32 if f == 64 else 128
            P.wait("tensor", prev_cons.get(fi % 2))
            for k in range(KT):
                wt = self.tt_tile(WT[k * 128:(k + 1) * 128,
                                     f * 128:f * 128 + wrows])
                if k == 0:
                    for gi, (c0, w) in enumerate(gl):
                        P.wait("tensor", h1_ev[k])
                for gi, (c0, w) in enumerate(gl):
                    f1 = _mm(pset[gi][0:wrows, 0:w], wt,
                             H1T[:, k * TF + c0:k * TF + c0 + w],
                             k == 0, k == KT - 1)
                    (P.inc if k == KT - 1 else P.run)("tensor", f1)
            mm_ev = P.ev("tensor")
            if f == 64:
                na_done = self.na_chain(pset, mm_ev, tag)
                prev_cons[fi % 2] = na_done
            elif f < 16:
                P.wait("scalar", mm_ev)
                SZ = self.t["SZ"]
                zmap = [(0, 3, 509), (1, 0, 512), (2, 0, 3)]
                o = 0
                for gi, p0, w in zmap:
                    ev = P.inc("scalar", _act(
                        SZ[:, f * 1024 + o:f * 1024 + o + w],
                        pset[gi][0:128, p0:p0 + w], AF.Silu,
                        bias=self.t["zbv"][:, f:f + 1]))
                    o += w
                prev_cons[fi % 2] = ev
            else:
                par = 0
                XE = self.t["XE0"]
                CVO = self.t["CVO0"]
                P.wait("scalar", mm_ev)
                P.wait("scalar", prev_xe[par])
                for gi, (c0, w) in enumerate(gl):
                    P.run("scalar", _acp(XE[:, c0:c0 + w], pset[gi][0:128, 0:w]))
                xe_ev = P.ev("scalar")
                P.wait("tensor", xe_ev)
                P.wait("tensor", prev_conv)
                for tap in range(4):
                    wd = self.tt_tile(conv_diag(f)[tap])
                    f1 = _mm(pp[2][:, 0:512], wd,
                             XE[:, tap:tap + 512], tap == 0, tap == 3)
                    f2 = _mm(pp[3][:, 0:512], wd,
                             XE[:, 512 + tap:512 + tap + 512],
                             tap == 0, tap == 3)
                    (P.inc if tap == 3 else P.run)("tensor", f1)
                    (P.inc if tap == 3 else P.run)("tensor", f2)
                cv_ev = P.ev("tensor")
                P.wait("scalar", cv_ev)
                P.wait("scalar", prev_cvo)
                ci = f - scr_base
                P.run("scalar", _act(CVO[:, 0:512], pp[2][:, 0:512], AF.Identity,
                                     bias=self.t["cbv"][:, ci:ci + 1]))
                ev2 = P.inc("scalar", _act(CVO[:, 512:1024], pp[3][:, 0:512],
                                           AF.Identity,
                                           bias=self.t["cbv"][:, ci:ci + 1]))
                prev_conv = ev2
                dma_ev = P.dma("sync", scr_dram[ci * 128:(ci + 1) * 128, :],
                               CVO[:, 0:1024], ev=ev2, sem="cvw")
                prev_cvo = dma_ev
                prev_xe[par] = cv_ev
                prev_cons[fi % 2] = xe_ev
        return na_done, prev_cvo

    def na_chain(self, pset, mm_ev, tag):
        P = self.P
        NAH, NAL = self.t["NAH"], self.t["NAL"]
        S2s = self.t["S2s"]
        NAF = self.t["NAF"]
        S1s = self.t["S1s"]
        gl = [(0, 512), (512, 512), (1024, 3)]
        na_stop = getattr(self, "na_stop", None)
        if na_stop == "a":
            return mm_ev
        P.wait("scalar", mm_ev)
        for gi, (c0, w) in enumerate(gl):
            P.run("scalar", _acp(S2s[:, c0:c0 + w], pset[gi][0:128, 0:w]))
        evac_ev = P.ev("scalar")
        if na_stop == "b":
            return evac_ev
        P.wait("vector", evac_ev)
        rel_ev = P.inc("vector", _ts(S1s[0:32, 0:TF], S2s[0:32, 0:TF], 0.0,
                                     None, ALU.max))
        P.wait("scalar", rel_ev)
        P.run("scalar", _act(S2s[0:32, 0:TF], S2s[0:32, 0:TF], AF.Abs))
        P.run("scalar", _act(S2s[0:32, 0:TF], S2s[0:32, 0:TF], AF.Exp, scale=-1.0))
        aev2 = P.inc("scalar", _act(S2s[0:32, 0:TF], S2s[0:32, 0:TF], AF.Ln,
                                    bias=1.0))
        if na_stop == "c":
            return aev2
        P.wait("vector", aev2)
        P.run("vector", _tt(NAF[0:32, 0:T], S1s[0:32, 3:TF],
                            S2s[0:32, 3:TF], ALU.add))
        P.run("vector", _cp(NAH[0:32, 0:T], NAF[0:32, 0:T]))
        ev = P.inc("vector", _tt(NAL[0:32, 0:T], NAF[0:32, 0:T],
                                 NAH[0:32, 0:T], ALU.subtract))
        return ev

    def ssd_chunk_common(self, c, scr, ntiles, dep_scr):
        P = self.P
        XBCC = self.t["XBCC"]
        pb6 = self.pp[6].bitcast(BF16)
        pb7 = self.pp[7].bitcast(BF16)
        ev = {}
        scr_r = scr.rearrange("(j p) t -> p j t", p=128)
        dmas = []
        for q in range(2):
            dmas.append(P.dma(
                "sync",
                XBCC[:, q * 16 * 128:(q + 1) * 16 * 128].rearrange(
                    "p (j t) -> p j t", j=16),
                scr_r[:, q * 16:(q + 1) * 16, c * 128:(c + 1) * 128],
                ev=dep_scr, sem="rbk"))
        if ntiles > 32:
            dmas.append(P.dma(
                "sync",
                self.t["XBCC2"][:].rearrange("p (j t) -> p j t", j=16),
                scr_r[:, 32:48, c * 128:(c + 1) * 128],
                ev=dep_scr, sem="rbk"))
        ev["xbcc"] = dmas[-1]
        P.wait("tensor", dmas[-1])
        P.wait("tensor", self.const_ev["c_id"])
        cid = self.t["c_id"][:]
        pb_free = [self.pb6_free, self.pb7_free]
        for dst_name, base in (("XTOK", 0), ("BTOK", 16)):
            dst = self.t[dst_name]
            for half in range(2):
                pb = pb6 if half == 0 else pb7
                P.wait("tensor", pb_free[half])
                for j in range(8):
                    jt = base + half * 8 + j
                    P.inc("tensor", _tp(pb[:, j * 128:(j + 1) * 128],
                                        XBCC[:, jt * 128:(jt + 1) * 128],
                                        cid))
                tev = P.ev("tensor")
                P.wait("scalar", tev)
                eev = P.inc("scalar", _acp(
                    dst[:, half * 1024:(half + 1) * 1024], pb[:]))
                ev["tp_evac_%s%d" % (dst_name, half)] = eev
                pb_free[half] = eev
        self.pb6_free, self.pb7_free = pb_free
        NAT = self.t["NAT"]
        P.wait("tensor", self.pb6_free)
        P.wait("tensor", self.na_ev)
        P.inc("tensor", _tp(pb6[:, 0:32], self.t["NAH"][0:32, c * 128:(c + 1) * 128],
                            cid[0:32, 0:32]))
        P.inc("tensor", _tp(pb6[:, 32:64],
                            self.t["NAL"][0:32, c * 128:(c + 1) * 128],
                            cid[0:32, 0:32]))
        tev = P.ev("tensor")
        P.wait("scalar", tev)
        nat_ev = P.inc("scalar", _acp(NAT[:], pb6[:, 0:64]))
        ev["nat"] = nat_ev
        self.pb6_free = nat_ev
        pp2 = self.pp[2]
        P.wait("tensor", nat_ev)
        P.run("tensor", _mm(pp2[:, 0:32], self.t["c_sln"][:], NAT[:, 0:32],
                            True, False))
        P.inc("tensor", _mm(pp2[:, 0:32], self.t["c_sln"][:], NAT[:, 32:64],
                            False, True))
        P.run("tensor", _mm(pp2[0:64, 32:64], self.t["c_negf"][:][:, 0:64],
                            NAT[:, 0:32], True, False))
        P.inc("tensor", _mm(pp2[0:64, 32:64], self.t["c_negf"][:][:, 0:64],
                            NAT[:, 32:64], False, True))
        dte_mm = P.ev("tensor")
        P.wait("scalar", dte_mm)
        P.run("scalar", _act(self.t["DTE"][:], pp2[:, 0:32], AF.Exp))
        ev["dteA"] = P.inc("scalar", _act(self.t["ABC"][:], pp2[0:64, 32:64],
                                          AF.Exp))
        return ev

    def ssd_sa_heads(self, ev, upd_prev):
        P = self.P
        XTOK, BTOK = self.t["XTOK"], self.t["BTOK"]
        HALL, SCN = self.t["HALL"], self.t["SCN"]
        psm = (self.pp[2], self.pp[3])
        last_s_cons = {0: ev["dteA"], 1: ev["dteA"]}
        vev = None
        for h in range(NH):
            par = h % 2
            P.wait("vector", ev["dteA"])
            P.wait("vector", ev["tp_evac_BTOK1"])
            bd = self.t["BDB"][:, par * 64:(par + 1) * 64]
            bdev = P.inc("vector", _ts(bd, BTOK[:, h * 64:(h + 1) * 64],
                                       self.t["DTE"][:, h:h + 1], None, ALU.mult))
            P.wait("tensor", bdev)
            P.wait("tensor", last_s_cons[par])
            P.wait("tensor", ev["tp_evac_XTOK1"])
            sev = P.inc("tensor", _mm(psm[par][0:64, 256:320], bd,
                                      XTOK[:, h * 64:(h + 1) * 64], True, True))
            P.wait("vector", sev)
            if upd_prev is not None:
                P.wait("vector", upd_prev)
            P.run("vector", _ts(SCN[0:64, 0:64], HALL[0:64, h * 64:(h + 1) * 64],
                                self.t["ABC"][:, h:h + 1], None, ALU.mult))
            vev = P.inc("vector", _tt(HALL[0:64, h * 64:(h + 1) * 64],
                                      SCN[0:64, 0:64], psm[par][0:64, 256:320],
                                      ALU.add))
            last_s_cons[par] = vev
        return vev

    def ssd_main_heads(self, c, ev, hprev_ev):
        P = self.P
        XTOK, BTOK, XBCC = self.t["XTOK"], self.t["BTOK"], self.t["XBCC"]
        HALL, HPREV, SCN = self.t["HALL"], self.t["HPREV"], self.t["SCN"]
        SEG, SCO, ECUM, CSB = self.t["SEG"], self.t["SCO"], self.t["ECUM"], self.t["CSB"]
        AUT, NAT, YT = self.t["AUT"], self.t["NAT"], self.t["YT"]
        SZ, YG = self.t["SZ"], self.t["YG"]
        pdif = (self.pp[0], self.pp[1])
        psm = (self.pp[2], self.pp[3])
        py = (self.pp[4], self.pp[5])
        cons = {"seg": {}, "sco": {}, "cs": {}, "ecum": {}, "psm": {},
                "py": {}, "aut": {}, "dif": {}}
        scan_prev = None
        for g in range(8):
            gp = g % 2
            P.wait("vector", ev["nat"])
            P.wait("vector", cons["aut"].get(0))
            auth = AUT[:, 0:512]
            autl = AUT[:, 512:1024]
            c4n = self.t["c_utx4n"][:].rearrange("p (a b) -> p a b", a=4)
            nah = NAT[:, g * 4:(g + 1) * 4].unsqueeze(2).to_broadcast([128, 4, 128])
            nal = NAT[:, 32 + g * 4:32 + (g + 1) * 4].unsqueeze(2).to_broadcast([128, 4, 128])
            P.run("vector", _tt(auth.rearrange("p (a b) -> p a b", a=4), c4n,
                                nah, ALU.mult))
            aut_ev = P.inc("vector", _tt(autl.rearrange("p (a b) -> p a b", a=4),
                                         c4n, nal, ALU.mult))
            P.wait("tensor", aut_ev)
            P.wait("tensor", cons["dif"].get(gp))
            P.run("tensor", _mm(pdif[gp][:], self.t["c_ut"][:], nah, True, False))
            P.run("tensor", _mm(pdif[gp][:], self.t["c_ut"][:], nal, False, False))
            P.run("tensor", _mm(pdif[gp][:], self.t["c_onesf"][:], auth, False, False))
            P.run("tensor", _mm(pdif[gp][:], self.t["c_onesf"][:], autl, False, False))
            dif_ev = P.inc("tensor", _mm(pdif[gp][:], self.t["c_id"][:],
                                         self.t["c_msk4"][:], False, True))
            P.wait("scalar", dif_ev)
            P.wait("scalar", cons["seg"].get(gp))
            seg_ev = P.inc("scalar", _act(SEG[:, gp * 512:(gp + 1) * 512],
                                          pdif[gp][:], AF.Exp))
            cons["dif"][gp] = seg_ev
            for hh in range(4):
                h = g * 4 + hh
                par = h % 2
                bh = XBCC[(h % 2) * 64:(h % 2) * 64 + 64,
                          (16 + h // 2) * 128:(17 + h // 2) * 128]
                ch = self.t["XBCC2"][(h % 2) * 64:(h % 2) * 64 + 64,
                                     (h // 2) * 128:(h // 2 + 1) * 128]
                xh = XBCC[(h % 2) * 64:(h % 2) * 64 + 64,
                          (h // 2) * 128:(h // 2 + 1) * 128]
                P.wait("tensor", cons["psm"].get(par, ev["dteA"]))
                mm_cb = P.inc("tensor", _mm(psm[par][0:128, 0:128], bh, ch,
                                            True, True))
                P.run("tensor", _mm(psm[par][0:64, 128:256],
                                    self.t["c_onesf"][:][:, 0:64],
                                    auth[:, hh * 128:(hh + 1) * 128], True, False))
                mm_e = P.inc("tensor", _mm(psm[par][0:64, 128:256],
                                           self.t["c_onesf"][:][:, 0:64],
                                           autl[:, hh * 128:(hh + 1) * 128],
                                           False, True))
                cons["aut"][0] = mm_e
                P.wait("scalar", mm_e)
                P.wait("scalar", cons["ecum"].get(par))
                rb = (h % 2) * 64
                ec_ev = P.inc("scalar", _act(
                    ECUM[rb:rb + 64, par * 128:(par + 1) * 128],
                    psm[par][0:64, 128:256], AF.Exp))
                P.wait("vector", seg_ev)
                P.wait("vector", mm_cb)
                P.wait("vector", cons["sco"].get(par))
                sco_ev = P.inc("vector", _tt(
                    SCO[:, par * 128:(par + 1) * 128],
                    SEG[:, gp * 512 + hh * 128:gp * 512 + (hh + 1) * 128],
                    psm[par][0:128, 0:128], ALU.mult))
                cons["seg"][gp] = sco_ev
                P.wait("vector", ec_ev)
                P.wait("vector", cons["cs"].get(par))
                cs_ev = P.inc("vector", _tt(
                    CSB[rb:rb + 64, par * 128:(par + 1) * 128],
                    ch, ECUM[rb:rb + 64, par * 128:(par + 1) * 128],
                    ALU.mult))
                cons["ecum"][par] = cs_ev
                P.wait("tensor", sco_ev)
                P.wait("tensor", cs_ev)
                P.wait("tensor", hprev_ev)
                P.wait("tensor", cons["py"].get(par))
                P.run("tensor", _mm(py[par][0:64, 0:128],
                                    XTOK[:, h * 64:(h + 1) * 64],
                                    SCO[:, par * 128:(par + 1) * 128], True, False))
                y_mm = P.inc("tensor", _mm(py[par][0:64, 0:128],
                                           HPREV[rb:rb + 64, h * 64:(h + 1) * 64],
                                           CSB[rb:rb + 64, par * 128:(par + 1) * 128],
                                           False, True))
                cons["sco"][par] = y_mm
                cons["cs"][par] = y_mm
                bd = self.t["BDB"][:, par * 64:(par + 1) * 64]
                P.wait("vector", ev["dteA"])
                bd_ev = P.inc("vector", _ts(bd, BTOK[:, h * 64:(h + 1) * 64],
                                            self.t["DTE"][:, h:h + 1], None,
                                            ALU.mult))
                P.wait("tensor", bd_ev)
                s_mm = P.inc("tensor", _mm(psm[par][0:64, 256:320], bd,
                                           XTOK[:, h * 64:(h + 1) * 64],
                                           True, True))
                P.wait("vector", y_mm)
                P.run("vector", _stt(YT[rb:rb + 64, par * 128:(par + 1) * 128],
                                     xh, self.t["dcol"][rb:rb + 64, h:h + 1],
                                     py[par][0:64, 0:128], ALU.mult, ALU.add))
                szsl = SZ[(h % 2) * 64:(h % 2) * 64 + 64,
                          (h // 2) * 1024 + c * 128:(h // 2) * 1024 + (c + 1) * 128]
                yg_ev = P.inc("vector", _tt(
                    YG[rb:rb + 64,
                       (h // 2) * 1024 + c * 128:(h // 2) * 1024 + (c + 1) * 128],
                    YT[rb:rb + 64, par * 128:(par + 1) * 128], szsl, ALU.mult))
                cons["py"][par] = yg_ev
                P.wait("vector", s_mm)
                P.run("vector", _ts(SCN[0:64, 0:64],
                                    HALL[0:64, h * 64:(h + 1) * 64],
                                    self.t["ABC"][:, h:h + 1], None, ALU.mult))
                scan_prev = P.inc("vector", _tt(HALL[0:64, h * 64:(h + 1) * 64],
                                                SCN[0:64, 0:64],
                                                psm[par][0:64, 256:320], ALU.add))
                cons["psm"][par] = scan_prev
        return scan_prev

    def ssd_pass(self, scr_ap, ntiles, main, dep_scr):
        P = self.P
        P.phase_start()
        upd = None
        for c in range(NCH):
            self.scan_step()
            dep = dep_scr[c] if isinstance(dep_scr, list) else dep_scr
            if upd is not None:
                P.barrier(("tensor", "vector", "scalar", "dsync", "dscal"), ("sync",))
            ev = self.ssd_chunk_common(c, scr_ap, ntiles, dep)
            if main:
                P.wait("vector", upd)
                P.run("vector", _cp(self.t["HPREV"][0:64, :], self.t["HALL"][:]))
                hprev_ev = P.inc("vector", _cp(self.t["HPREV"][64:128, :],
                                               self.t["HALL"][:]))
                upd = self.ssd_main_heads(c, ev, hprev_ev)
            else:
                upd = self.ssd_sa_heads(ev, upd)
        return upd

    def outproj(self, WOT, ins, upd):
        P = self.P
        YG = self.t["YG"]
        S1s, S2s, OH = self.t["S1s"], self.t["S2s"], self.t["OH"]
        pp = self.pp
        prev = {}
        prev_dma = {}
        h2_ev = [None] * KT
        P.phase_start()
        P.wait("tensor", upd)
        for do in range(KT):
            par = do % 2
            ps = (pp[0], pp[1]) if par == 0 else (pp[4], pp[5])
            Sb = S1s if par == 0 else S2s
            xrd = P.dma("sync", Sb[:, 0:1024],
                        ins["xn_scr"][do * 128:(do + 1) * 128, :],
                        ev=prev.get(par), sem="xr%d" % par)
            P.wait("tensor", prev.get(par))
            for k in range(KT):
                wt = self.tt_tile(WOT[k * 128:(k + 1) * 128,
                                      do * 128:(do + 1) * 128])
                for half in range(2):
                    f1 = _mm(ps[half][:], wt,
                             YG[:, k * 1024 + half * 512:k * 1024 + (half + 1) * 512],
                             k == 0, k == KT - 1)
                    (P.inc if k == KT - 1 else P.run)("tensor", f1)
            mm_ev = P.ev("tensor")
            P.wait("vector", mm_ev)
            P.wait("vector", xrd)
            P.wait("vector", prev_dma.get(par))
            for half in range(2):
                ev = P.inc("vector", _tt(
                    OH[:, par * 1024 + half * 512:par * 1024 + (half + 1) * 512],
                    ps[half][:], Sb[:, half * 512:(half + 1) * 512], ALU.add))
            prev[par] = ev
            h2_ev[do] = P.dma("sync", ins["h2_scr"][do * 128:(do + 1) * 128, :],
                              OH[:, par * 1024:(par + 1) * 1024], ev=ev,
                              sem="h2w%d" % par)
            prev_dma[par] = h2_ev[do]
        return h2_ev

    def rms2_h3(self, ins, h2_ev):
        P = self.P
        RW, RWB = self.t["RW"], self.t["RWB"]
        S1s, S2s, E6 = self.t["S1s"], self.t["S2s"], self.t["E6"]
        H3 = self.t["H3"]
        pp = self.pp
        r = lambda i: RW[0:1, i * 512:(i + 1) * 512]
        b = lambda i: RWB[0:1, i * 512:(i + 1) * 512]
        P.phase_start()
        gl = [(0, 512), (512, 512)]
        h3_ev = [None] * KT
        prev_v2 = None
        rstd_done = []
        for gi, (c0, w) in enumerate(gl):
            lds = {}
            def mk_src(k, c0=c0, w=w, gi=gi):
                par = k % 2
                Sb = S1s if par == 0 else S2s
                P.wait("sync", h2_ev[k])
                lds[k] = P.dma("sync", Sb[:, 0:1024],
                               ins["h2_scr"][k * 128:(k + 1) * 128, :],
                               ev=(P.ev("tensor") if (k >= 2 or gi > 0)
                                   else None),
                               sem="r2%d" % par)
                return Sb[:, c0:c0 + w]
            sev = self.stats_grp(mk_src, lambda k: lds[k], c0, w, None,
                                 r(0), E6)
            P.wait("scalar", sev)
            P.run("scalar", _act(r(1)[0:1, 0:w], r(0)[0:1, 0:w], AF.Ln,
                                 bias=EPS_RMS, scale=1.0 / D))
            aev = P.inc("scalar", _act(r(1)[0:1, 0:w], r(1)[0:1, 0:w],
                                       AF.Exp, scale=-0.5))
            hev = self.row_hilo(r(1)[0:1, 0:w], b(2)[0:1, 0:w],
                                b(3)[0:1, 0:w], w, aev)
            ppb = pp[3] if gi == 0 else pp[6]
            P.wait("tensor", hev)
            P.run("tensor", _mm(ppb[:, 0:w], self.t["c_ones_row"][0:1, :],
                                b(2)[0:1, 0:w], True, False))
            rstd_done.append(P.inc("tensor", _mm(
                ppb[:, 0:w], self.t["c_ones_row"][0:1, :],
                b(3)[0:1, 0:w], False, True)))
        P.wait("vector", rstd_done[0])
        P.wait("vector", rstd_done[1])
        prevc = {}
        for k in range(KT):
            par = k % 2
            Sb = S1s if par == 0 else S2s
            ld = P.dma("sync", Sb[:, 0:1024],
                       ins["h2_scr"][k * 128:(k + 1) * 128, :],
                       ev=prevc.get(par, rstd_done[1]), sem="r2%d" % par)
            P.wait("vector", ld)
            for gi, (c0, w) in enumerate(gl):
                ppb = pp[3] if gi == 0 else pp[6]
                ev = P.inc("vector", _stt(
                    H3[:, k * 1024 + c0:k * 1024 + c0 + w],
                    Sb[:, c0:c0 + w],
                    self.t["n2w"][:, k:k + 1], ppb[:, 0:w],
                    ALU.mult, ALU.mult))
            prevc[par] = ev
            h3_ev[k] = ev
        return h3_ev

    def ffn_gateup(self, WGT, WUT, h3_ev, sgu_gate_ev):
        P = self.P
        H1T, SGU = self.t["H3"], self.t["SGU"]
        S1s, S2s = self.t["S1s"], self.t["S2s"]
        pp = self.pp
        P.phase_start()
        prev = {}
        first = True
        ev = None
        for ff in range(64):
            self.scan_step()
            par = ff % 2
            ps = pp[0:4] if par == 0 else pp[4:8]
            P.wait("tensor", prev.get(par))
            if first:
                for k in range(KT):
                    P.wait("tensor", h3_ev[k])
                P.wait("vector", sgu_gate_ev)
                first = False
            for k in range(KT):
                wg = self.tt_tile(WGT[k * 128:(k + 1) * 128,
                                      ff * 128:(ff + 1) * 128])
                for half in range(2):
                    f1 = _mm(ps[half][:], wg,
                             H1T[:, k * 1024 + half * 512:k * 1024 + (half + 1) * 512],
                             k == 0, k == KT - 1)
                    P.inc("tensor", f1)
                wu = self.tt_tile(WUT[k * 128:(k + 1) * 128,
                                      ff * 128:(ff + 1) * 128])
                for half in range(2):
                    f1 = _mm(ps[2 + half][:], wu,
                             H1T[:, k * 1024 + half * 512:k * 1024 + (half + 1) * 512],
                             k == 0, k == KT - 1)
                    P.inc("tensor", f1)
            mm_ev = P.ev("tensor")
            Sb = S1s if par == 0 else S2s
            P.wait("scalar", mm_ev)
            P.run("scalar", _act(Sb[:, 0:512], ps[0][:], AF.Silu))
            sev = P.inc("scalar", _act(Sb[:, 512:1024], ps[1][:], AF.Silu))
            P.wait("vector", sev)
            for half in range(2):
                ev = P.inc("vector", _tt(
                    SGU[:, ff * 1024 + half * 512:ff * 1024 + (half + 1) * 512],
                    Sb[:, half * 512:(half + 1) * 512], ps[2 + half][:], ALU.mult))
            prev[par] = ev
        return ev

    def down_final(self, WDT, ins, sgu_ev, h2st_ev, wq_ev, scal_ev, out_ap):
        P = self.P
        SGU, XQ = self.t["SGU"], self.t["XQ"]
        S1s, S2s, E8 = self.t["S1s"], self.t["S2s"], self.t["E8"]
        OUTB, WQB = self.t["OUTB"], self.t["WQB"]
        COLS = self.t["COLS"]
        pp = self.pp
        prev = {}
        prev_dma = {}
        prev_wq = None
        out_dmas = []
        P.phase_start()
        P.wait("tensor", sgu_ev)
        for do in range(KT):
            par = do % 2
            ps = pp[0:4] if par == 0 else pp[4:8]
            wq_rd = P.dma("sync", WQB[:].rearrange("p (j t) -> p j t", j=KT),
                          ins["wq_scr"].rearrange("(j p) t -> p j t", p=128)[
                              :, :, do * 128:(do + 1) * 128],
                          ev=(prev_wq if do else wq_ev), sem="wqr")
            Sb = S1s if par == 0 else S2s
            xrd = P.dma("sync", Sb[:, 0:1024],
                        ins["xmT"][do * 128:(do + 1) * 128, 3:TF],
                        ev=prev.get(par), sem="xr%d" % par)
            P.wait("tensor", prev.get(par))
            for fk in range(64):
                wt = self.tt_tile(WDT[fk * 128:(fk + 1) * 128,
                                      do * 128:(do + 1) * 128])
                for half in range(2):
                    f1 = _mm(ps[half][:], wt,
                             SGU[:, fk * 1024 + half * 512:fk * 1024 + (half + 1) * 512],
                             fk == 0, fk == 63)
                    P.inc("tensor", f1)
            P.wait("tensor", wq_rd)
            P.wait("tensor", scal_ev)
            for k in range(KT):
                for half in range(2):
                    f1 = _mm(ps[2 + half][:], WQB[:, k * 128:(k + 1) * 128],
                             XQ[:, k * 1024 + half * 512:k * 1024 + (half + 1) * 512],
                             k == 0, k == KT - 1)
                    (P.inc if k == KT - 1 else P.run)("tensor", f1)
            mm_ev = P.ev("tensor")
            prev_wq = mm_ev
            h2rd = P.dma("sync", OUTB[:, (1 - par) * 1024:(2 - par) * 1024],
                         ins["h2_scr"][do * 128:(do + 1) * 128, :],
                         ev=prev_dma.get(1 - par), sem="r2%d" % (1 - par))
            P.wait("vector", mm_ev)
            P.wait("vector", xrd)
            P.wait("vector", h2rd)
            P.wait("vector", prev_dma.get(par))
            for half in range(2):
                eb = E8[:, half * 512:(half + 1) * 512]
                P.run("vector", _ts(eb, ps[2 + half][:], COLS[:, 1:2], None,
                                    ALU.mult))
                P.run("vector", _ts(eb, eb, self.t["biasf"][:, do:do + 1],
                                    None, ALU.add))
                ob = OUTB[:, par * 1024 + half * 512:par * 1024 + (half + 1) * 512]
                hb = OUTB[:, (1 - par) * 1024 + half * 512:
                          (1 - par) * 1024 + (half + 1) * 512]
                P.run("vector", _tt(hb, ps[half][:], hb, ALU.add))
                P.run("vector", _tt(hb, hb, Sb[:, half * 512:(half + 1) * 512],
                                    ALU.add))
                ev = P.inc("vector", _tt(ob, hb, eb, ALU.add))
            prev[par] = ev
            out_dmas.append(P.dma(
                "sync", out_ap[do * 128:(do + 1) * 128, :],
                OUTB[:, par * 1024:(par + 1) * 1024], ev=ev,
                sem="out%d" % par))
            prev_dma[par] = out_dmas[-1]
        P.wait("sync", out_dmas[-1])
        return out_dmas

    def bcast_col(self, src_scalar, dst_col, bnc_row, dep):
        P = self.P
        e1 = P.dma("sync", bnc_row[0:1, 0:1], src_scalar, ev=dep, sem="bnc")
        return P.dma("sync", dst_col,
                     bnc_row[0:1, 0:1].to_broadcast([128, 1]), ev=e1, sem="bnc")

    def part_max(self, col, dst_scalar, bnc_row, dep, tag):
        P = self.P
        M128 = self.t["M128"]
        e1 = P.dma("sync", bnc_row[:, 0:1], col, ev=dep, sem="bnc")
        e2 = P.dma("sync", M128[0:1, :], bnc_row.rearrange("p o -> o p"), ev=e1,
                   sem="bnc")
        P.wait("vector", e2)
        return P.inc("vector", _red(dst_scalar, M128[0:1, :], ALU.max))

    def fp8_weight_pass(self, ins):
        P = self.P
        WFB, WQ8 = self.t["WFB"], self.t["WQ8"]
        XSC, COLS, STT8 = self.t["XSC"], self.t["COLS"], self.t["STT8"]
        P.run("vector", _memset(STT8[:, 0:1], 0.0))
        lds = []
        prev = None
        for k in range(KT):
            ld = P.dma("sync", WFB[:], ins["wft"][k * 128:(k + 1) * 128, :],
                       ev=prev, sem="wfl")
            P.wait("vector", ld)
            P.run("vector", _red(STT8[:, 1:2], WFB[:], ALU.max, absval=True))
            prev = P.inc("vector", _tt(STT8[:, 0:1], STT8[:, 0:1], STT8[:, 1:2],
                                       ALU.max))
            lds.append((ld, prev))
        ws_ev = self.part_max(STT8[:, 0:1], XSC[0:1, 1:2], self.t["bncap"],
                              lds[-1][1], "ws")
        P.wait("vector", ws_ev)
        P.run("vector", _ts(XSC[0:1, 1:2], XSC[0:1, 1:2], EPS_FP8, None, ALU.max))
        P.run("vector", _recip(XSC[0:1, 4:5], XSC[0:1, 1:2]))
        wsi_ev = P.inc("vector", _tt(XSC[0:1, 5:6], XSC[0:1, 0:1],
                                     XSC[0:1, 1:2], ALU.mult))
        ce0 = self.bcast_col(XSC[0:1, 5:6], COLS[:, 1:2], self.t["bncap"], wsi_ev)
        ce = self.bcast_col(XSC[0:1, 4:5], COLS[:, 2:3], self.t["bncap2"], ce0)
        P.wait("vector", ce)
        prev_q = None
        for k in range(KT):
            ld = P.dma("sync", WFB[:], ins["wft"][k * 128:(k + 1) * 128, :],
                       ev=prev_q, sem="wfl")
            P.wait("vector", ld)
            qev = P.inc("vector", _ts(WQ8[:], WFB[:], COLS[:, 2:3], None,
                                      ALU.mult))
            prev_q = P.dma("sync", ins["wq_scr"][k * 128:(k + 1) * 128, :],
                           WQ8[:], ev=qev, sem="wqw")
        return prev_q

    def xs_finalize(self, xn_ev):
        P = self.P
        XSC, COLS, STT8 = self.t["XSC"], self.t["COLS"], self.t["STT8"]
        self.scan_flush()
        qev = P.ev("vector")
        qmax_ev = self.part_max(STT8[:, 7:8], XSC[0:1, 2:3], self.t["bncap"],
                                qev, "xs")
        P.wait("scalar", qmax_ev)
        P.run("scalar", _act(XSC[0:1, 0:1], XSC[0:1, 2:3], AF.Ln))
        xs_ev = P.inc("scalar", _act(XSC[0:1, 0:1], XSC[0:1, 0:1], AF.Exp,
                                     scale=0.5))
        P.wait("vector", xs_ev)
        P.run("vector", _ts(XSC[0:1, 0:1], XSC[0:1, 0:1], EPS_FP8, None, ALU.max))
        xi_ev = P.inc("vector", _recip(XSC[0:1, 3:4], XSC[0:1, 0:1]))
        c1 = self.bcast_col(XSC[0:1, 3:4], COLS[:, 0:1], self.t["bncap"], xi_ev)
        return c1

    def xq_quant(self, xn_ev, c2):
        P = self.P
        COLS = self.t["COLS"]
        XN, XQ = self.t["XN"], self.t["XQ"]
        P.wait("vector", c2)
        for k in range(KT):
            P.wait("vector", xn_ev[k])
            ev = P.inc("vector", _ts(XQ[:, k * 1024:(k + 1) * 1024],
                                     XN[:, k * TF + 3:(k + 1) * TF],
                                     COLS[:, 0:1], None, ALU.mult))
        return ev

    def build_all(self, ins, inp, out_ap, stop_after=None, dbg=None):
        nc, P = self.nc, self.P
        self.pp = [nc.alloc_psum_tensor("pp%d" % i, [128, 512], F32)
                   for i in range(8)]
        self.S("RING0", (128, TAPE_G), BF16)
        self.S("RING1", (128, TAPE_G), BF16)
        self.S("S1s", (128, TF), F32)
        self.S("S2s", (128, TF), F32)
        self.S("BNO", (128, 24), F32)
        self.S("STT8", (128, 8), F32)
        self.S("XSC", (1, 8), F32)
        self.S("M128", (1, 128), F32)
        self.S("COLS", (128, 8), F32)
        self.t["bncap"] = ins["bnc1"]
        self.t["bncap2"] = ins["bnc2"]
        self.setup_consts(ins, inp)
        self.S("HALL", (64, D), F32)
        self.S("SCN", (64, 64), F32)
        self.S("XBCC", (128, 4096), BF16)
        self.S("XTOK", (128, 2048), BF16)
        self.S("BTOK", (128, 2048), BF16)
        self.S("NAT", (128, 64), BF16)
        self.S("DTE", (128, 32), F32)
        self.S("ABC", (64, 32), F32)
        self.S("BDB", (128, 128), BF16)
        self.S("NAH", (32, T), BF16)
        self.S("NAL", (32, T), BF16)
        self.S("NAF", (32, T), F32)
        self.S("SZ", (128, KT * 1024), BF16)
        self.S("XN", (128, KT * TF), F32)
        self.S("H1T", (128, KT * TF), BF16)
        self.S("RW", (1, 1024), F32)
        self.S("RWB", (1, 2048), BF16)
        self.S("XE0", (128, TF), BF16)
        self.S("CVO0", (128, 1024), BF16)
        self.S("XSF", (128, 1024), F32)

        P.run("vector", _memset(self.t["HALL"][:], 0.0))
        hall0 = P.nop_inc("vector")
        self.scan_prepare(ins["xfull"])
        if stop_after == "consts":
            return self.dbg_dump(dbg, out_ap)

        WT = np.ascontiguousarray(np.asarray(inp["L0_attn_in_proj_w"], np.float32).T)
        cw = np.asarray(inp["L0_attn_conv_w"], np.float32)

        def conv_diag(f):
            ch0 = (f - 16) * 128
            out = []
            for tap in range(4):
                d = np.zeros((128, 128), np.float32)
                np.fill_diagonal(d, cw[ch0:ch0 + 128, tap])
                out.append(d)
            return out

        _, h1p_ev = self.ln_rms(ins["xpT"], "pf", gate_ev=self.cev)
        if stop_after == "pfln":
            return self.dbg_dump(dbg, out_ap)
        if stop_after and stop_after.startswith("pfj"):
            parts = stop_after.split("_")
            n = int(parts[1])
            self.noscan = True
            for tok in parts[2:]:
                if tok in ("a", "b", "c", "d"):
                    self.na_stop = tok
                if tok == "lo":
                    self.psetlo = True
            tiles = [64] if n == 0 else list(range(16, 16 + n))
            self.na_ev, _ = self.inproj(tiles, [None] * KT, WT, conv_diag,
                                        ins["pbc_scr"], 16, "pf")
            return self.dbg_dump(dbg, out_ap)
        if stop_after and stop_after.startswith("pfi"):
            parts = stop_after.split("_")
            n = int(parts[1])
            if "ns" in parts[2:]:
                self.noscan = True
            for tok in parts[2:]:
                if tok in ("a", "b", "c", "d"):
                    self.na_stop = tok
            tiles = [64] if n == 0 else list(range(16, 16 + n))
            self.na_ev, _ = self.inproj(tiles, h1p_ev, WT, conv_diag,
                                        ins["pbc_scr"], 16, "pf")
            return self.dbg_dump(dbg, out_ap)
        pf_tiles = list(range(16, 48)) + [64]
        self.na_ev, cvw_ev = self.inproj(pf_tiles, h1p_ev, WT, conv_diag,
                                         ins["pbc_scr"], 16, "pf")
        if stop_after == "pfinproj":
            return self.dbg_dump(dbg, out_ap)
        P.wait("vector", hall0)
        upd = self.ssd_pass(ins["pbc_scr"], 32, False, cvw_ev)
        P.wait("vector", upd)
        if stop_after == "pfssd":
            return self.dbg_dump(dbg, out_ap)
        P.wait("vector", self.const_ev["selc"])
        selev = P.inc("vector", _ts(self.t["HALL"][:], self.t["HALL"][:],
                                    self.t["selc"][:, 0:1], None, ALU.mult))
        xn_ev, h1_ev = self.ln_rms(ins["xmT"], "mn")
        if stop_after == "ln":
            return self.dbg_dump(dbg, out_ap)
        mn_tiles = list(range(16, 64)) + [64] + list(range(0, 16))
        self.na_ev, cvw_ev = self.inproj(mn_tiles, h1_ev, WT, conv_diag,
                                         ins["xbc_scr"], 16, "mn")
        if stop_after == "inproj":
            return self.dbg_dump(dbg, out_ap)
        c2 = self.xs_finalize(xn_ev)
        P.full_barrier()
        self.free("XSF", "CVO0", "XE0", "RWB", "RW")
        self.S("XQ", (128, KT * 1024), FP8, side="right")
        xq_ev = self.xq_quant(xn_ev, c2)
        xn_st = None
        for k in range(KT):
            P.wait("sync", xq_ev)
            xn_st = P.dma("sync", ins["xn_scr"][k * 128:(k + 1) * 128, :],
                          self.t["XN"][:, k * TF + 3:(k + 1) * TF], sem="xns")
        P.full_barrier()
        self.free("H1T", "XN")
        if stop_after == "xq":
            return self.dbg_dump(dbg, out_ap)
        self.S("YG", (128, KT * 1024), BF16)
        self.S("SEG", (128, 1024), BF16)
        self.S("SCO", (128, 256), BF16)
        self.S("ECUM", (128, 256), BF16)
        self.S("CSB", (128, 256), BF16)
        self.S("AUT", (128, 1024), BF16)
        self.S("YT", (128, 256), F32)
        self.S("HPREV", (128, D), BF16)
        self.S("XBCC2", (128, 2048), BF16)
        upd = self.ssd_pass(ins["xbc_scr"], 48, True, cvw_ev)
        if stop_after == "ssd":
            return self.dbg_dump(dbg, out_ap)
        P.full_barrier()
        self.free("XBCC2", "HPREV", "YT", "AUT", "CSB", "ECUM", "SCO", "SEG")
        self.S("OH", (128, 2048), F32)
        WOT = np.ascontiguousarray(np.asarray(inp["L0_attn_out_proj_w"], np.float32).T)
        h2_ev = self.outproj(WOT, ins, upd)
        if stop_after == "h2":
            return self.dbg_dump(dbg, out_ap)
        self.S("WFB", (128, 2048), BF16)
        self.S("WQ8", (128, 2048), FP8)
        wq_ev = self.fp8_weight_pass(ins)
        P.full_barrier()
        self.free("WQ8", "WFB", "OH", "YG", "SZ", "NAF", "NAL", "NAH", "BDB",
                  "ABC", "DTE", "NAT", "BTOK", "XTOK", "XBCC", "SCN", "HALL")
        self.S("H3", (128, KT * 1024), BF16)
        self.S("RW", (1, 1024), F32)
        self.S("RWB", (1, 2048), BF16)
        self.S("E6", (128, 1024), BF16)
        h3_ev = self.rms2_h3(ins, h2_ev)
        P.full_barrier()
        self.free("E6", "RWB", "RW")
        if stop_after == "h3":
            return self.dbg_dump(dbg, out_ap)
        self.S("SGU", (128, 64 * 1024), BF16, side="right")
        WGT = np.ascontiguousarray(np.asarray(inp["L0_attn_gate_w"], np.float32).T)
        WUT = np.ascontiguousarray(np.asarray(inp["L0_attn_up_w"], np.float32).T)
        sgu_ev = self.ffn_gateup(WGT, WUT, h3_ev, None)
        if stop_after == "sgu":
            return self.dbg_dump(dbg, out_ap)
        P.full_barrier()
        self.free("H3")
        self.S("OUTB", (128, 2048), F32)
        self.S("WQB", (128, 2048), FP8)
        self.S("E8", (128, 1024), BF16)
        WDT = np.ascontiguousarray(np.asarray(inp["L0_attn_down_w"], np.float32).T)
        self.down_final(WDT, ins, sgu_ev, None, wq_ev, xq_ev, out_ap)
        return None

    def dbg_dump(self, dbg, out_ap):
        P = self.P
        if dbg is None:
            return None
        name, col0 = dbg
        P.barrier(("tensor", "vector", "scalar", "dsync", "dscal", "cvw",
                   "h2w0", "h2w1", "wqw", "xns", "xld"), ("sync",))
        if name.startswith("@"):
            dram = self.dram_by_name[name[1:]]
            if dram.dtype == F32:
                e = P.dma("sync", out_ap[0:128, :],
                          dram[col0:col0 + 128, 0:1024], sem="out0")
                P.wait("sync", e)
                return None
            eb = self.t["S2s"][:].bitcast(BF16)
            dbs = self.t["S1s"]
            e1 = P.dma("sync", eb[:, 0:1024], dram[col0:col0 + 128, 0:1024],
                       sem="out0")
            P.wait("scalar", e1)
            e2 = P.inc("scalar", _acp(dbs[:, 0:1024], eb[:, 0:1024]))
            e3 = P.dma("sync", out_ap[0:128, :], dbs[:, 0:1024],
                       ev=e2, sem="out0")
            P.wait("sync", e3)
            return None
        srct = self.t[name]
        if srct.ap().dtype != F32:
            dbs = self.t["S1s"]
            e2 = P.inc("vector", _cp(dbs[:, 0:1024],
                                     srct[:, col0:col0 + 1024]))
            e3 = P.dma("sync", out_ap[0:128, :], dbs[:, 0:1024],
                       ev=e2, sem="out0")
        else:
            np_ = srct.ap().shape[0]
            e3 = P.dma("sync", out_ap[0:np_, :], srct[:, col0:col0 + 1024],
                       sem="out0")
        P.wait("sync", e3)
        return None


def build_program(inp, stop_after=None, dbg=None):
    nc = bass.Bass("TRN2", target_bir_lowering=False)
    P = Prog(nc)
    mk = MK(nc, P)
    ins = {}
    mk_dram = {}
    def din(name, shape, dt, kind="ExternalInput"):
        ins[name] = nc.dram_tensor(name, list(shape), dt, kind=kind).ap()
        mk_dram[name] = ins[name]
    din("xmT", (D, TF), F32)
    din("xpT", (D, TF), F32)
    din("xfull", (8192, D), F32)
    din("wft", (D, D), BF16)
    din("selc", (64, 1), F32)
    for nm in ("c_ones_row", "c_oz_row"):
        din(nm, (1, 128), BF16)
    din("c_ones_col", (128, 1), BF16)
    for nm in ("c_onesf", "c_negf", "c_ut", "c_sln", "c_id"):
        din(nm, (128, 128), BF16)
    for nm in ("c_utx4n", "c_msk4"):
        din(nm, (128, 512), BF16)
    for nm in ("n1w", "n2w", "zbv"):
        din(nm, (128, 16), F32)
    din("cbv", (128, 48), F32)
    din("dcol", (128, 32), F32)
    din("cvals", (128, 7), F32)
    din("biasf", (128, 16), F32)
    din("xbc_scr", (6144, 1024), BF16, "Internal")
    din("pbc_scr", (4096, 1024), BF16, "Internal")
    din("h2_scr", (D, 1024), F32, "Internal")
    din("xn_scr", (D, 1024), F32, "Internal")
    din("wq_scr", (D, D), FP8, "Internal")
    din("bnc1", (128, 1), F32, "Internal")
    din("bnc2", (128, 1), F32, "Internal")
    out_ap = nc.dram_tensor("out", [D, T], F32, kind="ExternalOutput").ap()
    mk.dram_by_name = mk_dram
    with nc.Block() as block:
        mk.build_all(ins, inp, out_ap, stop_after=stop_after, dbg=dbg)
        ngrp_cols = None
        mk.tape._close() if mk.tape.used else None
        ncols = max(mk.tape.groups, 1) * TAPE_G
        tape_dram = nc.dram_tensor("tape", [128, ncols], F32,
                                   kind="ExternalInput").ap()
        tape_np = mk.tape_finalize(tape_dram)
        P.build(block)
    return nc, mk, tape_np


def make_core_inputs(inp, core, tape_np, mkhost):
    x = np.asarray(inp["L0_attn_hidden_states"], np.float32).reshape(8192, D)
    b, half = core // 2, core % 2
    base = b * 2048 + half * 1024
    xm = np.zeros((TF, D), np.float32)
    xm[3:, :] = x[base:base + T]
    if half == 1:
        xm[0:3, :] = x[base - 3:base]
    xmT = np.ascontiguousarray(xm.T)
    xp = np.zeros((TF, D), np.float32)
    if half == 1:
        xp[3:, :] = x[b * 2048:b * 2048 + T]
    xpT = np.ascontiguousarray(xp.T)
    sel = np.full((64, 1), float(half), np.float32)
    d = {"xmT": xmT, "xpT": xpT, "xfull": x, "selc": sel, "tape": tape_np}
    wft = np.ascontiguousarray(np.asarray(inp["L0_ffn_weight"]).T)
    d["wft"] = wft
    d.update(mkhost)
    return d


def kernel(**inp):
    nc, mk, tape_np = build_program(inp)
    in_maps = [make_core_inputs(inp, c, tape_np, mk.host) for c in range(8)]
    res = run_bass_kernel_spmd(nc, in_maps, core_ids=list(range(8)))
    out = np.zeros((4, 2048, D), np.float32)
    for c in range(8):
        b, half = c // 2, c % 2
        out[b, half * 1024:(half + 1) * 1024, :] = res.results[c]["out"].T
    return out


# version 04
# speedup vs baseline: 1.1009x; 1.1009x over previous
import numpy as np
import ml_dtypes
import concourse.bass as bass
import concourse.mybir as mybir
from concourse.bass_utils import run_bass_kernel_spmd

F32 = mybir.dt.float32
BF16 = mybir.dt.bfloat16
FP8 = mybir.dt.float8e4
AF = mybir.ActivationFunctionType
ALU = mybir.AluOpType
AX = mybir.AxisListType
BF = ml_dtypes.bfloat16

T = 1024
TF = T + 3
NCH = 8
D = 2048
KT = 16
NH = 32
DS = 64
HD = 64
FIN = 65
TAPE_G = 3072
NRING = 2
EPS_LN = 1e-5
EPS_RMS = 1e-5
EPS_FP8 = 1e-6
MSKNEG = -1e30

def _mm(out, l, r, st, sp):
    return lambda t: t.matmul(out, l, r, start=st, stop=sp)

def _tp(out, in_, idm):
    return lambda t: t.transpose(out, in_, idm)

def _cp(dst, src):
    return lambda e: e.tensor_copy(dst, src)

def _acp(dst, src):
    return lambda s: s.copy(dst, src)

def _act(dst, src, f, bias=0.0, scale=1.0):
    return lambda s: s.activation(dst, src, f, bias=bias, scale=scale)

def _tt(dst, a, b, op):
    return lambda v: v.tensor_tensor(dst, a, b, op)

def _ts(dst, a, s1, s2, op0, op1=ALU.bypass, acc=None):
    if s2 is None:
        return lambda v: v.tensor_scalar(dst, a, s1, None, op0, accum_out=acc)
    return lambda v: v.tensor_scalar(dst, a, s1, s2, op0, op1, accum_out=acc)

def _stt(dst, a, sc, b, op0, op1):
    return lambda v: v.scalar_tensor_tensor(dst, a, sc, b, op0, op1)

def _red(dst, src, op, absval=False):
    return lambda v: v.tensor_reduce(dst, src, AX.X, op,
                                     apply_absolute_value=absval)

def _bn(dst, src):
    return lambda v: v.bn_stats(dst, src)

def _bna(dst, src):
    return lambda v: v.bn_aggr(dst, src)

def _memset(dst, c):
    return lambda v: v.memset(dst, c)

def _recip(dst, src):
    return lambda v: v.reciprocal(dst, src)


class Prog:
    ENGS = ("sync", "scalar", "vector", "tensor", "gpsimd")

    def __init__(self, nc):
        self.nc = nc
        self.steps = {e: [] for e in self.ENGS}
        self.semctx = []
        self.sem = {}
        for k in self.ENGS + ("dsync", "dscal", "dgp"):
            cm = nc.semaphore("sem_" + k)
            self.sem[k] = cm.__enter__()
            self.semctx.append(cm)
        self.cnt = {k: 0 for k in self.sem}
        self.last_wait = {e: {k: 0 for k in self.sem} for e in self.ENGS}

    def wait(self, e, ev):
        if ev is None:
            return
        sem_name, val = ev
        if val <= 0:
            return
        if self.last_wait[e][sem_name] >= val:
            return
        self.last_wait[e][sem_name] = val
        s = self.sem[sem_name]
        self.steps[e].append(lambda eng, s=s, val=val: eng.wait_ge(s, val))

    SELFSYNC = ("vector", "scalar")

    def run(self, e, fn):
        if e in self.SELFSYNC:
            self.inc(e, fn)
        else:
            self.steps[e].append(fn)

    def inc(self, e, fn, n=1):
        if e in self.SELFSYNC:
            self.wait(e, (e, self.cnt[e]))
        self.cnt[e] += n
        s, v = self.sem[e], self.cnt[e]
        self.steps[e].append(lambda eng, fn=fn, s=s, n=n: fn(eng).then_inc(s, n))
        return (e, v)

    def nop_inc(self, e, n=1):
        if e in self.SELFSYNC:
            self.wait(e, (e, self.cnt[e]))
        self.cnt[e] += n
        s, v = self.sem[e], self.cnt[e]
        self.steps[e].append(lambda eng, s=s, n=n: eng.nop().then_inc(s, n))
        return (e, v)

    def get_sem(self, name):
        if name not in self.sem:
            cm = self.nc.semaphore("sem_" + name)
            self.sem[name] = cm.__enter__()
            self.semctx.append(cm)
            self.cnt[name] = 0
            for e in self.ENGS:
                self.last_wait[e][name] = 0
        return self.sem[name]

    def dma(self, q, out, in_, ev=None, sem=None, **kw):
        qs = sem or {"sync": "dsync", "scalar": "dscal", "gpsimd": "dgp"}[q]
        s = self.get_sem(qs)
        if ev is not None:
            self.wait(q, ev)
        self.cnt[qs] += 16
        v = self.cnt[qs]
        def fn(eng, out=out, in_=in_, s=s, kw=kw):
            eng.dma_start(out=out, in_=in_, **kw).then_inc(s, 16)
        self.steps[q].append(fn)
        return (qs, v)

    def ev(self, e):
        return (e, self.cnt.get(e, 0))

    def phase_start(self):
        self.barrier(("scalar", "vector"), ("tensor",))

    def full_barrier(self, waiters=("tensor", "vector", "scalar", "sync")):
        prods = [k for k in self.cnt if self.cnt[k] > 0]
        for w in waiters:
            for p in prods:
                if p != w:
                    self.wait(w, (p, self.cnt[p]))

    def barrier(self, producers, waiters):
        evs = [self.ev(p) for p in producers if p in self.cnt]
        for w in waiters:
            for ev in evs:
                self.wait(w, ev)

    def build(self, block):
        for e in self.ENGS:
            steps = self.steps[e]
            def runner(eng, steps=steps):
                for fn in steps:
                    fn(eng)
            getattr(block, e)(runner)


class Tape:
    def __init__(self):
        self.blocks = []
        self.groups = 0
        self.used = 0
        self.pend = []

    def add(self, blk):
        w = blk.shape[1]
        assert blk.shape[0] == 128 and w <= TAPE_G, blk.shape
        if self.used + w > TAPE_G:
            self._close()
        g, off = self.groups, self.used
        self.pend.append(np.ascontiguousarray(blk, dtype=np.float32))
        self.used += w
        return g, off, w

    def _close(self):
        if self.used < TAPE_G:
            self.pend.append(np.zeros((128, TAPE_G - self.used), np.float32))
        self.blocks.extend(self.pend)
        self.pend = []
        self.groups += 1
        self.used = 0

    def finish(self):
        if self.used or not self.blocks:
            self._close()
        return np.concatenate(self.blocks, axis=1)


class MK:

    def __init__(self, nc, P):
        self.nc = nc
        self.P = P
        self.tape = Tape()
        self.tape_cons = {}
        self.cur_grp = -1
        self.host = {}
        self.t = {}
        self._stat_evac = None
        self.pb6_free = None
        self.pb7_free = None
        self.wsteps = []
        self.wsteps_i = 0

    def S(self, name, shape, dt=F32, side=None):
        self._scnt = getattr(self, "_scnt", {})
        n = self._scnt.get(name, 0)
        self._scnt[name] = n + 1
        bname = name if n == 0 else "%s_%d" % (name, n)
        cm = (self.nc.sbuf_tensor(bname, list(shape), dt, side=side)
              if side else self.nc.sbuf_tensor(bname, list(shape), dt))
        h = cm.__enter__()
        self.t[name] = h
        self._cms = getattr(self, "_cms", {})
        self._cms[name] = cm
        return h

    def free(self, *names):
        for n in names:
            self._cms[n].__exit__(None, None, None)
            del self._cms[n]
            del self.t[n]

    def tt_tile(self, blk):
        g, off, w = self.tape.add(blk)
        P = self.P
        if g != self.cur_grp:
            cnow = P.cnt.get("tensor", 0)
            if self.cur_grp >= 0:
                self.tape_cons[self.cur_grp] = ("tensor", cnow + 1)
                assert g < NRING or cnow > self._sw_cnt, \
                    "no tensor inc between tape group switches"
            self._sw_cnt = cnow
            self.cur_grp = g
            P.get_sem("tp%d" % (g % NRING)) and None
            P.wait("tensor", ("tp%d" % (g % NRING), 16 * (g // NRING + 1)))
        ring = self.t["RING%d" % (g % NRING)]
        return ring[:, off:off + w]

    def tape_finalize(self, tape_dram_ap):
        P = self.P
        if self.cur_grp >= 0 and self.cur_grp not in self.tape_cons:
            self.tape_cons[self.cur_grp] = P.ev("tensor")
        tape_np = self.tape.finish()
        ngrp = self.tape.groups
        for g in range(ngrp):
            if g >= NRING:
                P.wait("gpsimd", self.tape_cons[g - NRING])
            ring = self.t["RING%d" % (g % NRING)]
            P.dma("gpsimd", ring[:], tape_dram_ap[:, g * TAPE_G:(g + 1) * TAPE_G],
                  sem="tp%d" % (g % NRING))
        return tape_np

    def setup_consts(self, ins, inp):
        P = self.P
        iv, jv = np.meshgrid(np.arange(128), np.arange(128), indexing="ij")
        ones_r = np.ones((1, 128), np.float32)
        cdefs = {
            "c_ones_row": (np.ones((1, 128)), BF16, (1, 128)),
            "c_ones_col": (np.ones((128, 1)), BF16, (128, 1)),
            "c_oz_row": (np.full((1, 128), 1.0 / D), BF16, (1, 128)),
            "c_onesf": (np.ones((128, 128)), BF16, (128, 128)),
            "c_negf": (-np.ones((128, 128)), BF16, (128, 128)),
            "c_ut": ((iv <= jv).astype(np.float32), BF16, (128, 128)),
            "c_sln": (-(iv > jv).astype(np.float32), BF16, (128, 128)),
            "c_id": (np.eye(128), BF16, (128, 128)),
            "c_utx4n": (np.tile(-(iv <= jv).astype(np.float32), (1, 4)), BF16, (128, 512)),
            "c_msk4": (np.tile(np.where(jv >= iv, 0.0, MSKNEG), (1, 4)), BF16, (128, 512)),
        }
        n1w = np.asarray(inp["L0_attn_norm1_w"], np.float32).reshape(16, 128).T
        n2w = np.asarray(inp["L0_attn_norm2_w"], np.float32).reshape(16, 128).T
        zbv = np.asarray(inp["L0_attn_z_bias"], np.float32).reshape(16, 128).T
        cbv = np.asarray(inp["L0_attn_conv_b"], np.float32).reshape(48, 128).T
        dcol = np.broadcast_to(np.asarray(inp["L0_attn_D"], np.float32)[None, :], (128, 32))
        bias = np.asarray(inp["L0_ffn_bias"]).astype(np.float32).reshape(16, 128).T
        cdefs.update({
            "n1w": (n1w, F32, (128, 16)), "n2w": (n2w, F32, (128, 16)),
            "zbv": (zbv, F32, (128, 16)), "cbv": (cbv, F32, (128, 48)),
            "dcol": (dcol, F32, (128, 32)), "biasf": (bias, F32, (128, 16)),
        })
        cvals = [0.0, 1.0, 1e-5, 1.0 / D, -0.5, -1.0, 0.5]
        self.host["cvals"] = np.tile(np.asarray([cvals], np.float32), (128, 1))
        self.S("cvals_s", (128, 8), F32)
        self.const_ev = {}
        for nm, (arr, dt, shape) in cdefs.items():
            npdt = BF if dt == BF16 else np.float32
            self.host[nm] = np.ascontiguousarray(arr, dtype=npdt).reshape(shape)
            self.S(nm + "_s", shape, dt)
            self.t[nm] = self.t[nm + "_s"]
            self.const_ev[nm] = P.dma("sync", self.t[nm][:], ins[nm], sem="cbulk")
        cv_ev = P.dma("sync", self.t["cvals_s"][:, 0:7], ins["cvals"], sem="cbulk")
        for i, v in enumerate(cvals):
            self.nc.const_aps.register(F32, v, self.t["cvals_s"][:, i:i + 1])
        self.S("selc_s", (64, 1), F32)
        self.t["selc"] = self.t["selc_s"]
        self.const_ev["selc"] = P.dma("sync", self.t["selc"][:], ins["selc"], sem="cbulk")
        self.cev = P.ev("cbulk")
        for nm in list(self.const_ev):
            self.const_ev[nm] = self.cev
        P.wait("scalar", self.cev)
        P.wait("vector", self.cev)

    def stats_grp(self, src_fn, dep_fn, c0, w, s1pair, s2row, scratch):
        P = self.P
        Eb = scratch
        pp = self.pp
        want_sum = s1pair is not None
        mm_prev = None
        for k in range(KT):
            srcap = src_fn(k)
            P.wait("scalar", dep_fn(k))
            P.wait("scalar", mm_prev)
            if want_sum:
                P.run("scalar", _acp(Eb[:, 0:w], srcap))
            ev_sq = P.inc("scalar", _act(Eb[:, 512:512 + w], srcap, AF.Square))
            P.wait("tensor", ev_sq)
            P.wait("tensor", self._stat_evac)
            if want_sum:
                P.inc("tensor", _mm(pp[0][0:1, 0:w], self.t["c_ones_col"][:, 0:1],
                                    Eb[:, 0:w], k == 0, k == KT - 1))
            P.inc("tensor", _mm(pp[1][0:1, 0:w], self.t["c_ones_col"][:, 0:1],
                                Eb[:, 512:512 + w], k == 0, k == KT - 1))
            mm_prev = P.ev("tensor")
        P.wait("vector", mm_prev)
        if want_sum:
            P.run("vector", _cp(s1pair[0][0:1, 0:w], pp[0][0:1, 0:w]))
            P.run("vector", _tt(s1pair[1][0:1, 0:w], pp[0][0:1, 0:w],
                                s1pair[0][0:1, 0:w], ALU.subtract))
        ev = P.inc("vector", _cp(s2row[0:1, 0:w], pp[1][0:1, 0:w]))
        self._stat_evac = ev
        return ev

    def row_hilo(self, row, hi, lo, wtot, dep):
        P = self.P
        P.wait("vector", dep)
        P.run("vector", _cp(hi, row))
        return P.inc("vector", _tt(lo, row, hi, ALU.subtract))

    def bcast2(self, lhs_row, hi, lo, psum, w, dep, start=True):
        P = self.P
        P.wait("tensor", dep)
        P.run("tensor", _mm(psum[:, 0:w], lhs_row[0:1, :], hi, start, False))
        return P.inc("tensor", _mm(psum[:, 0:w], lhs_row[0:1, :], lo, False, True))

    def ln_rms(self, x_dram, tag, gate_ev=None):
        P = self.P
        XN, H1T = self.t["XN"], self.t["H1T"]
        RW, RWB = self.t["RW"], self.t["RWB"]
        S1s, XE0 = self.t["S1s"], self.t["XE0"]
        pp = self.pp
        r = lambda i: RW[0:1, i * 512:(i + 1) * 512]
        b = lambda i: RWB[0:1, i * 512:(i + 1) * 512]
        P.phase_start()
        xev = []
        for k in range(KT):
            xev.append(P.dma("sync", XN[:, k * TF:(k + 1) * TF],
                             x_dram[k * 128:(k + 1) * 128, :],
                             ev=(gate_ev if k == 0 else None), sem="xld"))
        gl = [(0, 512), (512, 512), (1024, 3)]
        xall = xev[-1]
        xn_ev = [None] * KT
        prev_v = None
        for (c0, w) in gl:
            sev = self.stats_grp(
                lambda k: XN[:, k * TF + c0:k * TF + c0 + w],
                lambda k: xall, c0, w, (b(0), b(1)), r(0), XE0)
            P.run("vector", _tt(r(1)[0:1, 0:w], b(0)[0:1, 0:w], b(1)[0:1, 0:w],
                                ALU.add))
            P.run("vector", _tt(r(1)[0:1, 0:w], r(1)[0:1, 0:w], r(1)[0:1, 0:w],
                                ALU.mult))
            vev = P.inc("vector", _stt(r(1)[0:1, 0:w], r(1)[0:1, 0:w],
                                       -1.0 / D, r(0)[0:1, 0:w],
                                       ALU.mult, ALU.add))
            P.wait("scalar", vev)
            P.run("scalar", _act(r(1)[0:1, 0:w], r(1)[0:1, 0:w], AF.Ln,
                                 bias=EPS_LN, scale=1.0 / D))
            aev = P.inc("scalar", _act(r(1)[0:1, 0:w], r(1)[0:1, 0:w],
                                       AF.Exp, scale=-0.5))
            hev2 = self.row_hilo(r(1)[0:1, 0:w], b(2)[0:1, 0:w],
                                 b(3)[0:1, 0:w], w, aev)
            P.wait("tensor", hev2)
            P.wait("tensor", prev_v)
            P.run("tensor", _mm(pp[2][:, 0:w], self.t["c_oz_row"][0:1, :],
                                b(0)[0:1, 0:w], True, False))
            P.inc("tensor", _mm(pp[2][:, 0:w], self.t["c_oz_row"][0:1, :],
                                b(1)[0:1, 0:w], False, True))
            P.run("tensor", _mm(pp[3][:, 0:w], self.t["c_ones_row"][0:1, :],
                                b(2)[0:1, 0:w], True, False))
            P.inc("tensor", _mm(pp[3][:, 0:w], self.t["c_ones_row"][0:1, :],
                                b(3)[0:1, 0:w], False, True))
            bev = P.ev("tensor")
            P.wait("vector", bev)
            for k in range(KT):
                xs = XN[:, k * TF + c0:k * TF + c0 + w]
                P.run("vector", _stt(S1s[:, 0:w], xs, 1.0, pp[2][:, 0:w],
                                     ALU.mult, ALU.subtract))
                ev = P.inc("vector", _tt(xs, S1s[:, 0:w], pp[3][:, 0:w],
                                         ALU.mult))
            prev_v = P.ev("vector")
        for k in range(KT):
            xn_ev[k] = prev_v
        h1_ev = [None] * KT
        prev_v2 = None
        for (c0, w) in gl:
            sev1 = self.stats_grp(
                lambda k: XN[:, k * TF + c0:k * TF + c0 + w],
                lambda k: xn_ev[k], c0, w, None, r(0), XE0)
            P.wait("scalar", sev1)
            P.run("scalar", _act(r(1)[0:1, 0:w], r(0)[0:1, 0:w], AF.Ln,
                                 bias=EPS_RMS, scale=1.0 / D))
            aev1 = P.inc("scalar", _act(r(1)[0:1, 0:w], r(1)[0:1, 0:w],
                                        AF.Exp, scale=-0.5))
            hev3 = self.row_hilo(r(1)[0:1, 0:w], b(2)[0:1, 0:w],
                                 b(3)[0:1, 0:w], w, aev1)
            P.wait("tensor", hev3)
            P.wait("tensor", prev_v2)
            P.run("tensor", _mm(pp[3][:, 0:w], self.t["c_ones_row"][0:1, :],
                                b(2)[0:1, 0:w], True, False))
            bev = P.inc("tensor", _mm(pp[3][:, 0:w], self.t["c_ones_row"][0:1, :],
                                      b(3)[0:1, 0:w], False, True))
            P.wait("vector", bev)
            for k in range(KT):
                ev = P.inc("vector", _stt(
                    H1T[:, k * TF + c0:k * TF + c0 + w],
                    XN[:, k * TF + c0:k * TF + c0 + w],
                    self.t["n1w"][:, k:k + 1], pp[3][:, 0:w],
                    ALU.mult, ALU.mult))
            prev_v2 = P.ev("vector")
        for k in range(KT):
            h1_ev[k] = prev_v2
        return xn_ev, h1_ev

    def scan_prepare(self, xfull_dram):
        P = self.P
        self.scan_dmas = []
        XSF = self.t["XSF"]
        for i in range(64):
            for hf in range(2):
                self.scan_dmas.append(
                    (XSF[:, hf * 1024:(hf + 1) * 1024],
                     xfull_dram[i * 128:(i + 1) * 128,
                                hf * 1024:(hf + 1) * 1024]))
        self.scan_i = 0
        self.scan_loaded = [P.dma("scalar", *self.scan_dmas[0], sem="scn"),
                            P.dma("scalar", *self.scan_dmas[1], sem="scn")]
        P.run("vector", _memset(self.t["STT8"][:, 7:8], 0.0))

    def scan_step(self):
        if getattr(self, "noscan", False):
            return
        if self.scan_i >= 128:
            return
        P = self.P
        i = self.scan_i
        self.scan_i += 1
        hf = i % 2
        slot = self.t["XSF"][:, hf * 1024:(hf + 1) * 1024]
        BNO, S8 = self.t["BNO"], self.t["STT8"]
        P.wait("vector", self.scan_loaded[i])
        for j in range(2):
            P.run("vector", _bn(BNO[:, hf * 12 + j * 6:hf * 12 + (j + 1) * 6],
                                slot[:, j * 512:(j + 1) * 512]))
        dst = S8[:, 2 + hf:3 + hf]
        P.run("vector", _red(dst, slot[:], ALU.max))
        ev = P.inc("vector", _red(S8[:, 4 + hf:5 + hf], slot[:], ALU.min))
        if i + 2 < 128:
            self.scan_loaded.append(
                P.dma("scalar", *self.scan_dmas[i + 2], ev=ev, sem="scn"))
        if hf == 0:
            return
        P.run("vector", _bna(S8[:, 0:2], BNO[:, 0:24]))
        P.run("vector", _tt(S8[:, 2:3], S8[:, 2:3], S8[:, 3:4], ALU.max))
        P.run("vector", _tt(S8[:, 4:5], S8[:, 4:5], S8[:, 5:6], ALU.min))
        P.run("vector", _tt(S8[:, 3:4], S8[:, 2:3], S8[:, 0:1], ALU.subtract))
        P.run("vector", _tt(S8[:, 5:6], S8[:, 0:1], S8[:, 4:5], ALU.subtract))
        P.run("vector", _tt(S8[:, 3:4], S8[:, 3:4], S8[:, 5:6], ALU.max))
        P.run("vector", _tt(S8[:, 3:4], S8[:, 3:4], S8[:, 3:4], ALU.mult))
        P.run("vector", _ts(S8[:, 6:7], S8[:, 1:2], EPS_LN, None, ALU.add))
        P.run("vector", _recip(S8[:, 6:7], S8[:, 6:7]))
        P.run("vector", _tt(S8[:, 3:4], S8[:, 3:4], S8[:, 6:7], ALU.mult))
        P.run("vector", _tt(S8[:, 7:8], S8[:, 7:8], S8[:, 3:4], ALU.max))

    def scan_flush(self):
        while self.scan_i < 128:
            self.scan_step()

    def inproj(self, ftiles, h1_ev, WT, conv_diag, scr_dram, scr_base, tag):
        P = self.P
        pp = self.pp
        H1T = self.t["H1T"]
        P.phase_start()
        gl = [(0, 512), (512, 512), (1024, 3)]
        prev_cons = {}
        prev_conv = None
        prev_cvo = None
        prev_xe = [None, None]
        na_done = None
        for fi, f in enumerate(ftiles):
            self.scan_step()
            self.scan_step()
            if getattr(self, "psetlo", False):
                pset = (pp[0], pp[1], pp[2])
            else:
                pset = (pp[4], pp[5], pp[6]) if fi % 2 == 0 else (pp[7], pp[0], pp[1])
            wrows = 32 if f == 64 else 128
            P.wait("tensor", prev_cons.get(fi % 2))
            for k in range(KT):
                wt = self.tt_tile(WT[k * 128:(k + 1) * 128,
                                     f * 128:f * 128 + wrows])
                if k == 0:
                    for gi, (c0, w) in enumerate(gl):
                        P.wait("tensor", h1_ev[k])
                for gi, (c0, w) in enumerate(gl):
                    f1 = _mm(pset[gi][0:wrows, 0:w], wt,
                             H1T[:, k * TF + c0:k * TF + c0 + w],
                             k == 0, k == KT - 1)
                    (P.inc if k == KT - 1 else P.run)("tensor", f1)
            mm_ev = P.ev("tensor")
            if f == 64:
                na_done = self.na_chain(pset, mm_ev, tag)
                prev_cons[fi % 2] = na_done
            elif f < 16:
                P.wait("scalar", mm_ev)
                SZ = self.t["SZ"]
                zmap = [(0, 3, 509), (1, 0, 512), (2, 0, 3)]
                o = 0
                for gi, p0, w in zmap:
                    ev = P.inc("scalar", _act(
                        SZ[:, f * 1024 + o:f * 1024 + o + w],
                        pset[gi][0:128, p0:p0 + w], AF.Silu,
                        bias=self.t["zbv"][:, f:f + 1]))
                    o += w
                prev_cons[fi % 2] = ev
            else:
                par = 0
                XE = self.t["XE0"]
                CVO = self.t["CVO0"]
                P.wait("scalar", mm_ev)
                P.wait("scalar", prev_xe[par])
                for gi, (c0, w) in enumerate(gl):
                    P.run("scalar", _acp(XE[:, c0:c0 + w], pset[gi][0:128, 0:w]))
                xe_ev = P.ev("scalar")
                P.wait("tensor", xe_ev)
                P.wait("tensor", prev_conv)
                for tap in range(4):
                    wd = self.tt_tile(conv_diag(f)[tap])
                    f1 = _mm(pp[2][:, 0:512], wd,
                             XE[:, tap:tap + 512], tap == 0, tap == 3)
                    f2 = _mm(pp[3][:, 0:512], wd,
                             XE[:, 512 + tap:512 + tap + 512],
                             tap == 0, tap == 3)
                    (P.inc if tap == 3 else P.run)("tensor", f1)
                    (P.inc if tap == 3 else P.run)("tensor", f2)
                cv_ev = P.ev("tensor")
                P.wait("scalar", cv_ev)
                P.wait("scalar", prev_cvo)
                ci = f - scr_base
                P.run("scalar", _act(CVO[:, 0:512], pp[2][:, 0:512], AF.Identity,
                                     bias=self.t["cbv"][:, ci:ci + 1]))
                ev2 = P.inc("scalar", _act(CVO[:, 512:1024], pp[3][:, 0:512],
                                           AF.Identity,
                                           bias=self.t["cbv"][:, ci:ci + 1]))
                prev_conv = ev2
                dma_ev = P.dma("sync", scr_dram[ci * 128:(ci + 1) * 128, :],
                               CVO[:, 0:1024], ev=ev2, sem="cvw")
                prev_cvo = dma_ev
                prev_xe[par] = cv_ev
                prev_cons[fi % 2] = xe_ev
        return na_done, prev_cvo

    def na_chain(self, pset, mm_ev, tag):
        P = self.P
        NAH, NAL = self.t["NAH"], self.t["NAL"]
        S2s = self.t["S2s"]
        NAF = self.t["NAF"]
        S1s = self.t["S1s"]
        gl = [(0, 512), (512, 512), (1024, 3)]
        na_stop = getattr(self, "na_stop", None)
        if na_stop == "a":
            return mm_ev
        P.wait("scalar", mm_ev)
        for gi, (c0, w) in enumerate(gl):
            P.run("scalar", _acp(S2s[:, c0:c0 + w], pset[gi][0:128, 0:w]))
        evac_ev = P.ev("scalar")
        if na_stop == "b":
            return evac_ev
        P.wait("vector", evac_ev)
        rel_ev = P.inc("vector", _ts(S1s[0:32, 0:TF], S2s[0:32, 0:TF], 0.0,
                                     None, ALU.max))
        P.wait("scalar", rel_ev)
        P.run("scalar", _act(S2s[0:32, 0:TF], S2s[0:32, 0:TF], AF.Abs))
        P.run("scalar", _act(S2s[0:32, 0:TF], S2s[0:32, 0:TF], AF.Exp, scale=-1.0))
        aev2 = P.inc("scalar", _act(S2s[0:32, 0:TF], S2s[0:32, 0:TF], AF.Ln,
                                    bias=1.0))
        if na_stop == "c":
            return aev2
        P.wait("vector", aev2)
        P.run("vector", _tt(NAF[0:32, 0:T], S1s[0:32, 3:TF],
                            S2s[0:32, 3:TF], ALU.add))
        P.run("vector", _cp(NAH[0:32, 0:T], NAF[0:32, 0:T]))
        ev = P.inc("vector", _tt(NAL[0:32, 0:T], NAF[0:32, 0:T],
                                 NAH[0:32, 0:T], ALU.subtract))
        return ev

    def ssd_chunk_common(self, c, scr, ntiles, dep_scr):
        P = self.P
        XBCC = self.t["XBCC"]
        pb6 = self.pp[6].bitcast(BF16)
        pb7 = self.pp[7].bitcast(BF16)
        ev = {}
        scr_r = scr.rearrange("(j p) t -> p j t", p=128)
        dmas = []
        for q in range(2):
            dmas.append(P.dma(
                "sync",
                XBCC[:, q * 16 * 128:(q + 1) * 16 * 128].rearrange(
                    "p (j t) -> p j t", j=16),
                scr_r[:, q * 16:(q + 1) * 16, c * 128:(c + 1) * 128],
                ev=dep_scr, sem="rbk"))
        if ntiles > 32:
            dmas.append(P.dma(
                "sync",
                self.t["XBCC2"][:].rearrange("p (j t) -> p j t", j=16),
                scr_r[:, 32:48, c * 128:(c + 1) * 128],
                ev=dep_scr, sem="rbk"))
        ev["xbcc"] = dmas[-1]
        P.wait("tensor", dmas[-1])
        P.wait("tensor", self.const_ev["c_id"])
        cid = self.t["c_id"][:]
        pb_free = [self.pb6_free, self.pb7_free]
        for dst_name, base in (("XTOK", 0), ("BTOK", 16)):
            dst = self.t[dst_name]
            for half in range(2):
                pb = pb6 if half == 0 else pb7
                P.wait("tensor", pb_free[half])
                for j in range(8):
                    jt = base + half * 8 + j
                    P.inc("tensor", _tp(pb[:, j * 128:(j + 1) * 128],
                                        XBCC[:, jt * 128:(jt + 1) * 128],
                                        cid))
                tev = P.ev("tensor")
                P.wait("scalar", tev)
                eev = P.inc("scalar", _acp(
                    dst[:, half * 1024:(half + 1) * 1024], pb[:]))
                ev["tp_evac_%s%d" % (dst_name, half)] = eev
                pb_free[half] = eev
        self.pb6_free, self.pb7_free = pb_free
        NAT = self.t["NAT"]
        P.wait("tensor", self.pb6_free)
        P.wait("tensor", self.na_ev)
        P.inc("tensor", _tp(pb6[:, 0:32], self.t["NAH"][0:32, c * 128:(c + 1) * 128],
                            cid[0:32, 0:32]))
        P.inc("tensor", _tp(pb6[:, 32:64],
                            self.t["NAL"][0:32, c * 128:(c + 1) * 128],
                            cid[0:32, 0:32]))
        tev = P.ev("tensor")
        P.wait("scalar", tev)
        nat_ev = P.inc("scalar", _acp(NAT[:], pb6[:, 0:64]))
        ev["nat"] = nat_ev
        self.pb6_free = nat_ev
        pp2 = self.pp[2]
        P.wait("tensor", nat_ev)
        P.run("tensor", _mm(pp2[:, 0:32], self.t["c_sln"][:], NAT[:, 0:32],
                            True, False))
        P.inc("tensor", _mm(pp2[:, 0:32], self.t["c_sln"][:], NAT[:, 32:64],
                            False, True))
        P.run("tensor", _mm(pp2[0:64, 32:64], self.t["c_negf"][:][:, 0:64],
                            NAT[:, 0:32], True, False))
        P.inc("tensor", _mm(pp2[0:64, 32:64], self.t["c_negf"][:][:, 0:64],
                            NAT[:, 32:64], False, True))
        dte_mm = P.ev("tensor")
        P.wait("scalar", dte_mm)
        P.run("scalar", _act(self.t["DTE"][:], pp2[:, 0:32], AF.Exp))
        ev["dteA"] = P.inc("scalar", _act(self.t["ABC"][:], pp2[0:64, 32:64],
                                          AF.Exp))
        return ev

    def ssd_sa_heads(self, ev, upd_prev):
        P = self.P
        XTOK, BTOK = self.t["XTOK"], self.t["BTOK"]
        HALL, SCN = self.t["HALL"], self.t["SCN"]
        psm = (self.pp[2], self.pp[3])
        last_s_cons = {0: ev["dteA"], 1: ev["dteA"]}
        vev = None
        for h in range(NH):
            par = h % 2
            P.wait("vector", ev["dteA"])
            P.wait("vector", ev["tp_evac_BTOK1"])
            bd = self.t["BDB"][:, par * 64:(par + 1) * 64]
            bdev = P.inc("vector", _ts(bd, BTOK[:, h * 64:(h + 1) * 64],
                                       self.t["DTE"][:, h:h + 1], None, ALU.mult))
            P.wait("tensor", bdev)
            P.wait("tensor", last_s_cons[par])
            P.wait("tensor", ev["tp_evac_XTOK1"])
            sev = P.inc("tensor", _mm(psm[par][0:64, 256:320], bd,
                                      XTOK[:, h * 64:(h + 1) * 64], True, True))
            P.wait("vector", sev)
            if upd_prev is not None:
                P.wait("vector", upd_prev)
            P.run("vector", _ts(SCN[0:64, 0:64], HALL[0:64, h * 64:(h + 1) * 64],
                                self.t["ABC"][:, h:h + 1], None, ALU.mult))
            vev = P.inc("vector", _tt(HALL[0:64, h * 64:(h + 1) * 64],
                                      SCN[0:64, 0:64], psm[par][0:64, 256:320],
                                      ALU.add))
            last_s_cons[par] = vev
        return vev

    def ssd_main_heads(self, c, ev, hprev_ev):
        P = self.P
        XTOK, BTOK, XBCC = self.t["XTOK"], self.t["BTOK"], self.t["XBCC"]
        HALL, HPREV, SCN = self.t["HALL"], self.t["HPREV"], self.t["SCN"]
        SEG, SCO, ECUM, CSB = self.t["SEG"], self.t["SCO"], self.t["ECUM"], self.t["CSB"]
        AUT, NAT, YT = self.t["AUT"], self.t["NAT"], self.t["YT"]
        SZ, YG = self.t["SZ"], self.t["YG"]
        pdif = (self.pp[0], self.pp[1])
        psm = (self.pp[2], self.pp[3])
        py = (self.pp[4], self.pp[5])
        cons = {"seg": {}, "sco": {}, "cs": {}, "ecum": {}, "psm": {},
                "py": {}, "aut": {}, "dif": {}}
        scan_prev = None
        for g in range(8):
            gp = g % 2
            P.wait("vector", ev["nat"])
            P.wait("vector", cons["aut"].get(0))
            auth = AUT[:, 0:512]
            autl = AUT[:, 512:1024]
            c4n = self.t["c_utx4n"][:].rearrange("p (a b) -> p a b", a=4)
            nah = NAT[:, g * 4:(g + 1) * 4].unsqueeze(2).to_broadcast([128, 4, 128])
            nal = NAT[:, 32 + g * 4:32 + (g + 1) * 4].unsqueeze(2).to_broadcast([128, 4, 128])
            P.run("vector", _tt(auth.rearrange("p (a b) -> p a b", a=4), c4n,
                                nah, ALU.mult))
            aut_ev = P.inc("vector", _tt(autl.rearrange("p (a b) -> p a b", a=4),
                                         c4n, nal, ALU.mult))
            P.wait("tensor", aut_ev)
            P.wait("tensor", cons["dif"].get(gp))
            P.run("tensor", _mm(pdif[gp][:], self.t["c_ut"][:], nah, True, False))
            P.run("tensor", _mm(pdif[gp][:], self.t["c_ut"][:], nal, False, False))
            P.run("tensor", _mm(pdif[gp][:], self.t["c_onesf"][:], auth, False, False))
            P.run("tensor", _mm(pdif[gp][:], self.t["c_onesf"][:], autl, False, False))
            dif_ev = P.inc("tensor", _mm(pdif[gp][:], self.t["c_id"][:],
                                         self.t["c_msk4"][:], False, True))
            P.wait("scalar", dif_ev)
            P.wait("scalar", cons["seg"].get(gp))
            seg_ev = P.inc("scalar", _act(SEG[:, gp * 512:(gp + 1) * 512],
                                          pdif[gp][:], AF.Exp))
            cons["dif"][gp] = seg_ev
            for hh in range(4):
                h = g * 4 + hh
                par = h % 2
                self.wstep()
                bh = XBCC[(h % 2) * 64:(h % 2) * 64 + 64,
                          (16 + h // 2) * 128:(17 + h // 2) * 128]
                ch = self.t["XBCC2"][(h % 2) * 64:(h % 2) * 64 + 64,
                                     (h // 2) * 128:(h // 2 + 1) * 128]
                xh = XBCC[(h % 2) * 64:(h % 2) * 64 + 64,
                          (h // 2) * 128:(h // 2 + 1) * 128]
                P.wait("tensor", cons["psm"].get(par, ev["dteA"]))
                mm_cb = P.inc("tensor", _mm(psm[par][0:128, 0:128], bh, ch,
                                            True, True))
                P.run("tensor", _mm(psm[par][0:64, 128:256],
                                    self.t["c_onesf"][:][:, 0:64],
                                    auth[:, hh * 128:(hh + 1) * 128], True, False))
                mm_e = P.inc("tensor", _mm(psm[par][0:64, 128:256],
                                           self.t["c_onesf"][:][:, 0:64],
                                           autl[:, hh * 128:(hh + 1) * 128],
                                           False, True))
                cons["aut"][0] = mm_e
                P.wait("scalar", mm_e)
                P.wait("scalar", cons["ecum"].get(par))
                rb = (h % 2) * 64
                ec_ev = P.inc("scalar", _act(
                    ECUM[rb:rb + 64, par * 128:(par + 1) * 128],
                    psm[par][0:64, 128:256], AF.Exp))
                P.wait("vector", seg_ev)
                P.wait("vector", mm_cb)
                P.wait("vector", cons["sco"].get(par))
                sco_ev = P.inc("vector", _tt(
                    SCO[:, par * 128:(par + 1) * 128],
                    SEG[:, gp * 512 + hh * 128:gp * 512 + (hh + 1) * 128],
                    psm[par][0:128, 0:128], ALU.mult))
                cons["seg"][gp] = sco_ev
                P.wait("vector", ec_ev)
                P.wait("vector", cons["cs"].get(par))
                cs_ev = P.inc("vector", _tt(
                    CSB[rb:rb + 64, par * 128:(par + 1) * 128],
                    ch, ECUM[rb:rb + 64, par * 128:(par + 1) * 128],
                    ALU.mult))
                cons["ecum"][par] = cs_ev
                P.wait("tensor", sco_ev)
                P.wait("tensor", cs_ev)
                P.wait("tensor", hprev_ev)
                P.wait("tensor", cons["py"].get(par))
                P.run("tensor", _mm(py[par][0:64, 0:128],
                                    XTOK[:, h * 64:(h + 1) * 64],
                                    SCO[:, par * 128:(par + 1) * 128], True, False))
                y_mm = P.inc("tensor", _mm(py[par][0:64, 0:128],
                                           HPREV[rb:rb + 64, h * 64:(h + 1) * 64],
                                           CSB[rb:rb + 64, par * 128:(par + 1) * 128],
                                           False, True))
                cons["sco"][par] = y_mm
                cons["cs"][par] = y_mm
                bd = self.t["BDB"][:, par * 64:(par + 1) * 64]
                P.wait("vector", ev["dteA"])
                bd_ev = P.inc("vector", _ts(bd, BTOK[:, h * 64:(h + 1) * 64],
                                            self.t["DTE"][:, h:h + 1], None,
                                            ALU.mult))
                P.wait("tensor", bd_ev)
                s_mm = P.inc("tensor", _mm(psm[par][0:64, 256:320], bd,
                                           XTOK[:, h * 64:(h + 1) * 64],
                                           True, True))
                P.wait("vector", y_mm)
                P.run("vector", _stt(YT[rb:rb + 64, par * 128:(par + 1) * 128],
                                     xh, self.t["dcol"][rb:rb + 64, h:h + 1],
                                     py[par][0:64, 0:128], ALU.mult, ALU.add))
                szsl = SZ[(h % 2) * 64:(h % 2) * 64 + 64,
                          (h // 2) * 1024 + c * 128:(h // 2) * 1024 + (c + 1) * 128]
                yg_ev = P.inc("vector", _tt(
                    YG[rb:rb + 64,
                       (h // 2) * 1024 + c * 128:(h // 2) * 1024 + (c + 1) * 128],
                    YT[rb:rb + 64, par * 128:(par + 1) * 128], szsl, ALU.mult))
                cons["py"][par] = yg_ev
                P.wait("vector", s_mm)
                P.run("vector", _ts(SCN[0:64, 0:64],
                                    HALL[0:64, h * 64:(h + 1) * 64],
                                    self.t["ABC"][:, h:h + 1], None, ALU.mult))
                scan_prev = P.inc("vector", _tt(HALL[0:64, h * 64:(h + 1) * 64],
                                                SCN[0:64, 0:64],
                                                psm[par][0:64, 256:320], ALU.add))
                cons["psm"][par] = scan_prev
        return scan_prev

    def ssd_pass(self, scr_ap, ntiles, main, dep_scr):
        P = self.P
        P.phase_start()
        upd = None
        for c in range(NCH):
            self.scan_step()
            dep = dep_scr[c] if isinstance(dep_scr, list) else dep_scr
            if upd is not None:
                P.barrier(("tensor", "vector", "scalar", "dsync", "dscal"), ("sync",))
            ev = self.ssd_chunk_common(c, scr_ap, ntiles, dep)
            if main:
                P.wait("vector", upd)
                P.run("vector", _cp(self.t["HPREV"][0:64, :], self.t["HALL"][:]))
                hprev_ev = P.inc("vector", _cp(self.t["HPREV"][64:128, :],
                                               self.t["HALL"][:]))
                upd = self.ssd_main_heads(c, ev, hprev_ev)
            else:
                upd = self.ssd_sa_heads(ev, upd)
        return upd

    def outproj(self, WOT, ins, upd):
        P = self.P
        YG = self.t["YG"]
        S1s, S2s, OH = self.t["S1s"], self.t["S2s"], self.t["OH"]
        E5 = self.t["E5"]
        pp = self.pp
        ppst = (pp[2], pp[3])
        st_mm = {}
        st_pend = None
        prev = {}
        prev_dma = {}
        h2_ev = [None] * KT

        def emit_stats():
            sq_ev, sdo = st_pend
            P.wait("tensor", sq_ev)
            for half in range(2):
                fm = _mm(ppst[half][0:1, 0:512], self.t["c_ones_col"][:, 0:1],
                         E5[:, (sdo % 2) * 1024 + half * 512:
                            (sdo % 2) * 1024 + (half + 1) * 512],
                         sdo == 0, sdo == KT - 1)
                (P.inc if sdo == KT - 1 else P.run)("tensor", fm)
            st_mm[sdo] = P.ev("tensor")

        P.phase_start()
        P.wait("tensor", upd)
        for do in range(KT):
            par = do % 2
            ps = (pp[0], pp[1]) if par == 0 else (pp[4], pp[5])
            Sb = S1s if par == 0 else S2s
            xrd = P.dma("sync", Sb[:, 0:1024],
                        ins["xn_scr"][do * 128:(do + 1) * 128, :],
                        ev=prev.get(par), sem="xr%d" % par)
            P.wait("tensor", prev.get(par))
            for k in range(KT):
                wt = self.tt_tile(WOT[k * 128:(k + 1) * 128,
                                      do * 128:(do + 1) * 128])
                for half in range(2):
                    f1 = _mm(ps[half][:], wt,
                             YG[:, k * 1024 + half * 512:k * 1024 + (half + 1) * 512],
                             k == 0, k == KT - 1)
                    (P.inc if k == KT - 1 else P.run)("tensor", f1)
            mm_ev = P.ev("tensor")
            if st_pend is not None:
                emit_stats()
                st_pend = None
            P.wait("vector", mm_ev)
            P.wait("vector", xrd)
            P.wait("vector", prev_dma.get(par))
            for half in range(2):
                ev = P.inc("vector", _tt(
                    OH[:, par * 1024 + half * 512:par * 1024 + (half + 1) * 512],
                    ps[half][:], Sb[:, half * 512:(half + 1) * 512], ALU.add))
            prev[par] = ev
            P.wait("scalar", ev)
            P.wait("scalar", st_mm.get(do - 2))
            eslot = E5[:, (do % 2) * 1024:(do % 2) * 1024 + 1024]
            P.run("scalar", _act(eslot[:, 0:512],
                                 OH[:, par * 1024:par * 1024 + 512], AF.Square))
            sq_ev = P.inc("scalar", _act(
                eslot[:, 512:1024],
                OH[:, par * 1024 + 512:par * 1024 + 1024], AF.Square))
            st_pend = (sq_ev, do)
            h2_ev[do] = P.dma("sync", ins["h2_scr"][do * 128:(do + 1) * 128, :],
                              OH[:, par * 1024:(par + 1) * 1024], ev=ev,
                              sem="h2w%d" % par)
            prev_dma[par] = h2_ev[do]
        emit_stats()
        return h2_ev, st_mm[KT - 1]

    def rms2_h3(self, ins, h2_ev, stats_ev):
        P = self.P
        RW, RWB = self.t["RW"], self.t["RWB"]
        S1s, S2s = self.t["S1s"], self.t["S2s"]
        H3 = self.t["H3"]
        pp = self.pp
        r = lambda i: RW[0:1, i * 512:(i + 1) * 512]
        b = lambda i: RWB[0:1, i * 512:(i + 1) * 512]
        P.phase_start()
        gl = [(0, 512), (512, 512)]
        h3_ev = [None] * KT
        rstd_done = []
        for gi, (c0, w) in enumerate(gl):
            ppsrc = pp[2] if gi == 0 else pp[3]
            P.wait("vector", stats_ev)
            vev = P.inc("vector", _cp(r(0)[0:1, 0:w], ppsrc[0:1, 0:w]))
            P.wait("scalar", vev)
            P.run("scalar", _act(r(1)[0:1, 0:w], r(0)[0:1, 0:w], AF.Ln,
                                 bias=EPS_RMS, scale=1.0 / D))
            aev = P.inc("scalar", _act(r(1)[0:1, 0:w], r(1)[0:1, 0:w],
                                       AF.Exp, scale=-0.5))
            hev = self.row_hilo(r(1)[0:1, 0:w], b(2)[0:1, 0:w],
                                b(3)[0:1, 0:w], w, aev)
            ppb = pp[6] if gi == 0 else pp[7]
            P.wait("tensor", hev)
            P.run("tensor", _mm(ppb[:, 0:w], self.t["c_ones_row"][0:1, :],
                                b(2)[0:1, 0:w], True, False))
            rstd_done.append(P.inc("tensor", _mm(
                ppb[:, 0:w], self.t["c_ones_row"][0:1, :],
                b(3)[0:1, 0:w], False, True)))
        P.wait("vector", rstd_done[0])
        P.wait("vector", rstd_done[1])
        prevc = {}
        for k in range(KT):
            par = k % 2
            Sb = S1s if par == 0 else S2s
            P.wait("sync", h2_ev[k])
            ld = P.dma("sync", Sb[:, 0:1024],
                       ins["h2_scr"][k * 128:(k + 1) * 128, :],
                       ev=prevc.get(par, rstd_done[1]), sem="r2%d" % par)
            P.wait("vector", ld)
            for gi, (c0, w) in enumerate(gl):
                ppb = pp[6] if gi == 0 else pp[7]
                ev = P.inc("vector", _stt(
                    H3[:, k * 1024 + c0:k * 1024 + c0 + w],
                    Sb[:, c0:c0 + w],
                    self.t["n2w"][:, k:k + 1], ppb[:, 0:w],
                    ALU.mult, ALU.mult))
            prevc[par] = ev
            h3_ev[k] = ev
        return h3_ev

    def ffn_gateup(self, WGT, WUT, h3_ev, sgu_gate_ev):
        P = self.P
        H1T, SGU = self.t["H3"], self.t["SGU"]
        S1s, S2s = self.t["S1s"], self.t["S2s"]
        pp = self.pp
        P.phase_start()
        prev = {}
        first = True
        ev = None
        for ff in range(64):
            self.scan_step()
            par = ff % 2
            ps = pp[0:4] if par == 0 else pp[4:8]
            P.wait("tensor", prev.get(par))
            if first:
                for k in range(KT):
                    P.wait("tensor", h3_ev[k])
                P.wait("vector", sgu_gate_ev)
                first = False
            for k in range(KT):
                wg = self.tt_tile(WGT[k * 128:(k + 1) * 128,
                                      ff * 128:(ff + 1) * 128])
                for half in range(2):
                    f1 = _mm(ps[half][:], wg,
                             H1T[:, k * 1024 + half * 512:k * 1024 + (half + 1) * 512],
                             k == 0, k == KT - 1)
                    P.inc("tensor", f1)
                wu = self.tt_tile(WUT[k * 128:(k + 1) * 128,
                                      ff * 128:(ff + 1) * 128])
                for half in range(2):
                    f1 = _mm(ps[2 + half][:], wu,
                             H1T[:, k * 1024 + half * 512:k * 1024 + (half + 1) * 512],
                             k == 0, k == KT - 1)
                    P.inc("tensor", f1)
            mm_ev = P.ev("tensor")
            Sb = S1s if par == 0 else S2s
            P.wait("scalar", mm_ev)
            P.run("scalar", _act(Sb[:, 0:512], ps[0][:], AF.Silu))
            sev = P.inc("scalar", _act(Sb[:, 512:1024], ps[1][:], AF.Silu))
            P.wait("vector", sev)
            for half in range(2):
                ev = P.inc("vector", _tt(
                    SGU[:, ff * 1024 + half * 512:ff * 1024 + (half + 1) * 512],
                    Sb[:, half * 512:(half + 1) * 512], ps[2 + half][:], ALU.mult))
            prev[par] = ev
        return ev

    def down_final(self, WDT, ins, sgu_ev, h2st_ev, wq_ev, scal_ev, out_ap):
        P = self.P
        SGU, XQ = self.t["SGU"], self.t["XQ"]
        S1s, S2s, E8 = self.t["S1s"], self.t["S2s"], self.t["E8"]
        OUTB, WQB = self.t["OUTB"], self.t["WQB"]
        COLS = self.t["COLS"]
        pp = self.pp
        prev = {}
        prev_dma = {}
        prev_wq = None
        out_dmas = []
        P.phase_start()
        P.wait("tensor", sgu_ev)
        for do in range(KT):
            par = do % 2
            ps = pp[0:4] if par == 0 else pp[4:8]
            wq_rd = P.dma("sync", WQB[:].rearrange("p (j t) -> p j t", j=KT),
                          ins["wq_scr"].rearrange("(j p) t -> p j t", p=128)[
                              :, :, do * 128:(do + 1) * 128],
                          ev=(prev_wq if do else wq_ev), sem="wqr")
            Sb = S1s if par == 0 else S2s
            xrd = P.dma("sync", Sb[:, 0:1024],
                        ins["xmT"][do * 128:(do + 1) * 128, 3:TF],
                        ev=prev.get(par), sem="xr%d" % par)
            P.wait("tensor", prev.get(par))
            for fk in range(64):
                wt = self.tt_tile(WDT[fk * 128:(fk + 1) * 128,
                                      do * 128:(do + 1) * 128])
                for half in range(2):
                    f1 = _mm(ps[half][:], wt,
                             SGU[:, fk * 1024 + half * 512:fk * 1024 + (half + 1) * 512],
                             fk == 0, fk == 63)
                    P.inc("tensor", f1)
            P.wait("tensor", wq_rd)
            P.wait("tensor", scal_ev)
            for k in range(KT):
                for half in range(2):
                    f1 = _mm(ps[2 + half][:], WQB[:, k * 128:(k + 1) * 128],
                             XQ[:, k * 1024 + half * 512:k * 1024 + (half + 1) * 512],
                             k == 0, k == KT - 1)
                    (P.inc if k == KT - 1 else P.run)("tensor", f1)
            mm_ev = P.ev("tensor")
            prev_wq = mm_ev
            h2rd = P.dma("sync", OUTB[:, (1 - par) * 1024:(2 - par) * 1024],
                         ins["h2_scr"][do * 128:(do + 1) * 128, :],
                         ev=prev_dma.get(1 - par), sem="r2%d" % (1 - par))
            P.wait("vector", mm_ev)
            P.wait("vector", xrd)
            P.wait("vector", h2rd)
            P.wait("vector", prev_dma.get(par))
            for half in range(2):
                eb = E8[:, half * 512:(half + 1) * 512]
                P.run("vector", _ts(eb, ps[2 + half][:], COLS[:, 1:2], None,
                                    ALU.mult))
                P.run("vector", _ts(eb, eb, self.t["biasf"][:, do:do + 1],
                                    None, ALU.add))
                ob = OUTB[:, par * 1024 + half * 512:par * 1024 + (half + 1) * 512]
                hb = OUTB[:, (1 - par) * 1024 + half * 512:
                          (1 - par) * 1024 + (half + 1) * 512]
                P.run("vector", _tt(hb, ps[half][:], hb, ALU.add))
                P.run("vector", _tt(hb, hb, Sb[:, half * 512:(half + 1) * 512],
                                    ALU.add))
                ev = P.inc("vector", _tt(ob, hb, eb, ALU.add))
            prev[par] = ev
            out_dmas.append(P.dma(
                "sync", out_ap[do * 128:(do + 1) * 128, :],
                OUTB[:, par * 1024:(par + 1) * 1024], ev=ev,
                sem="out%d" % par))
            prev_dma[par] = out_dmas[-1]
        P.wait("sync", out_dmas[-1])
        return out_dmas

    def bcast_col(self, src_scalar, dst_col, bnc_row, dep):
        P = self.P
        e1 = P.dma("sync", bnc_row[0:1, 0:1], src_scalar, ev=dep, sem="bnc")
        return P.dma("sync", dst_col,
                     bnc_row[0:1, 0:1].to_broadcast([128, 1]), ev=e1, sem="bnc")

    def part_max(self, col, dst_scalar, bnc_row, dep, tag):
        P = self.P
        M128 = self.t["M128"]
        e1 = P.dma("sync", bnc_row[:, 0:1], col, ev=dep, sem="bnc")
        e2 = P.dma("sync", M128[0:1, :], bnc_row.rearrange("p o -> o p"), ev=e1,
                   sem="bnc")
        P.wait("vector", e2)
        return P.inc("vector", _red(dst_scalar, M128[0:1, :], ALU.max))

    def fp8_wsteps_init(self, ins):
        P = self.P
        WFB, WQ8 = self.t["WFB"], self.t["WQ8"]
        WS8, XSC, COLS = self.t["WS8"], self.t["XSC"], self.t["COLS"]
        st = {}
        steps = []

        def p1(k):
            def f():
                slot = WFB[:, (k % 2) * 2048:(k % 2) * 2048 + 2048]
                ld = P.dma("sync", slot, ins["wft"][k * 128:(k + 1) * 128, :],
                           ev=st.get(("c", k % 2)), sem="wfl")
                P.wait("vector", ld)
                if k == 0:
                    P.run("vector", _memset(WS8[:, 0:1], 0.0))
                P.run("vector", _red(WS8[:, 1:2], slot, ALU.max, absval=True))
                st[("c", k % 2)] = P.inc("vector", _tt(
                    WS8[:, 0:1], WS8[:, 0:1], WS8[:, 1:2], ALU.max))
            return f

        def mid():
            mx = self.part_max(WS8[:, 0:1], XSC[0:1, 1:2], self.t["bncap3"],
                               st[("c", 1)], "ws")
            P.wait("vector", mx)
            P.run("vector", _ts(XSC[0:1, 1:2], XSC[0:1, 1:2], EPS_FP8, None,
                                ALU.max))
            P.run("vector", _recip(XSC[0:1, 4:5], XSC[0:1, 1:2]))
            wsi = P.inc("vector", _tt(XSC[0:1, 5:6], XSC[0:1, 0:1],
                                      XSC[0:1, 1:2], ALU.mult))
            ce0 = self.bcast_col(XSC[0:1, 5:6], COLS[:, 1:2],
                                 self.t["bncap3"], wsi)
            st["bc"] = self.bcast_col(XSC[0:1, 4:5], COLS[:, 2:3],
                                      self.t["bncap4"], ce0)

        def p2(k):
            def f():
                slot = WFB[:, (k % 2) * 2048:(k % 2) * 2048 + 2048]
                q = WQ8[:, (k % 2) * 2048:(k % 2) * 2048 + 2048]
                ld = P.dma("sync", slot, ins["wft"][k * 128:(k + 1) * 128, :],
                           ev=(st.get(("q", k % 2)) or st.get(("c", k % 2))),
                           sem="wfl")
                P.wait("vector", ld)
                if k == 0:
                    P.wait("vector", st["bc"])
                qev = P.inc("vector", _ts(q, slot, COLS[:, 2:3], None,
                                          ALU.mult))
                st[("q", k % 2)] = P.dma(
                    "sync", ins["wq_scr"][k * 128:(k + 1) * 128, :], q,
                    ev=qev, sem="wqw")
                if k == KT - 1:
                    self._wq_ev = st[("q", k % 2)]
            return f

        for k in range(KT):
            steps.append(p1(k))
        steps.append(mid)
        for k in range(KT):
            steps.append(p2(k))
        self.wsteps = steps
        self.wsteps_i = 0

    def wstep(self):
        if self.wsteps_i < len(self.wsteps):
            f = self.wsteps[self.wsteps_i]
            self.wsteps_i += 1
            f()

    def fp8_weight_pass(self, ins):
        P = self.P
        WFB, WQ8 = self.t["WFB"], self.t["WQ8"]
        XSC, COLS, STT8 = self.t["XSC"], self.t["COLS"], self.t["STT8"]
        P.run("vector", _memset(STT8[:, 0:1], 0.0))
        lds = []
        prev = None
        for k in range(KT):
            ld = P.dma("sync", WFB[:], ins["wft"][k * 128:(k + 1) * 128, :],
                       ev=prev, sem="wfl")
            P.wait("vector", ld)
            P.run("vector", _red(STT8[:, 1:2], WFB[:], ALU.max, absval=True))
            prev = P.inc("vector", _tt(STT8[:, 0:1], STT8[:, 0:1], STT8[:, 1:2],
                                       ALU.max))
            lds.append((ld, prev))
        ws_ev = self.part_max(STT8[:, 0:1], XSC[0:1, 1:2], self.t["bncap"],
                              lds[-1][1], "ws")
        P.wait("vector", ws_ev)
        P.run("vector", _ts(XSC[0:1, 1:2], XSC[0:1, 1:2], EPS_FP8, None, ALU.max))
        P.run("vector", _recip(XSC[0:1, 4:5], XSC[0:1, 1:2]))
        wsi_ev = P.inc("vector", _tt(XSC[0:1, 5:6], XSC[0:1, 0:1],
                                     XSC[0:1, 1:2], ALU.mult))
        ce0 = self.bcast_col(XSC[0:1, 5:6], COLS[:, 1:2], self.t["bncap"], wsi_ev)
        ce = self.bcast_col(XSC[0:1, 4:5], COLS[:, 2:3], self.t["bncap2"], ce0)
        P.wait("vector", ce)
        prev_q = None
        for k in range(KT):
            ld = P.dma("sync", WFB[:], ins["wft"][k * 128:(k + 1) * 128, :],
                       ev=prev_q, sem="wfl")
            P.wait("vector", ld)
            qev = P.inc("vector", _ts(WQ8[:], WFB[:], COLS[:, 2:3], None,
                                      ALU.mult))
            prev_q = P.dma("sync", ins["wq_scr"][k * 128:(k + 1) * 128, :],
                           WQ8[:], ev=qev, sem="wqw")
        return prev_q

    def xs_finalize(self, xn_ev):
        P = self.P
        XSC, COLS, STT8 = self.t["XSC"], self.t["COLS"], self.t["STT8"]
        self.scan_flush()
        qev = P.ev("vector")
        qmax_ev = self.part_max(STT8[:, 7:8], XSC[0:1, 2:3], self.t["bncap"],
                                qev, "xs")
        P.wait("scalar", qmax_ev)
        P.run("scalar", _act(XSC[0:1, 0:1], XSC[0:1, 2:3], AF.Ln))
        xs_ev = P.inc("scalar", _act(XSC[0:1, 0:1], XSC[0:1, 0:1], AF.Exp,
                                     scale=0.5))
        P.wait("vector", xs_ev)
        P.run("vector", _ts(XSC[0:1, 0:1], XSC[0:1, 0:1], EPS_FP8, None, ALU.max))
        xi_ev = P.inc("vector", _recip(XSC[0:1, 3:4], XSC[0:1, 0:1]))
        c1 = self.bcast_col(XSC[0:1, 3:4], COLS[:, 0:1], self.t["bncap"], xi_ev)
        return c1

    def xq_quant(self, xn_ev, c2):
        P = self.P
        COLS = self.t["COLS"]
        XN, XQ = self.t["XN"], self.t["XQ"]
        P.wait("vector", c2)
        for k in range(KT):
            P.wait("vector", xn_ev[k])
            ev = P.inc("vector", _ts(XQ[:, k * 1024:(k + 1) * 1024],
                                     XN[:, k * TF + 3:(k + 1) * TF],
                                     COLS[:, 0:1], None, ALU.mult))
        return ev

    def build_all(self, ins, inp, out_ap, stop_after=None, dbg=None):
        nc, P = self.nc, self.P
        self.pp = [nc.alloc_psum_tensor("pp%d" % i, [128, 512], F32)
                   for i in range(8)]
        self.S("RING0", (128, TAPE_G), BF16)
        self.S("RING1", (128, TAPE_G), BF16)
        self.S("S1s", (128, TF), F32)
        self.S("S2s", (128, TF), F32)
        self.S("BNO", (128, 24), F32)
        self.S("STT8", (128, 8), F32)
        self.S("XSC", (1, 8), F32)
        self.S("M128", (1, 128), F32)
        self.S("COLS", (128, 8), F32)
        self.S("WS8", (128, 2), F32)
        self.t["bncap"] = ins["bnc1"]
        self.t["bncap2"] = ins["bnc2"]
        self.t["bncap3"] = ins["bnc3"]
        self.t["bncap4"] = ins["bnc4"]
        self.setup_consts(ins, inp)
        self.S("HALL", (64, D), F32)
        self.S("SCN", (64, 64), F32)
        self.S("XBCC", (128, 4096), BF16)
        self.S("XTOK", (128, 2048), BF16)
        self.S("BTOK", (128, 2048), BF16)
        self.S("NAT", (128, 64), BF16)
        self.S("DTE", (128, 32), F32)
        self.S("ABC", (64, 32), F32)
        self.S("BDB", (128, 128), BF16)
        self.S("NAH", (32, T), BF16)
        self.S("NAL", (32, T), BF16)
        self.S("NAF", (32, T), F32)
        self.S("SZ", (128, KT * 1024), BF16)
        self.S("XN", (128, KT * TF), F32)
        self.S("H1T", (128, KT * TF), BF16)
        self.S("RW", (1, 1024), F32)
        self.S("RWB", (1, 2048), BF16)
        self.S("XE0", (128, TF), BF16)
        self.S("CVO0", (128, 1024), BF16)
        self.S("XSF", (128, 2048), F32)

        P.run("vector", _memset(self.t["HALL"][:], 0.0))
        hall0 = P.nop_inc("vector")
        self.scan_prepare(ins["xfull"])
        if stop_after == "consts":
            return self.dbg_dump(dbg, out_ap)

        WT = np.ascontiguousarray(np.asarray(inp["L0_attn_in_proj_w"], np.float32).T)
        cw = np.asarray(inp["L0_attn_conv_w"], np.float32)

        def conv_diag(f):
            ch0 = (f - 16) * 128
            out = []
            for tap in range(4):
                d = np.zeros((128, 128), np.float32)
                np.fill_diagonal(d, cw[ch0:ch0 + 128, tap])
                out.append(d)
            return out

        _, h1p_ev = self.ln_rms(ins["xpT"], "pf", gate_ev=self.cev)
        if stop_after == "pfln":
            return self.dbg_dump(dbg, out_ap)
        if stop_after and stop_after.startswith("pfj"):
            parts = stop_after.split("_")
            n = int(parts[1])
            self.noscan = True
            for tok in parts[2:]:
                if tok in ("a", "b", "c", "d"):
                    self.na_stop = tok
                if tok == "lo":
                    self.psetlo = True
            tiles = [64] if n == 0 else list(range(16, 16 + n))
            self.na_ev, _ = self.inproj(tiles, [None] * KT, WT, conv_diag,
                                        ins["pbc_scr"], 16, "pf")
            return self.dbg_dump(dbg, out_ap)
        if stop_after and stop_after.startswith("pfi"):
            parts = stop_after.split("_")
            n = int(parts[1])
            if "ns" in parts[2:]:
                self.noscan = True
            for tok in parts[2:]:
                if tok in ("a", "b", "c", "d"):
                    self.na_stop = tok
            tiles = [64] if n == 0 else list(range(16, 16 + n))
            self.na_ev, _ = self.inproj(tiles, h1p_ev, WT, conv_diag,
                                        ins["pbc_scr"], 16, "pf")
            return self.dbg_dump(dbg, out_ap)
        pf_tiles = list(range(16, 48)) + [64]
        self.na_ev, cvw_ev = self.inproj(pf_tiles, h1p_ev, WT, conv_diag,
                                         ins["pbc_scr"], 16, "pf")
        if stop_after == "pfinproj":
            return self.dbg_dump(dbg, out_ap)
        P.wait("vector", hall0)
        upd = self.ssd_pass(ins["pbc_scr"], 32, False, cvw_ev)
        P.wait("vector", upd)
        if stop_after == "pfssd":
            return self.dbg_dump(dbg, out_ap)
        P.wait("vector", self.const_ev["selc"])
        selev = P.inc("vector", _ts(self.t["HALL"][:], self.t["HALL"][:],
                                    self.t["selc"][:, 0:1], None, ALU.mult))
        xn_ev, h1_ev = self.ln_rms(ins["xmT"], "mn")
        if stop_after == "ln":
            return self.dbg_dump(dbg, out_ap)
        mn_tiles = list(range(16, 64)) + [64] + list(range(0, 16))
        self.na_ev, cvw_ev = self.inproj(mn_tiles, h1_ev, WT, conv_diag,
                                         ins["xbc_scr"], 16, "mn")
        if stop_after == "inproj":
            return self.dbg_dump(dbg, out_ap)
        c2 = self.xs_finalize(xn_ev)
        P.full_barrier()
        self.free("XSF", "CVO0", "XE0", "RWB", "RW")
        self.S("XQ", (128, KT * 1024), FP8, side="right")
        xq_ev = self.xq_quant(xn_ev, c2)
        xn_st = None
        for k in range(KT):
            P.wait("sync", xq_ev)
            xn_st = P.dma("sync", ins["xn_scr"][k * 128:(k + 1) * 128, :],
                          self.t["XN"][:, k * TF + 3:(k + 1) * TF], sem="xns")
        P.full_barrier()
        self.free("H1T", "XN")
        if stop_after == "xq":
            return self.dbg_dump(dbg, out_ap)
        self.S("WFB", (128, 4096), BF16)
        self.S("WQ8", (128, 4096), FP8)
        self.fp8_wsteps_init(ins)
        self.S("YG", (128, KT * 1024), BF16)
        self.S("SEG", (128, 1024), BF16)
        self.S("SCO", (128, 256), BF16)
        self.S("ECUM", (128, 256), BF16)
        self.S("CSB", (128, 256), BF16)
        self.S("AUT", (128, 1024), BF16)
        self.S("YT", (128, 256), F32)
        self.S("HPREV", (128, D), BF16)
        self.S("XBCC2", (128, 2048), BF16)
        upd = self.ssd_pass(ins["xbc_scr"], 48, True, cvw_ev)
        if stop_after == "ssd":
            return self.dbg_dump(dbg, out_ap)
        P.full_barrier()
        self.free("XBCC2", "HPREV", "YT", "AUT", "CSB", "ECUM", "SCO", "SEG")
        self.S("OH", (128, 2048), F32)
        self.S("E5", (128, 2048), BF16)
        WOT = np.ascontiguousarray(np.asarray(inp["L0_attn_out_proj_w"], np.float32).T)
        h2_ev, st_ev = self.outproj(WOT, ins, upd)
        if stop_after == "h2":
            return self.dbg_dump(dbg, out_ap)
        P.full_barrier()
        self.free("E5", "OH", "YG", "WQ8", "WFB", "SZ", "NAF", "NAL", "NAH",
                  "BDB", "ABC", "DTE", "NAT", "BTOK", "XTOK", "XBCC", "SCN",
                  "HALL")
        self.S("H3", (128, KT * 1024), BF16)
        self.S("RW", (1, 1024), F32)
        self.S("RWB", (1, 2048), BF16)
        h3_ev = self.rms2_h3(ins, h2_ev, st_ev)
        P.full_barrier()
        self.free("RWB", "RW")
        if stop_after == "h3":
            return self.dbg_dump(dbg, out_ap)
        self.S("SGU", (128, 64 * 1024), BF16, side="right")
        WGT = np.ascontiguousarray(np.asarray(inp["L0_attn_gate_w"], np.float32).T)
        WUT = np.ascontiguousarray(np.asarray(inp["L0_attn_up_w"], np.float32).T)
        sgu_ev = self.ffn_gateup(WGT, WUT, h3_ev, None)
        if stop_after == "sgu":
            return self.dbg_dump(dbg, out_ap)
        P.full_barrier()
        self.free("H3")
        self.S("OUTB", (128, 2048), F32)
        self.S("WQB", (128, 2048), FP8)
        self.S("E8", (128, 1024), BF16)
        WDT = np.ascontiguousarray(np.asarray(inp["L0_attn_down_w"], np.float32).T)
        self.down_final(WDT, ins, sgu_ev, None, self._wq_ev, xq_ev, out_ap)
        return None

    def dbg_dump(self, dbg, out_ap):
        P = self.P
        if dbg is None:
            return None
        name, col0 = dbg
        P.barrier(("tensor", "vector", "scalar", "dsync", "dscal", "cvw",
                   "h2w0", "h2w1", "wqw", "xns", "xld"), ("sync",))
        if name.startswith("@"):
            dram = self.dram_by_name[name[1:]]
            if dram.dtype == F32:
                e = P.dma("sync", out_ap[0:128, :],
                          dram[col0:col0 + 128, 0:1024], sem="out0")
                P.wait("sync", e)
                return None
            eb = self.t["S2s"][:].bitcast(BF16)
            dbs = self.t["S1s"]
            e1 = P.dma("sync", eb[:, 0:1024], dram[col0:col0 + 128, 0:1024],
                       sem="out0")
            P.wait("scalar", e1)
            e2 = P.inc("scalar", _acp(dbs[:, 0:1024], eb[:, 0:1024]))
            e3 = P.dma("sync", out_ap[0:128, :], dbs[:, 0:1024],
                       ev=e2, sem="out0")
            P.wait("sync", e3)
            return None
        srct = self.t[name]
        if srct.ap().dtype != F32:
            dbs = self.t["S1s"]
            e2 = P.inc("vector", _cp(dbs[:, 0:1024],
                                     srct[:, col0:col0 + 1024]))
            e3 = P.dma("sync", out_ap[0:128, :], dbs[:, 0:1024],
                       ev=e2, sem="out0")
        else:
            np_ = srct.ap().shape[0]
            e3 = P.dma("sync", out_ap[0:np_, :], srct[:, col0:col0 + 1024],
                       sem="out0")
        P.wait("sync", e3)
        return None


def build_program(inp, stop_after=None, dbg=None):
    nc = bass.Bass("TRN2", target_bir_lowering=False)
    P = Prog(nc)
    mk = MK(nc, P)
    ins = {}
    mk_dram = {}
    def din(name, shape, dt, kind="ExternalInput"):
        ins[name] = nc.dram_tensor(name, list(shape), dt, kind=kind).ap()
        mk_dram[name] = ins[name]
    din("xmT", (D, TF), F32)
    din("xpT", (D, TF), F32)
    din("xfull", (8192, D), F32)
    din("wft", (D, D), BF16)
    din("selc", (64, 1), F32)
    for nm in ("c_ones_row", "c_oz_row"):
        din(nm, (1, 128), BF16)
    din("c_ones_col", (128, 1), BF16)
    for nm in ("c_onesf", "c_negf", "c_ut", "c_sln", "c_id"):
        din(nm, (128, 128), BF16)
    for nm in ("c_utx4n", "c_msk4"):
        din(nm, (128, 512), BF16)
    for nm in ("n1w", "n2w", "zbv"):
        din(nm, (128, 16), F32)
    din("cbv", (128, 48), F32)
    din("dcol", (128, 32), F32)
    din("cvals", (128, 7), F32)
    din("biasf", (128, 16), F32)
    din("xbc_scr", (6144, 1024), BF16, "Internal")
    din("pbc_scr", (4096, 1024), BF16, "Internal")
    din("h2_scr", (D, 1024), F32, "Internal")
    din("xn_scr", (D, 1024), F32, "Internal")
    din("wq_scr", (D, D), FP8, "Internal")
    din("bnc1", (128, 1), F32, "Internal")
    din("bnc2", (128, 1), F32, "Internal")
    din("bnc3", (128, 1), F32, "Internal")
    din("bnc4", (128, 1), F32, "Internal")
    out_ap = nc.dram_tensor("out", [D, T], F32, kind="ExternalOutput").ap()
    mk.dram_by_name = mk_dram
    with nc.Block() as block:
        mk.build_all(ins, inp, out_ap, stop_after=stop_after, dbg=dbg)
        ngrp_cols = None
        mk.tape._close() if mk.tape.used else None
        ncols = max(mk.tape.groups, 1) * TAPE_G
        tape_dram = nc.dram_tensor("tape", [128, ncols], F32,
                                   kind="ExternalInput").ap()
        tape_np = mk.tape_finalize(tape_dram)
        P.build(block)
    return nc, mk, tape_np


def make_core_inputs(inp, core, tape_np, mkhost):
    x = np.asarray(inp["L0_attn_hidden_states"], np.float32).reshape(8192, D)
    b, half = core // 2, core % 2
    base = b * 2048 + half * 1024
    xm = np.zeros((TF, D), np.float32)
    xm[3:, :] = x[base:base + T]
    if half == 1:
        xm[0:3, :] = x[base - 3:base]
    xmT = np.ascontiguousarray(xm.T)
    xp = np.zeros((TF, D), np.float32)
    if half == 1:
        xp[3:, :] = x[b * 2048:b * 2048 + T]
    xpT = np.ascontiguousarray(xp.T)
    sel = np.full((64, 1), float(half), np.float32)
    d = {"xmT": xmT, "xpT": xpT, "xfull": x, "selc": sel, "tape": tape_np}
    wft = np.ascontiguousarray(np.asarray(inp["L0_ffn_weight"]).T)
    d["wft"] = wft
    d.update(mkhost)
    return d


def kernel(**inp):
    nc, mk, tape_np = build_program(inp)
    in_maps = [make_core_inputs(inp, c, tape_np, mk.host) for c in range(8)]
    res = run_bass_kernel_spmd(nc, in_maps, core_ids=list(range(8)))
    out = np.zeros((4, 2048, D), np.float32)
    for c in range(8):
        b, half = c // 2, c % 2
        out[b, half * 1024:(half + 1) * 1024, :] = res.results[c]["out"].T
    return out


# version 05
# speedup vs baseline: 1.1586x; 1.0524x over previous
import numpy as np
import ml_dtypes
import concourse.bass as bass
import concourse.mybir as mybir
from concourse.bass_utils import run_bass_kernel_spmd

F32 = mybir.dt.float32
BF16 = mybir.dt.bfloat16
FP8 = mybir.dt.float8e4
AF = mybir.ActivationFunctionType
ALU = mybir.AluOpType
AX = mybir.AxisListType
BF = ml_dtypes.bfloat16

T = 1024
TF = T + 3
NCH = 8
D = 2048
KT = 16
NH = 32
DS = 64
HD = 64
FIN = 65
TAPE_G = 3072
NRING = 2
EPS_LN = 1e-5
EPS_RMS = 1e-5
EPS_FP8 = 1e-6
MSKNEG = -1e30

def _mm(out, l, r, st, sp):
    return lambda t: t.matmul(out, l, r, start=st, stop=sp)

def _tp(out, in_, idm):
    return lambda t: t.transpose(out, in_, idm)

def _cp(dst, src):
    return lambda e: e.tensor_copy(dst, src)

def _acp(dst, src):
    return lambda s: s.copy(dst, src)

def _act(dst, src, f, bias=0.0, scale=1.0):
    return lambda s: s.activation(dst, src, f, bias=bias, scale=scale)

def _tt(dst, a, b, op):
    return lambda v: v.tensor_tensor(dst, a, b, op)

def _ts(dst, a, s1, s2, op0, op1=ALU.bypass, acc=None):
    if s2 is None:
        return lambda v: v.tensor_scalar(dst, a, s1, None, op0, accum_out=acc)
    return lambda v: v.tensor_scalar(dst, a, s1, s2, op0, op1, accum_out=acc)

def _stt(dst, a, sc, b, op0, op1):
    return lambda v: v.scalar_tensor_tensor(dst, a, sc, b, op0, op1)

def _red(dst, src, op, absval=False):
    return lambda v: v.tensor_reduce(dst, src, AX.X, op,
                                     apply_absolute_value=absval)

def _bn(dst, src):
    return lambda v: v.bn_stats(dst, src)

def _bna(dst, src):
    return lambda v: v.bn_aggr(dst, src)

def _memset(dst, c):
    return lambda v: v.memset(dst, c)

def _recip(dst, src):
    return lambda v: v.reciprocal(dst, src)


class Prog:
    ENGS = ("sync", "scalar", "vector", "tensor", "gpsimd")

    def __init__(self, nc):
        self.nc = nc
        self.steps = {e: [] for e in self.ENGS}
        self.semctx = []
        self.sem = {}
        for k in self.ENGS + ("dsync", "dscal", "dgp"):
            cm = nc.semaphore("sem_" + k)
            self.sem[k] = cm.__enter__()
            self.semctx.append(cm)
        self.cnt = {k: 0 for k in self.sem}
        self.last_wait = {e: {k: 0 for k in self.sem} for e in self.ENGS}

    def wait(self, e, ev):
        if ev is None:
            return
        sem_name, val = ev
        if val <= 0:
            return
        if self.last_wait[e][sem_name] >= val:
            return
        self.last_wait[e][sem_name] = val
        s = self.sem[sem_name]
        self.steps[e].append(lambda eng, s=s, val=val: eng.wait_ge(s, val))

    SELFSYNC = ("vector", "scalar")

    def run(self, e, fn):
        if e in self.SELFSYNC:
            self.inc(e, fn)
        else:
            self.steps[e].append(fn)

    def inc(self, e, fn, n=1):
        if e in self.SELFSYNC:
            self.wait(e, (e, self.cnt[e]))
        self.cnt[e] += n
        s, v = self.sem[e], self.cnt[e]
        self.steps[e].append(lambda eng, fn=fn, s=s, n=n: fn(eng).then_inc(s, n))
        return (e, v)

    def nop_inc(self, e, n=1):
        if e in self.SELFSYNC:
            self.wait(e, (e, self.cnt[e]))
        self.cnt[e] += n
        s, v = self.sem[e], self.cnt[e]
        self.steps[e].append(lambda eng, s=s, n=n: eng.nop().then_inc(s, n))
        return (e, v)

    def get_sem(self, name):
        if name not in self.sem:
            cm = self.nc.semaphore("sem_" + name)
            self.sem[name] = cm.__enter__()
            self.semctx.append(cm)
            self.cnt[name] = 0
            for e in self.ENGS:
                self.last_wait[e][name] = 0
        return self.sem[name]

    def dma(self, q, out, in_, ev=None, sem=None, **kw):
        qs = sem or {"sync": "dsync", "scalar": "dscal", "gpsimd": "dgp"}[q]
        s = self.get_sem(qs)
        if ev is not None:
            self.wait(q, ev)
        self.cnt[qs] += 16
        v = self.cnt[qs]
        def fn(eng, out=out, in_=in_, s=s, kw=kw):
            eng.dma_start(out=out, in_=in_, **kw).then_inc(s, 16)
        self.steps[q].append(fn)
        return (qs, v)

    def ev(self, e):
        return (e, self.cnt.get(e, 0))

    def phase_start(self):
        self.barrier(("scalar", "vector"), ("tensor",))

    def full_barrier(self, waiters=("tensor", "vector", "scalar", "sync")):
        prods = [k for k in self.cnt if self.cnt[k] > 0]
        for w in waiters:
            for p in prods:
                if p != w:
                    self.wait(w, (p, self.cnt[p]))

    def barrier(self, producers, waiters):
        evs = [self.ev(p) for p in producers if p in self.cnt]
        for w in waiters:
            for ev in evs:
                self.wait(w, ev)

    def build(self, block):
        for e in self.ENGS:
            steps = self.steps[e]
            def runner(eng, steps=steps):
                for fn in steps:
                    fn(eng)
            getattr(block, e)(runner)


class Tape:
    def __init__(self):
        self.blocks = []
        self.groups = 0
        self.used = 0
        self.pend = []

    def add(self, blk):
        w = blk.shape[1]
        assert blk.shape[0] == 128 and w <= TAPE_G, blk.shape
        if self.used + w > TAPE_G:
            self._close()
        g, off = self.groups, self.used
        self.pend.append(np.ascontiguousarray(blk, dtype=np.float32))
        self.used += w
        return g, off, w

    def _close(self):
        if self.used < TAPE_G:
            self.pend.append(np.zeros((128, TAPE_G - self.used), np.float32))
        self.blocks.extend(self.pend)
        self.pend = []
        self.groups += 1
        self.used = 0

    def finish(self):
        if self.used or not self.blocks:
            self._close()
        return np.concatenate(self.blocks, axis=1)


class MK:

    def __init__(self, nc, P):
        self.nc = nc
        self.P = P
        self.tape = Tape()
        self.tape_cons = {}
        self.cur_grp = -1
        self.host = {}
        self.t = {}
        self._stat_evac = None
        self.pb6_free = None
        self.pb7_free = None
        self.wsteps = []
        self.wsteps_i = 0

    def S(self, name, shape, dt=F32, side=None):
        self._scnt = getattr(self, "_scnt", {})
        n = self._scnt.get(name, 0)
        self._scnt[name] = n + 1
        bname = name if n == 0 else "%s_%d" % (name, n)
        cm = (self.nc.sbuf_tensor(bname, list(shape), dt, side=side)
              if side else self.nc.sbuf_tensor(bname, list(shape), dt))
        h = cm.__enter__()
        self.t[name] = h
        self._cms = getattr(self, "_cms", {})
        self._cms[name] = cm
        return h

    def free(self, *names):
        for n in names:
            self._cms[n].__exit__(None, None, None)
            del self._cms[n]
            del self.t[n]

    def tt_tile(self, blk):
        g, off, w = self.tape.add(blk)
        P = self.P
        if g != self.cur_grp:
            cnow = P.cnt.get("tensor", 0)
            if self.cur_grp >= 0:
                self.tape_cons[self.cur_grp] = ("tensor", cnow + 1)
                assert g < NRING or cnow > self._sw_cnt, \
                    "no tensor inc between tape group switches"
            self._sw_cnt = cnow
            self.cur_grp = g
            P.get_sem("tp%d" % (g % NRING)) and None
            P.wait("tensor", ("tp%d" % (g % NRING), 16 * (g // NRING + 1)))
        ring = self.t["RING%d" % (g % NRING)]
        return ring[:, off:off + w]

    def tape_finalize(self, tape_dram_ap):
        P = self.P
        if self.cur_grp >= 0 and self.cur_grp not in self.tape_cons:
            self.tape_cons[self.cur_grp] = P.ev("tensor")
        tape_np = self.tape.finish()
        ngrp = self.tape.groups
        for g in range(ngrp):
            if g >= NRING:
                P.wait("gpsimd", self.tape_cons[g - NRING])
            ring = self.t["RING%d" % (g % NRING)]
            P.dma("gpsimd", ring[:], tape_dram_ap[:, g * TAPE_G:(g + 1) * TAPE_G],
                  sem="tp%d" % (g % NRING))
        return tape_np

    def setup_consts(self, ins, inp):
        P = self.P
        iv, jv = np.meshgrid(np.arange(128), np.arange(128), indexing="ij")
        ones_r = np.ones((1, 128), np.float32)
        cdefs = {
            "c_ones_row": (np.ones((1, 128)), BF16, (1, 128)),
            "c_ones_col": (np.ones((128, 1)), BF16, (128, 1)),
            "c_oz_row": (np.full((1, 128), 1.0 / D), BF16, (1, 128)),
            "c_onesf": (np.ones((128, 128)), BF16, (128, 128)),
            "c_negf": (-np.ones((128, 128)), BF16, (128, 128)),
            "c_ut": ((iv <= jv).astype(np.float32), BF16, (128, 128)),
            "c_sln": (-(iv > jv).astype(np.float32), BF16, (128, 128)),
            "c_id": (np.eye(128), BF16, (128, 128)),
            "c_utx4n": (np.tile(-(iv <= jv).astype(np.float32), (1, 4)), BF16, (128, 512)),
            "c_msk4": (np.tile(np.where(jv >= iv, 0.0, MSKNEG), (1, 4)), BF16, (128, 512)),
        }
        n1w = np.asarray(inp["L0_attn_norm1_w"], np.float32).reshape(16, 128).T
        n2w = np.asarray(inp["L0_attn_norm2_w"], np.float32).reshape(16, 128).T
        zbv = np.asarray(inp["L0_attn_z_bias"], np.float32).reshape(16, 128).T
        cbv = np.asarray(inp["L0_attn_conv_b"], np.float32).reshape(48, 128).T
        dcol = np.broadcast_to(np.asarray(inp["L0_attn_D"], np.float32)[None, :], (128, 32))
        bias = np.asarray(inp["L0_ffn_bias"]).astype(np.float32).reshape(16, 128).T
        cdefs.update({
            "n1w": (n1w, F32, (128, 16)), "n2w": (n2w, F32, (128, 16)),
            "zbv": (zbv, F32, (128, 16)), "cbv": (cbv, F32, (128, 48)),
            "dcol": (dcol, F32, (128, 32)), "biasf": (bias, F32, (128, 16)),
        })
        cvals = [0.0, 1.0, 1e-5, 1.0 / D, -0.5, -1.0, 0.5]
        self.host["cvals"] = np.tile(np.asarray([cvals], np.float32), (128, 1))
        self.S("cvals_s", (128, 8), F32)
        self.const_ev = {}
        for nm, (arr, dt, shape) in cdefs.items():
            npdt = BF if dt == BF16 else np.float32
            self.host[nm] = np.ascontiguousarray(arr, dtype=npdt).reshape(shape)
            self.S(nm + "_s", shape, dt)
            self.t[nm] = self.t[nm + "_s"]
            self.const_ev[nm] = P.dma("sync", self.t[nm][:], ins[nm], sem="cbulk")
        cv_ev = P.dma("sync", self.t["cvals_s"][:, 0:7], ins["cvals"], sem="cbulk")
        for i, v in enumerate(cvals):
            self.nc.const_aps.register(F32, v, self.t["cvals_s"][:, i:i + 1])
        self.S("selc_s", (64, 1), F32)
        self.t["selc"] = self.t["selc_s"]
        self.const_ev["selc"] = P.dma("sync", self.t["selc"][:], ins["selc"], sem="cbulk")
        self.cev = P.ev("cbulk")
        for nm in list(self.const_ev):
            self.const_ev[nm] = self.cev
        P.wait("scalar", self.cev)
        P.wait("vector", self.cev)

    def stats_grp(self, src_fn, dep_fn, c0, w, s1pair, s2row, scratch):
        P = self.P
        Eb = scratch
        pp = self.pp
        want_sum = s1pair is not None
        mm_prev = None
        for k in range(KT):
            srcap = src_fn(k)
            P.wait("scalar", dep_fn(k))
            P.wait("scalar", mm_prev)
            if want_sum:
                P.run("scalar", _acp(Eb[:, 0:w], srcap))
            ev_sq = P.inc("scalar", _act(Eb[:, 512:512 + w], srcap, AF.Square))
            P.wait("tensor", ev_sq)
            P.wait("tensor", self._stat_evac)
            if want_sum:
                P.inc("tensor", _mm(pp[0][0:1, 0:w], self.t["c_ones_col"][:, 0:1],
                                    Eb[:, 0:w], k == 0, k == KT - 1))
            P.inc("tensor", _mm(pp[1][0:1, 0:w], self.t["c_ones_col"][:, 0:1],
                                Eb[:, 512:512 + w], k == 0, k == KT - 1))
            mm_prev = P.ev("tensor")
        P.wait("vector", mm_prev)
        if want_sum:
            P.run("vector", _cp(s1pair[0][0:1, 0:w], pp[0][0:1, 0:w]))
            P.run("vector", _tt(s1pair[1][0:1, 0:w], pp[0][0:1, 0:w],
                                s1pair[0][0:1, 0:w], ALU.subtract))
        ev = P.inc("vector", _cp(s2row[0:1, 0:w], pp[1][0:1, 0:w]))
        self._stat_evac = ev
        return ev

    def row_hilo(self, row, hi, lo, wtot, dep):
        P = self.P
        P.wait("vector", dep)
        P.run("vector", _cp(hi, row))
        return P.inc("vector", _tt(lo, row, hi, ALU.subtract))

    def bcast2(self, lhs_row, hi, lo, psum, w, dep, start=True):
        P = self.P
        P.wait("tensor", dep)
        P.run("tensor", _mm(psum[:, 0:w], lhs_row[0:1, :], hi, start, False))
        return P.inc("tensor", _mm(psum[:, 0:w], lhs_row[0:1, :], lo, False, True))

    def ln_rms(self, x_dram, tag, gate_ev=None):
        P = self.P
        XN, H1T = self.t["XN"], self.t["H1T"]
        RW, RWB = self.t["RW"], self.t["RWB"]
        S1s, XE0 = self.t["S1s"], self.t["XE0"]
        pp = self.pp
        r = lambda i: RW[0:1, i * 512:(i + 1) * 512]
        b = lambda i: RWB[0:1, i * 512:(i + 1) * 512]
        P.phase_start()
        xev = []
        for k in range(KT):
            xev.append(P.dma("sync", XN[:, k * TF:(k + 1) * TF],
                             x_dram[k * 128:(k + 1) * 128, :],
                             ev=(gate_ev if k == 0 else None), sem="xld"))
        gl = [(0, 512), (512, 512), (1024, 3)]
        xall = xev[-1]
        xn_ev = [None] * KT
        prev_v = None
        for (c0, w) in gl:
            sev = self.stats_grp(
                lambda k: XN[:, k * TF + c0:k * TF + c0 + w],
                lambda k: xall, c0, w, (b(0), b(1)), r(0), XE0)
            P.run("vector", _tt(r(1)[0:1, 0:w], b(0)[0:1, 0:w], b(1)[0:1, 0:w],
                                ALU.add))
            P.run("vector", _tt(r(1)[0:1, 0:w], r(1)[0:1, 0:w], r(1)[0:1, 0:w],
                                ALU.mult))
            vev = P.inc("vector", _stt(r(1)[0:1, 0:w], r(1)[0:1, 0:w],
                                       -1.0 / D, r(0)[0:1, 0:w],
                                       ALU.mult, ALU.add))
            P.wait("scalar", vev)
            P.run("scalar", _act(r(1)[0:1, 0:w], r(1)[0:1, 0:w], AF.Ln,
                                 bias=EPS_LN, scale=1.0 / D))
            aev = P.inc("scalar", _act(r(1)[0:1, 0:w], r(1)[0:1, 0:w],
                                       AF.Exp, scale=-0.5))
            hev2 = self.row_hilo(r(1)[0:1, 0:w], b(2)[0:1, 0:w],
                                 b(3)[0:1, 0:w], w, aev)
            P.wait("tensor", hev2)
            P.wait("tensor", prev_v)
            P.run("tensor", _mm(pp[2][:, 0:w], self.t["c_oz_row"][0:1, :],
                                b(0)[0:1, 0:w], True, False))
            P.inc("tensor", _mm(pp[2][:, 0:w], self.t["c_oz_row"][0:1, :],
                                b(1)[0:1, 0:w], False, True))
            P.run("tensor", _mm(pp[3][:, 0:w], self.t["c_ones_row"][0:1, :],
                                b(2)[0:1, 0:w], True, False))
            P.inc("tensor", _mm(pp[3][:, 0:w], self.t["c_ones_row"][0:1, :],
                                b(3)[0:1, 0:w], False, True))
            bev = P.ev("tensor")
            P.wait("vector", bev)
            for k in range(KT):
                xs = XN[:, k * TF + c0:k * TF + c0 + w]
                P.run("vector", _stt(S1s[:, 0:w], xs, 1.0, pp[2][:, 0:w],
                                     ALU.mult, ALU.subtract))
                ev = P.inc("vector", _tt(xs, S1s[:, 0:w], pp[3][:, 0:w],
                                         ALU.mult))
            prev_v = P.ev("vector")
        for k in range(KT):
            xn_ev[k] = prev_v
        h1_ev = [None] * KT
        prev_v2 = None
        for (c0, w) in gl:
            sev1 = self.stats_grp(
                lambda k: XN[:, k * TF + c0:k * TF + c0 + w],
                lambda k: xn_ev[k], c0, w, None, r(0), XE0)
            P.wait("scalar", sev1)
            P.run("scalar", _act(r(1)[0:1, 0:w], r(0)[0:1, 0:w], AF.Ln,
                                 bias=EPS_RMS, scale=1.0 / D))
            aev1 = P.inc("scalar", _act(r(1)[0:1, 0:w], r(1)[0:1, 0:w],
                                        AF.Exp, scale=-0.5))
            hev3 = self.row_hilo(r(1)[0:1, 0:w], b(2)[0:1, 0:w],
                                 b(3)[0:1, 0:w], w, aev1)
            P.wait("tensor", hev3)
            P.wait("tensor", prev_v2)
            P.run("tensor", _mm(pp[3][:, 0:w], self.t["c_ones_row"][0:1, :],
                                b(2)[0:1, 0:w], True, False))
            bev = P.inc("tensor", _mm(pp[3][:, 0:w], self.t["c_ones_row"][0:1, :],
                                      b(3)[0:1, 0:w], False, True))
            P.wait("vector", bev)
            for k in range(KT):
                ev = P.inc("vector", _stt(
                    H1T[:, k * TF + c0:k * TF + c0 + w],
                    XN[:, k * TF + c0:k * TF + c0 + w],
                    self.t["n1w"][:, k:k + 1], pp[3][:, 0:w],
                    ALU.mult, ALU.mult))
            prev_v2 = P.ev("vector")
        for k in range(KT):
            h1_ev[k] = prev_v2
        return xn_ev, h1_ev

    def scan_prepare(self, xfull_dram):
        P = self.P
        self.scan_dmas = []
        XSF = self.t["XSF"]
        for i in range(64):
            for hf in range(2):
                self.scan_dmas.append(
                    (XSF[:, hf * 1024:(hf + 1) * 1024],
                     xfull_dram[i * 128:(i + 1) * 128,
                                hf * 1024:(hf + 1) * 1024]))
        self.scan_i = 0
        self.scan_loaded = [P.dma("scalar", *self.scan_dmas[0], sem="scn"),
                            P.dma("scalar", *self.scan_dmas[1], sem="scn")]
        P.run("vector", _memset(self.t["STT8"][:, 7:8], 0.0))

    def scan_step(self):
        if getattr(self, "noscan", False):
            return
        if self.scan_i >= 128:
            return
        P = self.P
        i = self.scan_i
        self.scan_i += 1
        hf = i % 2
        slot = self.t["XSF"][:, hf * 1024:(hf + 1) * 1024]
        BNO, S8 = self.t["BNO"], self.t["STT8"]
        P.wait("vector", self.scan_loaded[i])
        for j in range(2):
            P.run("vector", _bn(BNO[:, hf * 12 + j * 6:hf * 12 + (j + 1) * 6],
                                slot[:, j * 512:(j + 1) * 512]))
        dst = S8[:, 2 + hf:3 + hf]
        P.run("vector", _red(dst, slot[:], ALU.max))
        ev = P.inc("vector", _red(S8[:, 4 + hf:5 + hf], slot[:], ALU.min))
        if i + 2 < 128:
            self.scan_loaded.append(
                P.dma("scalar", *self.scan_dmas[i + 2], ev=ev, sem="scn"))
        if hf == 0:
            return
        P.run("vector", _bna(S8[:, 0:2], BNO[:, 0:24]))
        P.run("vector", _tt(S8[:, 2:3], S8[:, 2:3], S8[:, 3:4], ALU.max))
        P.run("vector", _tt(S8[:, 4:5], S8[:, 4:5], S8[:, 5:6], ALU.min))
        P.run("vector", _tt(S8[:, 3:4], S8[:, 2:3], S8[:, 0:1], ALU.subtract))
        P.run("vector", _tt(S8[:, 5:6], S8[:, 0:1], S8[:, 4:5], ALU.subtract))
        P.run("vector", _tt(S8[:, 3:4], S8[:, 3:4], S8[:, 5:6], ALU.max))
        P.run("vector", _tt(S8[:, 3:4], S8[:, 3:4], S8[:, 3:4], ALU.mult))
        P.run("vector", _ts(S8[:, 6:7], S8[:, 1:2], EPS_LN, None, ALU.add))
        P.run("vector", _recip(S8[:, 6:7], S8[:, 6:7]))
        P.run("vector", _tt(S8[:, 3:4], S8[:, 3:4], S8[:, 6:7], ALU.mult))
        P.run("vector", _tt(S8[:, 7:8], S8[:, 7:8], S8[:, 3:4], ALU.max))

    def scan_flush(self):
        while self.scan_i < 128:
            self.scan_step()

    def inproj(self, ftiles, h1_ev, WT, conv_diag, scr_dram, scr_base, tag):
        P = self.P
        pp = self.pp
        H1T = self.t["H1T"]
        P.phase_start()
        gl = [(0, 512), (512, 512), (1024, 3)]
        prev_cons = {}
        prev_conv = None
        prev_cvo = None
        prev_xe = [None, None]
        na_done = None
        for fi, f in enumerate(ftiles):
            self.scan_step()
            self.scan_step()
            if getattr(self, "psetlo", False):
                pset = (pp[0], pp[1], pp[2])
            else:
                pset = (pp[4], pp[5], pp[6]) if fi % 2 == 0 else (pp[7], pp[0], pp[1])
            wrows = 32 if f == 64 else 128
            P.wait("tensor", prev_cons.get(fi % 2))
            for k in range(KT):
                wt = self.tt_tile(WT[k * 128:(k + 1) * 128,
                                     f * 128:f * 128 + wrows])
                if k == 0:
                    for gi, (c0, w) in enumerate(gl):
                        P.wait("tensor", h1_ev[k])
                for gi, (c0, w) in enumerate(gl):
                    f1 = _mm(pset[gi][0:wrows, 0:w], wt,
                             H1T[:, k * TF + c0:k * TF + c0 + w],
                             k == 0, k == KT - 1)
                    (P.inc if k == KT - 1 else P.run)("tensor", f1)
            mm_ev = P.ev("tensor")
            if f == 64:
                na_done = self.na_chain(pset, mm_ev, tag)
                prev_cons[fi % 2] = na_done
            elif f < 16:
                P.wait("scalar", mm_ev)
                SZ = self.t["SZ"]
                zmap = [(0, 3, 509), (1, 0, 512), (2, 0, 3)]
                o = 0
                for gi, p0, w in zmap:
                    ev = P.inc("scalar", _act(
                        SZ[:, f * 1024 + o:f * 1024 + o + w],
                        pset[gi][0:128, p0:p0 + w], AF.Silu,
                        bias=self.t["zbv"][:, f:f + 1]))
                    o += w
                prev_cons[fi % 2] = ev
            else:
                par = 0
                XE = self.t["XE0"]
                CVO = self.t["CVO0"]
                P.wait("scalar", mm_ev)
                P.wait("scalar", prev_xe[par])
                for gi, (c0, w) in enumerate(gl):
                    P.run("scalar", _acp(XE[:, c0:c0 + w], pset[gi][0:128, 0:w]))
                xe_ev = P.ev("scalar")
                P.wait("tensor", xe_ev)
                P.wait("tensor", prev_conv)
                for tap in range(4):
                    wd = self.tt_tile(conv_diag(f)[tap])
                    f1 = _mm(pp[2][:, 0:512], wd,
                             XE[:, tap:tap + 512], tap == 0, tap == 3)
                    f2 = _mm(pp[3][:, 0:512], wd,
                             XE[:, 512 + tap:512 + tap + 512],
                             tap == 0, tap == 3)
                    (P.inc if tap == 3 else P.run)("tensor", f1)
                    (P.inc if tap == 3 else P.run)("tensor", f2)
                cv_ev = P.ev("tensor")
                P.wait("scalar", cv_ev)
                P.wait("scalar", prev_cvo)
                ci = f - scr_base
                P.run("scalar", _act(CVO[:, 0:512], pp[2][:, 0:512], AF.Identity,
                                     bias=self.t["cbv"][:, ci:ci + 1]))
                ev2 = P.inc("scalar", _act(CVO[:, 512:1024], pp[3][:, 0:512],
                                           AF.Identity,
                                           bias=self.t["cbv"][:, ci:ci + 1]))
                prev_conv = ev2
                dma_ev = P.dma("sync", scr_dram[ci * 128:(ci + 1) * 128, :],
                               CVO[:, 0:1024], ev=ev2, sem="cvw")
                prev_cvo = dma_ev
                prev_xe[par] = cv_ev
                prev_cons[fi % 2] = xe_ev
        return na_done, prev_cvo

    def na_chain(self, pset, mm_ev, tag):
        P = self.P
        NAH, NAL = self.t["NAH"], self.t["NAL"]
        S2s = self.t["S2s"]
        S1s = self.t["S1s"]
        gl = [(0, 512), (512, 512), (1024, 3)]
        na_stop = getattr(self, "na_stop", None)
        if na_stop == "a":
            return mm_ev
        P.wait("scalar", mm_ev)
        for gi, (c0, w) in enumerate(gl):
            P.run("scalar", _acp(S2s[:, c0:c0 + w], pset[gi][0:128, 0:w]))
        evac_ev = P.ev("scalar")
        if na_stop == "b":
            return evac_ev
        P.wait("vector", evac_ev)
        rel_ev = P.inc("vector", _ts(S1s[0:32, 0:TF], S2s[0:32, 0:TF], 0.0,
                                     None, ALU.max))
        P.wait("scalar", rel_ev)
        P.run("scalar", _act(S2s[0:32, 0:TF], S2s[0:32, 0:TF], AF.Abs))
        P.run("scalar", _act(S2s[0:32, 0:TF], S2s[0:32, 0:TF], AF.Exp, scale=-1.0))
        aev2 = P.inc("scalar", _act(S2s[0:32, 0:TF], S2s[0:32, 0:TF], AF.Ln,
                                    bias=1.0))
        if na_stop == "c":
            return aev2
        P.wait("vector", aev2)
        P.run("vector", _tt(S2s[0:32, 3:TF], S1s[0:32, 3:TF],
                            S2s[0:32, 3:TF], ALU.add))
        P.run("vector", _cp(NAH[0:32, 0:T], S2s[0:32, 3:TF]))
        ev = P.inc("vector", _tt(NAL[0:32, 0:T], S2s[0:32, 3:TF],
                                 NAH[0:32, 0:T], ALU.subtract))
        return ev

    def ssd_chunk_common(self, c, scr, ntiles, dep_scr):
        P = self.P
        XBCC = self.t["XBCC"]
        pb6 = self.pp[6].bitcast(BF16)
        pb7 = self.pp[7].bitcast(BF16)
        ev = {}
        scr_r = scr.rearrange("(j p) t -> p j t", p=128)
        dmas = []
        for q in range(2):
            dmas.append(P.dma(
                "sync",
                XBCC[:, q * 16 * 128:(q + 1) * 16 * 128].rearrange(
                    "p (j t) -> p j t", j=16),
                scr_r[:, q * 16:(q + 1) * 16, c * 128:(c + 1) * 128],
                ev=dep_scr, sem="rbk"))
        if ntiles > 32:
            dmas.append(P.dma(
                "sync",
                self.t["XBCC2"][:].rearrange("p (j t) -> p j t", j=16),
                scr_r[:, 32:48, c * 128:(c + 1) * 128],
                ev=dep_scr, sem="rbk"))
        ev["xbcc"] = dmas[-1]
        P.wait("tensor", dmas[-1])
        P.wait("tensor", self.const_ev["c_id"])
        cid = self.t["c_id"][:]
        pb_free = [self.pb6_free, self.pb7_free]
        for dst_name, base in (("XTOK", 0), ("BTOK", 16)):
            dst = self.t[dst_name]
            for half in range(2):
                pb = pb6 if half == 0 else pb7
                P.wait("tensor", pb_free[half])
                for j in range(8):
                    jt = base + half * 8 + j
                    P.inc("tensor", _tp(pb[:, j * 128:(j + 1) * 128],
                                        XBCC[:, jt * 128:(jt + 1) * 128],
                                        cid))
                tev = P.ev("tensor")
                P.wait("scalar", tev)
                eev = P.inc("scalar", _acp(
                    dst[:, half * 1024:(half + 1) * 1024], pb[:]))
                ev["tp_evac_%s%d" % (dst_name, half)] = eev
                pb_free[half] = eev
        self.pb6_free, self.pb7_free = pb_free
        NAT = self.t["NAT"]
        P.wait("tensor", self.pb6_free)
        P.wait("tensor", self.na_ev)
        P.inc("tensor", _tp(pb6[:, 0:32], self.t["NAH"][0:32, c * 128:(c + 1) * 128],
                            cid[0:32, 0:32]))
        P.inc("tensor", _tp(pb6[:, 32:64],
                            self.t["NAL"][0:32, c * 128:(c + 1) * 128],
                            cid[0:32, 0:32]))
        tev = P.ev("tensor")
        P.wait("scalar", tev)
        nat_ev = P.inc("scalar", _acp(NAT[:], pb6[:, 0:64]))
        ev["nat"] = nat_ev
        self.pb6_free = nat_ev
        pp2 = self.pp[2]
        P.wait("tensor", nat_ev)
        P.run("tensor", _mm(pp2[:, 0:32], self.t["c_sln"][:], NAT[:, 0:32],
                            True, False))
        P.inc("tensor", _mm(pp2[:, 0:32], self.t["c_sln"][:], NAT[:, 32:64],
                            False, True))
        P.run("tensor", _mm(pp2[0:64, 32:64], self.t["c_negf"][:][:, 0:64],
                            NAT[:, 0:32], True, False))
        P.inc("tensor", _mm(pp2[0:64, 32:64], self.t["c_negf"][:][:, 0:64],
                            NAT[:, 32:64], False, True))
        dte_mm = P.ev("tensor")
        P.wait("scalar", dte_mm)
        P.run("scalar", _act(self.t["DTE"][:], pp2[:, 0:32], AF.Exp))
        ev["dteA"] = P.inc("scalar", _act(self.t["ABC"][:], pp2[0:64, 32:64],
                                          AF.Exp))
        return ev

    def ssd_sa_heads(self, ev, upd_prev):
        P = self.P
        XTOK, BTOK = self.t["XTOK"], self.t["BTOK"]
        HALL = self.t["HALL"]
        pb = (self.pp[0], self.pp[1], self.pp[4], self.pp[5])
        if upd_prev is None:
            self._sa_cons = {}
        vev = None
        for g in range(4):
            P.wait("vector", ev["dteA"])
            P.wait("vector", ev["tp_evac_BTOK1"])
            P.wait("vector", self._sa_cons.get(("bd", g % 2)))
            bdg = self.t["BDB"][:, (g % 2) * 512:(g % 2) * 512 + 512]
            bdev = P.inc("vector", _tt(
                bdg.rearrange("p (a b) -> p a b", a=8),
                BTOK[:, g * 512:(g + 1) * 512].rearrange("p (a b) -> p a b",
                                                         a=8),
                self.t["DTE"][:, g * 8:(g + 1) * 8].unsqueeze(2)
                    .to_broadcast([128, 8, 64]),
                ALU.mult))
            P.wait("tensor", bdev)
            P.wait("tensor", self._sa_cons.get(g, ev["dteA"]))
            P.wait("tensor", ev["tp_evac_XTOK1"])
            for hh in range(8):
                h = g * 8 + hh
                (P.inc if hh == 7 else P.run)("tensor", _mm(
                    pb[g][0:64, hh * 64:(hh + 1) * 64],
                    bdg[:, hh * 64:(hh + 1) * 64],
                    XTOK[:, h * 64:(h + 1) * 64], True, True))
            sev = P.ev("tensor")
            hv = HALL[0:64, g * 512:(g + 1) * 512]
            P.run("vector", _tt(
                hv.rearrange("p (a b) -> p a b", a=8),
                hv.rearrange("p (a b) -> p a b", a=8),
                self.t["ABC"][:, g * 8:(g + 1) * 8].unsqueeze(2)
                    .to_broadcast([64, 8, 64]),
                ALU.mult))
            P.wait("vector", sev)
            vev = P.inc("vector", _tt(hv, hv, pb[g][0:64, 0:512], ALU.add))
            self._sa_cons[g] = vev
            self._sa_cons[("bd", g % 2)] = sev
        return vev

    def ssd_main_heads(self, c, ev, hprev_ev):
        P = self.P
        XTOK, BTOK, XBCC = self.t["XTOK"], self.t["BTOK"], self.t["XBCC"]
        HALL, HPREV, SCN = self.t["HALL"], self.t["HPREV"], self.t["SCN"]
        SEG, SCO, ECUM, CSB = self.t["SEG"], self.t["SCO"], self.t["ECUM"], self.t["CSB"]
        AUT, NAT, YT = self.t["AUT"], self.t["NAT"], self.t["YT"]
        SZ, YG = self.t["SZ"], self.t["YG"]
        pdif = (self.pp[0], self.pp[1])
        psm = (self.pp[2], self.pp[3])
        py = (self.pp[4], self.pp[5])
        cons = {"seg": {}, "sco": {}, "cs": {}, "ecum": {}, "psm": {},
                "py": {}, "aut": {}, "dif": {}}
        scan_prev = None
        for g in range(8):
            gp = g % 2
            P.wait("vector", ev["nat"])
            P.wait("vector", cons["aut"].get(0))
            auth = AUT[:, 0:512]
            autl = AUT[:, 512:1024]
            c4n = self.t["c_utx4n"][:].rearrange("p (a b) -> p a b", a=4)
            nah = NAT[:, g * 4:(g + 1) * 4].unsqueeze(2).to_broadcast([128, 4, 128])
            nal = NAT[:, 32 + g * 4:32 + (g + 1) * 4].unsqueeze(2).to_broadcast([128, 4, 128])
            P.run("vector", _tt(auth.rearrange("p (a b) -> p a b", a=4), c4n,
                                nah, ALU.mult))
            aut_ev = P.inc("vector", _tt(autl.rearrange("p (a b) -> p a b", a=4),
                                         c4n, nal, ALU.mult))
            P.wait("tensor", aut_ev)
            P.wait("tensor", cons["dif"].get(gp))
            P.run("tensor", _mm(pdif[gp][:], self.t["c_ut"][:], nah, True, False))
            P.run("tensor", _mm(pdif[gp][:], self.t["c_ut"][:], nal, False, False))
            P.run("tensor", _mm(pdif[gp][:], self.t["c_onesf"][:], auth, False, False))
            P.run("tensor", _mm(pdif[gp][:], self.t["c_onesf"][:], autl, False, False))
            dif_ev = P.inc("tensor", _mm(pdif[gp][:], self.t["c_id"][:],
                                         self.t["c_msk4"][:], False, True))
            P.wait("scalar", dif_ev)
            P.wait("scalar", cons["seg"].get(gp))
            seg_ev = P.inc("scalar", _act(SEG[:, gp * 512:(gp + 1) * 512],
                                          pdif[gp][:], AF.Exp))
            cons["dif"][gp] = seg_ev
            for hh in range(4):
                h = g * 4 + hh
                par = h % 2
                self.wstep()
                bh = XBCC[(h % 2) * 64:(h % 2) * 64 + 64,
                          (16 + h // 2) * 128:(17 + h // 2) * 128]
                ch = self.t["XBCC2"][(h % 2) * 64:(h % 2) * 64 + 64,
                                     (h // 2) * 128:(h // 2 + 1) * 128]
                xh = XBCC[(h % 2) * 64:(h % 2) * 64 + 64,
                          (h // 2) * 128:(h // 2 + 1) * 128]
                P.wait("tensor", cons["psm"].get(par, ev["dteA"]))
                mm_cb = P.inc("tensor", _mm(psm[par][0:128, 0:128], bh, ch,
                                            True, True))
                P.run("tensor", _mm(psm[par][0:64, 128:256],
                                    self.t["c_onesf"][:][:, 0:64],
                                    auth[:, hh * 128:(hh + 1) * 128], True, False))
                mm_e = P.inc("tensor", _mm(psm[par][0:64, 128:256],
                                           self.t["c_onesf"][:][:, 0:64],
                                           autl[:, hh * 128:(hh + 1) * 128],
                                           False, True))
                cons["aut"][0] = mm_e
                P.wait("scalar", mm_e)
                P.wait("scalar", cons["ecum"].get(par))
                rb = (h % 2) * 64
                ec_ev = P.inc("scalar", _act(
                    ECUM[rb:rb + 64, par * 128:(par + 1) * 128],
                    psm[par][0:64, 128:256], AF.Exp))
                P.wait("vector", seg_ev)
                P.wait("vector", mm_cb)
                P.wait("vector", cons["sco"].get(par))
                sco_ev = P.inc("vector", _tt(
                    SCO[:, par * 128:(par + 1) * 128],
                    SEG[:, gp * 512 + hh * 128:gp * 512 + (hh + 1) * 128],
                    psm[par][0:128, 0:128], ALU.mult))
                cons["seg"][gp] = sco_ev
                P.wait("vector", ec_ev)
                P.wait("vector", cons["cs"].get(par))
                cs_ev = P.inc("vector", _tt(
                    CSB[rb:rb + 64, par * 128:(par + 1) * 128],
                    ch, ECUM[rb:rb + 64, par * 128:(par + 1) * 128],
                    ALU.mult))
                cons["ecum"][par] = cs_ev
                P.wait("tensor", sco_ev)
                P.wait("tensor", cs_ev)
                P.wait("tensor", hprev_ev)
                P.wait("tensor", cons["py"].get(par))
                P.run("tensor", _mm(py[par][0:64, 0:128],
                                    XTOK[:, h * 64:(h + 1) * 64],
                                    SCO[:, par * 128:(par + 1) * 128], True, False))
                y_mm = P.inc("tensor", _mm(py[par][0:64, 0:128],
                                           HPREV[rb:rb + 64, h * 64:(h + 1) * 64],
                                           CSB[rb:rb + 64, par * 128:(par + 1) * 128],
                                           False, True))
                cons["sco"][par] = y_mm
                cons["cs"][par] = y_mm
                bd = self.t["BDB"][:, par * 64:(par + 1) * 64]
                P.wait("vector", ev["dteA"])
                bd_ev = P.inc("vector", _ts(bd, BTOK[:, h * 64:(h + 1) * 64],
                                            self.t["DTE"][:, h:h + 1], None,
                                            ALU.mult))
                P.wait("tensor", bd_ev)
                s_mm = P.inc("tensor", _mm(psm[par][0:64, 256:320], bd,
                                           XTOK[:, h * 64:(h + 1) * 64],
                                           True, True))
                P.wait("vector", y_mm)
                P.run("vector", _stt(YT[rb:rb + 64, par * 128:(par + 1) * 128],
                                     xh, self.t["dcol"][rb:rb + 64, h:h + 1],
                                     py[par][0:64, 0:128], ALU.mult, ALU.add))
                szsl = SZ[(h % 2) * 64:(h % 2) * 64 + 64,
                          (h // 2) * 1024 + c * 128:(h // 2) * 1024 + (c + 1) * 128]
                yg_ev = P.inc("vector", _tt(
                    YG[rb:rb + 64,
                       (h // 2) * 1024 + c * 128:(h // 2) * 1024 + (c + 1) * 128],
                    YT[rb:rb + 64, par * 128:(par + 1) * 128], szsl, ALU.mult))
                cons["py"][par] = yg_ev
                P.wait("vector", s_mm)
                P.run("vector", _ts(SCN[0:64, 0:64],
                                    HALL[0:64, h * 64:(h + 1) * 64],
                                    self.t["ABC"][:, h:h + 1], None, ALU.mult))
                scan_prev = P.inc("vector", _tt(HALL[0:64, h * 64:(h + 1) * 64],
                                                SCN[0:64, 0:64],
                                                psm[par][0:64, 256:320], ALU.add))
                cons["psm"][par] = scan_prev
        return scan_prev

    def ssd_pass(self, scr_ap, ntiles, main, dep_scr):
        P = self.P
        P.phase_start()
        upd = None
        for c in range(NCH):
            self.scan_step()
            dep = dep_scr[c] if isinstance(dep_scr, list) else dep_scr
            if upd is not None:
                P.barrier(("tensor", "vector", "scalar", "dsync", "dscal"), ("sync",))
            ev = self.ssd_chunk_common(c, scr_ap, ntiles, dep)
            if main:
                P.wait("vector", upd)
                P.run("vector", _cp(self.t["HPREV"][0:64, :], self.t["HALL"][:]))
                hprev_ev = P.inc("vector", _cp(self.t["HPREV"][64:128, :],
                                               self.t["HALL"][:]))
                upd = self.ssd_main_heads(c, ev, hprev_ev)
            else:
                upd = self.ssd_sa_heads(ev, upd)
        return upd

    def outproj(self, WOT, ins, upd):
        P = self.P
        YG = self.t["YG"]
        S1s, S2s, OH = self.t["S1s"], self.t["S2s"], self.t["OH"]
        E5 = self.t["E5"]
        pp = self.pp
        ppst = (pp[2], pp[3])
        st_mm = {}
        st_pend = None
        prev = {}
        prev_dma = {}
        h2_ev = [None] * KT

        def emit_stats():
            sq_ev, sdo = st_pend
            P.wait("tensor", sq_ev)
            for half in range(2):
                fm = _mm(ppst[half][0:1, 0:512], self.t["c_ones_col"][:, 0:1],
                         E5[:, (sdo % 2) * 1024 + half * 512:
                            (sdo % 2) * 1024 + (half + 1) * 512],
                         sdo == 0, sdo == KT - 1)
                (P.inc if sdo == KT - 1 else P.run)("tensor", fm)
            st_mm[sdo] = P.ev("tensor")

        P.phase_start()
        P.wait("tensor", upd)
        for do in range(KT):
            par = do % 2
            ps = (pp[0], pp[1]) if par == 0 else (pp[4], pp[5])
            Sb = S1s if par == 0 else S2s
            xrd = P.dma("sync", Sb[:, 0:1024],
                        ins["xn_scr"][do * 128:(do + 1) * 128, :],
                        ev=prev.get(par), sem="xr%d" % par)
            P.wait("tensor", prev.get(par))
            for k in range(KT):
                wt = self.tt_tile(WOT[k * 128:(k + 1) * 128,
                                      do * 128:(do + 1) * 128])
                for half in range(2):
                    f1 = _mm(ps[half][:], wt,
                             YG[:, k * 1024 + half * 512:k * 1024 + (half + 1) * 512],
                             k == 0, k == KT - 1)
                    (P.inc if k == KT - 1 else P.run)("tensor", f1)
            mm_ev = P.ev("tensor")
            if st_pend is not None:
                emit_stats()
                st_pend = None
            P.wait("vector", mm_ev)
            P.wait("vector", xrd)
            P.wait("vector", prev_dma.get(par))
            for half in range(2):
                ev = P.inc("vector", _tt(
                    OH[:, par * 1024 + half * 512:par * 1024 + (half + 1) * 512],
                    ps[half][:], Sb[:, half * 512:(half + 1) * 512], ALU.add))
            prev[par] = ev
            P.wait("scalar", ev)
            P.wait("scalar", st_mm.get(do - 2))
            eslot = E5[:, (do % 2) * 1024:(do % 2) * 1024 + 1024]
            P.run("scalar", _act(eslot[:, 0:512],
                                 OH[:, par * 1024:par * 1024 + 512], AF.Square))
            sq_ev = P.inc("scalar", _act(
                eslot[:, 512:1024],
                OH[:, par * 1024 + 512:par * 1024 + 1024], AF.Square))
            st_pend = (sq_ev, do)
            h2_ev[do] = P.dma("sync", ins["h2_scr"][do * 128:(do + 1) * 128, :],
                              OH[:, par * 1024:(par + 1) * 1024], ev=ev,
                              sem="h2w%d" % par)
            prev_dma[par] = h2_ev[do]
        emit_stats()
        return h2_ev, st_mm[KT - 1]

    def rms2_h3(self, ins, h2_ev, stats_ev):
        P = self.P
        RW, RWB = self.t["RW"], self.t["RWB"]
        S1s, S2s = self.t["S1s"], self.t["S2s"]
        H3 = self.t["H3"]
        pp = self.pp
        r = lambda i: RW[0:1, i * 512:(i + 1) * 512]
        b = lambda i: RWB[0:1, i * 512:(i + 1) * 512]
        P.phase_start()
        gl = [(0, 512), (512, 512)]
        h3_ev = [None] * KT
        rstd_done = []
        for gi, (c0, w) in enumerate(gl):
            ppsrc = pp[2] if gi == 0 else pp[3]
            P.wait("vector", stats_ev)
            vev = P.inc("vector", _cp(r(0)[0:1, 0:w], ppsrc[0:1, 0:w]))
            P.wait("scalar", vev)
            P.run("scalar", _act(r(1)[0:1, 0:w], r(0)[0:1, 0:w], AF.Ln,
                                 bias=EPS_RMS, scale=1.0 / D))
            aev = P.inc("scalar", _act(r(1)[0:1, 0:w], r(1)[0:1, 0:w],
                                       AF.Exp, scale=-0.5))
            hev = self.row_hilo(r(1)[0:1, 0:w], b(2)[0:1, 0:w],
                                b(3)[0:1, 0:w], w, aev)
            ppb = pp[6] if gi == 0 else pp[7]
            P.wait("tensor", hev)
            P.run("tensor", _mm(ppb[:, 0:w], self.t["c_ones_row"][0:1, :],
                                b(2)[0:1, 0:w], True, False))
            rstd_done.append(P.inc("tensor", _mm(
                ppb[:, 0:w], self.t["c_ones_row"][0:1, :],
                b(3)[0:1, 0:w], False, True)))
        P.wait("vector", rstd_done[0])
        P.wait("vector", rstd_done[1])
        prevc = {}
        for k in range(KT):
            par = k % 2
            Sb = S1s if par == 0 else S2s
            P.wait("sync", h2_ev[k])
            ld = P.dma("sync", Sb[:, 0:1024],
                       ins["h2_scr"][k * 128:(k + 1) * 128, :],
                       ev=prevc.get(par, rstd_done[1]), sem="r2%d" % par)
            P.wait("vector", ld)
            for gi, (c0, w) in enumerate(gl):
                ppb = pp[6] if gi == 0 else pp[7]
                ev = P.inc("vector", _stt(
                    H3[:, k * 1024 + c0:k * 1024 + c0 + w],
                    Sb[:, c0:c0 + w],
                    self.t["n2w"][:, k:k + 1], ppb[:, 0:w],
                    ALU.mult, ALU.mult))
            prevc[par] = ev
            h3_ev[k] = ev
        return h3_ev

    def ffn_gateup(self, WGT, WUT, h3_ev, sgu_gate_ev):
        P = self.P
        H1T, SGU = self.t["H3"], self.t["SGU"]
        S1s, S2s = self.t["S1s"], self.t["S2s"]
        pp = self.pp
        P.phase_start()
        prev = {}
        first = True
        ev = None
        for ff in range(64):
            self.scan_step()
            par = ff % 2
            ps = pp[0:4] if par == 0 else pp[4:8]
            P.wait("tensor", prev.get(par))
            if first:
                for k in range(KT):
                    P.wait("tensor", h3_ev[k])
                P.wait("vector", sgu_gate_ev)
                first = False
            for k in range(KT):
                wg = self.tt_tile(WGT[k * 128:(k + 1) * 128,
                                      ff * 128:(ff + 1) * 128])
                for half in range(2):
                    f1 = _mm(ps[half][:], wg,
                             H1T[:, k * 1024 + half * 512:k * 1024 + (half + 1) * 512],
                             k == 0, k == KT - 1)
                    P.inc("tensor", f1)
                wu = self.tt_tile(WUT[k * 128:(k + 1) * 128,
                                      ff * 128:(ff + 1) * 128])
                for half in range(2):
                    f1 = _mm(ps[2 + half][:], wu,
                             H1T[:, k * 1024 + half * 512:k * 1024 + (half + 1) * 512],
                             k == 0, k == KT - 1)
                    P.inc("tensor", f1)
            mm_ev = P.ev("tensor")
            Sb = S1s if par == 0 else S2s
            P.wait("scalar", mm_ev)
            P.run("scalar", _act(Sb[:, 0:512], ps[0][:], AF.Silu))
            sev = P.inc("scalar", _act(Sb[:, 512:1024], ps[1][:], AF.Silu))
            P.wait("vector", sev)
            for half in range(2):
                ev = P.inc("vector", _tt(
                    SGU[:, ff * 1024 + half * 512:ff * 1024 + (half + 1) * 512],
                    Sb[:, half * 512:(half + 1) * 512], ps[2 + half][:], ALU.mult))
            prev[par] = ev
        return ev

    def down_final(self, WDT, ins, sgu_ev, h2st_ev, wq_ev, scal_ev, out_ap):
        P = self.P
        SGU, XQ = self.t["SGU"], self.t["XQ"]
        S1s, S2s, E8 = self.t["S1s"], self.t["S2s"], self.t["E8"]
        OUTB, WQB = self.t["OUTB"], self.t["WQB"]
        COLS = self.t["COLS"]
        pp = self.pp
        prev = {}
        prev_dma = {}
        prev_wq = None
        out_dmas = []
        P.phase_start()
        P.wait("tensor", sgu_ev)
        for do in range(KT):
            par = do % 2
            ps = pp[0:4] if par == 0 else pp[4:8]
            wq_rd = P.dma("sync", WQB[:].rearrange("p (j t) -> p j t", j=KT),
                          ins["wq_scr"].rearrange("(j p) t -> p j t", p=128)[
                              :, :, do * 128:(do + 1) * 128],
                          ev=(prev_wq if do else wq_ev), sem="wqr")
            Sb = S1s if par == 0 else S2s
            xrd = P.dma("sync", Sb[:, 0:1024],
                        ins["xmT"][do * 128:(do + 1) * 128, 3:TF],
                        ev=prev.get(par), sem="xr%d" % par)
            P.wait("tensor", prev.get(par))
            for fk in range(64):
                wt = self.tt_tile(WDT[fk * 128:(fk + 1) * 128,
                                      do * 128:(do + 1) * 128])
                for half in range(2):
                    f1 = _mm(ps[half][:], wt,
                             SGU[:, fk * 1024 + half * 512:fk * 1024 + (half + 1) * 512],
                             fk == 0, fk == 63)
                    P.inc("tensor", f1)
            P.wait("tensor", wq_rd)
            P.wait("tensor", scal_ev)
            for k in range(KT):
                for half in range(2):
                    f1 = _mm(ps[2 + half][:], WQB[:, k * 128:(k + 1) * 128],
                             XQ[:, k * 1024 + half * 512:k * 1024 + (half + 1) * 512],
                             k == 0, k == KT - 1)
                    (P.inc if k == KT - 1 else P.run)("tensor", f1)
            mm_ev = P.ev("tensor")
            prev_wq = mm_ev
            h2rd = P.dma("sync", OUTB[:, (1 - par) * 1024:(2 - par) * 1024],
                         ins["h2_scr"][do * 128:(do + 1) * 128, :],
                         ev=prev_dma.get(1 - par), sem="r2%d" % (1 - par))
            P.wait("vector", mm_ev)
            P.wait("vector", xrd)
            P.wait("vector", h2rd)
            P.wait("vector", prev_dma.get(par))
            for half in range(2):
                eb = E8[:, half * 512:(half + 1) * 512]
                P.run("vector", _ts(eb, ps[2 + half][:], COLS[:, 1:2], None,
                                    ALU.mult))
                P.run("vector", _ts(eb, eb, self.t["biasf"][:, do:do + 1],
                                    None, ALU.add))
                ob = OUTB[:, par * 1024 + half * 512:par * 1024 + (half + 1) * 512]
                hb = OUTB[:, (1 - par) * 1024 + half * 512:
                          (1 - par) * 1024 + (half + 1) * 512]
                P.run("vector", _tt(hb, ps[half][:], hb, ALU.add))
                P.run("vector", _tt(hb, hb, Sb[:, half * 512:(half + 1) * 512],
                                    ALU.add))
                ev = P.inc("vector", _tt(ob, hb, eb, ALU.add))
            prev[par] = ev
            out_dmas.append(P.dma(
                "sync", out_ap[do * 128:(do + 1) * 128, :],
                OUTB[:, par * 1024:(par + 1) * 1024], ev=ev,
                sem="out%d" % par))
            prev_dma[par] = out_dmas[-1]
        P.wait("sync", out_dmas[-1])
        return out_dmas

    def bcast_col(self, src_scalar, dst_col, bnc_row, dep):
        P = self.P
        e1 = P.dma("sync", bnc_row[0:1, 0:1], src_scalar, ev=dep, sem="bnc")
        return P.dma("sync", dst_col,
                     bnc_row[0:1, 0:1].to_broadcast([128, 1]), ev=e1, sem="bnc")

    def part_max(self, col, dst_scalar, bnc_row, dep, tag):
        P = self.P
        M128 = self.t["M128"]
        e1 = P.dma("sync", bnc_row[:, 0:1], col, ev=dep, sem="bnc")
        e2 = P.dma("sync", M128[0:1, :], bnc_row.rearrange("p o -> o p"), ev=e1,
                   sem="bnc")
        P.wait("vector", e2)
        return P.inc("vector", _red(dst_scalar, M128[0:1, :], ALU.max))

    def fp8_wsteps_init(self, ins):
        P = self.P
        WFB, WQ8 = self.t["WFB"], self.t["WQ8"]
        WS8, XSC, COLS = self.t["WS8"], self.t["XSC"], self.t["COLS"]
        st = {}
        steps = []

        def p1(k):
            def f():
                slot = WFB[:, (k % 2) * 2048:(k % 2) * 2048 + 2048]
                ld = P.dma("sync", slot, ins["wft"][k * 128:(k + 1) * 128, :],
                           ev=st.get(("c", k % 2)), sem="wfl")
                P.wait("vector", ld)
                if k == 0:
                    P.run("vector", _memset(WS8[:, 0:1], 0.0))
                P.run("vector", _red(WS8[:, 1:2], slot, ALU.max, absval=True))
                st[("c", k % 2)] = P.inc("vector", _tt(
                    WS8[:, 0:1], WS8[:, 0:1], WS8[:, 1:2], ALU.max))
            return f

        def mid():
            mx = self.part_max(WS8[:, 0:1], XSC[0:1, 1:2], self.t["bncap3"],
                               st[("c", 1)], "ws")
            P.wait("vector", mx)
            P.run("vector", _ts(XSC[0:1, 1:2], XSC[0:1, 1:2], EPS_FP8, None,
                                ALU.max))
            P.run("vector", _recip(XSC[0:1, 4:5], XSC[0:1, 1:2]))
            wsi = P.inc("vector", _tt(XSC[0:1, 5:6], XSC[0:1, 0:1],
                                      XSC[0:1, 1:2], ALU.mult))
            ce0 = self.bcast_col(XSC[0:1, 5:6], COLS[:, 1:2],
                                 self.t["bncap3"], wsi)
            st["bc"] = self.bcast_col(XSC[0:1, 4:5], COLS[:, 2:3],
                                      self.t["bncap4"], ce0)

        def p2(k):
            def f():
                slot = WFB[:, (k % 2) * 2048:(k % 2) * 2048 + 2048]
                q = WQ8[:, (k % 2) * 2048:(k % 2) * 2048 + 2048]
                ld = P.dma("sync", slot, ins["wft"][k * 128:(k + 1) * 128, :],
                           ev=(st.get(("q", k % 2)) or st.get(("c", k % 2))),
                           sem="wfl")
                P.wait("vector", ld)
                if k == 0:
                    P.wait("vector", st["bc"])
                qev = P.inc("vector", _ts(q, slot, COLS[:, 2:3], None,
                                          ALU.mult))
                st[("q", k % 2)] = P.dma(
                    "sync", ins["wq_scr"][k * 128:(k + 1) * 128, :], q,
                    ev=qev, sem="wqw")
                if k == KT - 1:
                    self._wq_ev = st[("q", k % 2)]
            return f

        for k in range(KT):
            steps.append(p1(k))
        steps.append(mid)
        for k in range(KT):
            steps.append(p2(k))
        self.wsteps = steps
        self.wsteps_i = 0

    def wstep(self):
        if self.wsteps_i < len(self.wsteps):
            f = self.wsteps[self.wsteps_i]
            self.wsteps_i += 1
            f()

    def fp8_weight_pass(self, ins):
        P = self.P
        WFB, WQ8 = self.t["WFB"], self.t["WQ8"]
        XSC, COLS, STT8 = self.t["XSC"], self.t["COLS"], self.t["STT8"]
        P.run("vector", _memset(STT8[:, 0:1], 0.0))
        lds = []
        prev = None
        for k in range(KT):
            ld = P.dma("sync", WFB[:], ins["wft"][k * 128:(k + 1) * 128, :],
                       ev=prev, sem="wfl")
            P.wait("vector", ld)
            P.run("vector", _red(STT8[:, 1:2], WFB[:], ALU.max, absval=True))
            prev = P.inc("vector", _tt(STT8[:, 0:1], STT8[:, 0:1], STT8[:, 1:2],
                                       ALU.max))
            lds.append((ld, prev))
        ws_ev = self.part_max(STT8[:, 0:1], XSC[0:1, 1:2], self.t["bncap"],
                              lds[-1][1], "ws")
        P.wait("vector", ws_ev)
        P.run("vector", _ts(XSC[0:1, 1:2], XSC[0:1, 1:2], EPS_FP8, None, ALU.max))
        P.run("vector", _recip(XSC[0:1, 4:5], XSC[0:1, 1:2]))
        wsi_ev = P.inc("vector", _tt(XSC[0:1, 5:6], XSC[0:1, 0:1],
                                     XSC[0:1, 1:2], ALU.mult))
        ce0 = self.bcast_col(XSC[0:1, 5:6], COLS[:, 1:2], self.t["bncap"], wsi_ev)
        ce = self.bcast_col(XSC[0:1, 4:5], COLS[:, 2:3], self.t["bncap2"], ce0)
        P.wait("vector", ce)
        prev_q = None
        for k in range(KT):
            ld = P.dma("sync", WFB[:], ins["wft"][k * 128:(k + 1) * 128, :],
                       ev=prev_q, sem="wfl")
            P.wait("vector", ld)
            qev = P.inc("vector", _ts(WQ8[:], WFB[:], COLS[:, 2:3], None,
                                      ALU.mult))
            prev_q = P.dma("sync", ins["wq_scr"][k * 128:(k + 1) * 128, :],
                           WQ8[:], ev=qev, sem="wqw")
        return prev_q

    def xs_finalize(self, xn_ev):
        P = self.P
        XSC, COLS, STT8 = self.t["XSC"], self.t["COLS"], self.t["STT8"]
        self.scan_flush()
        qev = P.ev("vector")
        qmax_ev = self.part_max(STT8[:, 7:8], XSC[0:1, 2:3], self.t["bncap"],
                                qev, "xs")
        P.wait("scalar", qmax_ev)
        P.run("scalar", _act(XSC[0:1, 0:1], XSC[0:1, 2:3], AF.Ln))
        xs_ev = P.inc("scalar", _act(XSC[0:1, 0:1], XSC[0:1, 0:1], AF.Exp,
                                     scale=0.5))
        P.wait("vector", xs_ev)
        P.run("vector", _ts(XSC[0:1, 0:1], XSC[0:1, 0:1], EPS_FP8, None, ALU.max))
        xi_ev = P.inc("vector", _recip(XSC[0:1, 3:4], XSC[0:1, 0:1]))
        c1 = self.bcast_col(XSC[0:1, 3:4], COLS[:, 0:1], self.t["bncap"], xi_ev)
        return c1

    def xq_quant(self, xn_ev, c2):
        P = self.P
        COLS = self.t["COLS"]
        XN, XQ = self.t["XN"], self.t["XQ"]
        P.wait("vector", c2)
        for k in range(KT):
            P.wait("vector", xn_ev[k])
            ev = P.inc("vector", _ts(XQ[:, k * 1024:(k + 1) * 1024],
                                     XN[:, k * TF + 3:(k + 1) * TF],
                                     COLS[:, 0:1], None, ALU.mult))
        return ev

    def build_all(self, ins, inp, out_ap, stop_after=None, dbg=None):
        nc, P = self.nc, self.P
        self.pp = [nc.alloc_psum_tensor("pp%d" % i, [128, 512], F32)
                   for i in range(8)]
        self.S("RING0", (128, TAPE_G), BF16)
        self.S("RING1", (128, TAPE_G), BF16)
        self.S("S1s", (128, TF), F32)
        self.S("S2s", (128, TF), F32)
        self.S("BNO", (128, 24), F32)
        self.S("STT8", (128, 8), F32)
        self.S("XSC", (1, 8), F32)
        self.S("M128", (1, 128), F32)
        self.S("COLS", (128, 8), F32)
        self.S("WS8", (128, 2), F32)
        self.t["bncap"] = ins["bnc1"]
        self.t["bncap2"] = ins["bnc2"]
        self.t["bncap3"] = ins["bnc3"]
        self.t["bncap4"] = ins["bnc4"]
        self.setup_consts(ins, inp)
        self.S("HALL", (64, D), F32)
        self.S("SCN", (64, 64), F32)
        self.S("XBCC", (128, 4096), BF16)
        self.S("XTOK", (128, 2048), BF16)
        self.S("BTOK", (128, 2048), BF16)
        self.S("NAT", (128, 64), BF16)
        self.S("DTE", (128, 32), F32)
        self.S("ABC", (64, 32), F32)
        self.S("BDB", (128, 1024), BF16)
        self.S("NAH", (32, T), BF16)
        self.S("NAL", (32, T), BF16)
        self.S("SZ", (128, KT * 1024), BF16)
        self.S("XN", (128, KT * TF), F32)
        self.S("H1T", (128, KT * TF), BF16)
        self.S("RW", (1, 1024), F32)
        self.S("RWB", (1, 2048), BF16)
        self.S("XE0", (128, TF), BF16)
        self.S("CVO0", (128, 1024), BF16)
        self.S("XSF", (128, 2048), F32, side="right")

        P.run("vector", _memset(self.t["HALL"][:], 0.0))
        hall0 = P.nop_inc("vector")
        self.scan_prepare(ins["xfull"])
        if stop_after == "consts":
            return self.dbg_dump(dbg, out_ap)

        WT = np.ascontiguousarray(np.asarray(inp["L0_attn_in_proj_w"], np.float32).T)
        cw = np.asarray(inp["L0_attn_conv_w"], np.float32)

        def conv_diag(f):
            ch0 = (f - 16) * 128
            out = []
            for tap in range(4):
                d = np.zeros((128, 128), np.float32)
                np.fill_diagonal(d, cw[ch0:ch0 + 128, tap])
                out.append(d)
            return out

        _, h1p_ev = self.ln_rms(ins["xpT"], "pf", gate_ev=self.cev)
        if stop_after == "pfln":
            return self.dbg_dump(dbg, out_ap)
        if stop_after and stop_after.startswith("pfj"):
            parts = stop_after.split("_")
            n = int(parts[1])
            self.noscan = True
            for tok in parts[2:]:
                if tok in ("a", "b", "c", "d"):
                    self.na_stop = tok
                if tok == "lo":
                    self.psetlo = True
            tiles = [64] if n == 0 else list(range(16, 16 + n))
            self.na_ev, _ = self.inproj(tiles, [None] * KT, WT, conv_diag,
                                        ins["pbc_scr"], 16, "pf")
            return self.dbg_dump(dbg, out_ap)
        if stop_after and stop_after.startswith("pfi"):
            parts = stop_after.split("_")
            n = int(parts[1])
            if "ns" in parts[2:]:
                self.noscan = True
            for tok in parts[2:]:
                if tok in ("a", "b", "c", "d"):
                    self.na_stop = tok
            tiles = [64] if n == 0 else list(range(16, 16 + n))
            self.na_ev, _ = self.inproj(tiles, h1p_ev, WT, conv_diag,
                                        ins["pbc_scr"], 16, "pf")
            return self.dbg_dump(dbg, out_ap)
        pf_tiles = list(range(16, 48)) + [64]
        self.na_ev, cvw_ev = self.inproj(pf_tiles, h1p_ev, WT, conv_diag,
                                         ins["pbc_scr"], 16, "pf")
        if stop_after == "pfinproj":
            return self.dbg_dump(dbg, out_ap)
        P.wait("vector", hall0)
        upd = self.ssd_pass(ins["pbc_scr"], 32, False, cvw_ev)
        P.wait("vector", upd)
        if stop_after == "pfssd":
            return self.dbg_dump(dbg, out_ap)
        P.wait("vector", self.const_ev["selc"])
        selev = P.inc("vector", _ts(self.t["HALL"][:], self.t["HALL"][:],
                                    self.t["selc"][:, 0:1], None, ALU.mult))
        xn_ev, h1_ev = self.ln_rms(ins["xmT"], "mn")
        if stop_after == "ln":
            return self.dbg_dump(dbg, out_ap)
        mn_tiles = list(range(16, 64)) + [64] + list(range(0, 16))
        self.na_ev, cvw_ev = self.inproj(mn_tiles, h1_ev, WT, conv_diag,
                                         ins["xbc_scr"], 16, "mn")
        if stop_after == "inproj":
            return self.dbg_dump(dbg, out_ap)
        c2 = self.xs_finalize(xn_ev)
        P.full_barrier()
        self.free("XSF", "CVO0", "XE0", "RWB", "RW")
        self.S("XQ", (128, KT * 1024), FP8, side="right")
        xq_ev = self.xq_quant(xn_ev, c2)
        xn_st = None
        for k in range(KT):
            P.wait("sync", xq_ev)
            xn_st = P.dma("sync", ins["xn_scr"][k * 128:(k + 1) * 128, :],
                          self.t["XN"][:, k * TF + 3:(k + 1) * TF], sem="xns")
        P.full_barrier()
        self.free("H1T", "XN")
        if stop_after == "xq":
            return self.dbg_dump(dbg, out_ap)
        self.S("WFB", (128, 4096), BF16)
        self.S("WQ8", (128, 4096), FP8)
        self.fp8_wsteps_init(ins)
        self.S("YG", (128, KT * 1024), BF16)
        self.S("SEG", (128, 1024), BF16)
        self.S("SCO", (128, 256), BF16)
        self.S("ECUM", (128, 256), BF16)
        self.S("CSB", (128, 256), BF16)
        self.S("AUT", (128, 1024), BF16)
        self.S("YT", (128, 256), F32)
        self.S("HPREV", (128, D), BF16)
        self.S("XBCC2", (128, 2048), BF16)
        upd = self.ssd_pass(ins["xbc_scr"], 48, True, cvw_ev)
        if stop_after == "ssd":
            return self.dbg_dump(dbg, out_ap)
        P.full_barrier()
        self.free("XBCC2", "HPREV", "YT", "AUT", "CSB", "ECUM", "SCO", "SEG")
        self.S("OH", (128, 2048), F32)
        self.S("E5", (128, 2048), BF16)
        WOT = np.ascontiguousarray(np.asarray(inp["L0_attn_out_proj_w"], np.float32).T)
        h2_ev, st_ev = self.outproj(WOT, ins, upd)
        if stop_after == "h2":
            return self.dbg_dump(dbg, out_ap)
        P.full_barrier()
        self.free("E5", "OH", "YG", "WQ8", "WFB", "SZ", "NAL", "NAH",
                  "BDB", "ABC", "DTE", "NAT", "BTOK", "XTOK", "XBCC", "SCN",
                  "HALL")
        self.S("H3", (128, KT * 1024), BF16)
        self.S("RW", (1, 1024), F32)
        self.S("RWB", (1, 2048), BF16)
        h3_ev = self.rms2_h3(ins, h2_ev, st_ev)
        P.full_barrier()
        self.free("RWB", "RW")
        if stop_after == "h3":
            return self.dbg_dump(dbg, out_ap)
        self.S("SGU", (128, 64 * 1024), BF16, side="right")
        WGT = np.ascontiguousarray(np.asarray(inp["L0_attn_gate_w"], np.float32).T)
        WUT = np.ascontiguousarray(np.asarray(inp["L0_attn_up_w"], np.float32).T)
        sgu_ev = self.ffn_gateup(WGT, WUT, h3_ev, None)
        if stop_after == "sgu":
            return self.dbg_dump(dbg, out_ap)
        P.full_barrier()
        self.free("H3")
        self.S("OUTB", (128, 2048), F32)
        self.S("WQB", (128, 2048), FP8)
        self.S("E8", (128, 1024), BF16)
        WDT = np.ascontiguousarray(np.asarray(inp["L0_attn_down_w"], np.float32).T)
        self.down_final(WDT, ins, sgu_ev, None, self._wq_ev, xq_ev, out_ap)
        return None

    def dbg_dump(self, dbg, out_ap):
        P = self.P
        if dbg is None:
            return None
        name, col0 = dbg
        P.barrier(("tensor", "vector", "scalar", "dsync", "dscal", "cvw",
                   "h2w0", "h2w1", "wqw", "xns", "xld"), ("sync",))
        if name.startswith("@"):
            dram = self.dram_by_name[name[1:]]
            if dram.dtype == F32:
                e = P.dma("sync", out_ap[0:128, :],
                          dram[col0:col0 + 128, 0:1024], sem="out0")
                P.wait("sync", e)
                return None
            eb = self.t["S2s"][:].bitcast(BF16)
            dbs = self.t["S1s"]
            e1 = P.dma("sync", eb[:, 0:1024], dram[col0:col0 + 128, 0:1024],
                       sem="out0")
            P.wait("scalar", e1)
            e2 = P.inc("scalar", _acp(dbs[:, 0:1024], eb[:, 0:1024]))
            e3 = P.dma("sync", out_ap[0:128, :], dbs[:, 0:1024],
                       ev=e2, sem="out0")
            P.wait("sync", e3)
            return None
        srct = self.t[name]
        if srct.ap().dtype != F32:
            dbs = self.t["S1s"]
            e2 = P.inc("vector", _cp(dbs[:, 0:1024],
                                     srct[:, col0:col0 + 1024]))
            e3 = P.dma("sync", out_ap[0:128, :], dbs[:, 0:1024],
                       ev=e2, sem="out0")
        else:
            np_ = srct.ap().shape[0]
            e3 = P.dma("sync", out_ap[0:np_, :], srct[:, col0:col0 + 1024],
                       sem="out0")
        P.wait("sync", e3)
        return None


def build_program(inp, stop_after=None, dbg=None):
    nc = bass.Bass("TRN2", target_bir_lowering=False)
    P = Prog(nc)
    mk = MK(nc, P)
    ins = {}
    mk_dram = {}
    def din(name, shape, dt, kind="ExternalInput"):
        ins[name] = nc.dram_tensor(name, list(shape), dt, kind=kind).ap()
        mk_dram[name] = ins[name]
    din("xmT", (D, TF), F32)
    din("xpT", (D, TF), F32)
    din("xfull", (8192, D), F32)
    din("wft", (D, D), BF16)
    din("selc", (64, 1), F32)
    for nm in ("c_ones_row", "c_oz_row"):
        din(nm, (1, 128), BF16)
    din("c_ones_col", (128, 1), BF16)
    for nm in ("c_onesf", "c_negf", "c_ut", "c_sln", "c_id"):
        din(nm, (128, 128), BF16)
    for nm in ("c_utx4n", "c_msk4"):
        din(nm, (128, 512), BF16)
    for nm in ("n1w", "n2w", "zbv"):
        din(nm, (128, 16), F32)
    din("cbv", (128, 48), F32)
    din("dcol", (128, 32), F32)
    din("cvals", (128, 7), F32)
    din("biasf", (128, 16), F32)
    din("xbc_scr", (6144, 1024), BF16, "Internal")
    din("pbc_scr", (4096, 1024), BF16, "Internal")
    din("h2_scr", (D, 1024), F32, "Internal")
    din("xn_scr", (D, 1024), F32, "Internal")
    din("wq_scr", (D, D), FP8, "Internal")
    din("bnc1", (128, 1), F32, "Internal")
    din("bnc2", (128, 1), F32, "Internal")
    din("bnc3", (128, 1), F32, "Internal")
    din("bnc4", (128, 1), F32, "Internal")
    out_ap = nc.dram_tensor("out", [D, T], F32, kind="ExternalOutput").ap()
    mk.dram_by_name = mk_dram
    with nc.Block() as block:
        mk.build_all(ins, inp, out_ap, stop_after=stop_after, dbg=dbg)
        ngrp_cols = None
        mk.tape._close() if mk.tape.used else None
        ncols = max(mk.tape.groups, 1) * TAPE_G
        tape_dram = nc.dram_tensor("tape", [128, ncols], F32,
                                   kind="ExternalInput").ap()
        tape_np = mk.tape_finalize(tape_dram)
        P.build(block)
    return nc, mk, tape_np


def make_core_inputs(inp, core, tape_np, mkhost):
    x = np.asarray(inp["L0_attn_hidden_states"], np.float32).reshape(8192, D)
    b, half = core // 2, core % 2
    base = b * 2048 + half * 1024
    xm = np.zeros((TF, D), np.float32)
    xm[3:, :] = x[base:base + T]
    if half == 1:
        xm[0:3, :] = x[base - 3:base]
    xmT = np.ascontiguousarray(xm.T)
    xp = np.zeros((TF, D), np.float32)
    if half == 1:
        xp[3:, :] = x[b * 2048:b * 2048 + T]
    xpT = np.ascontiguousarray(xp.T)
    sel = np.full((64, 1), float(half), np.float32)
    d = {"xmT": xmT, "xpT": xpT, "xfull": x, "selc": sel, "tape": tape_np}
    wft = np.ascontiguousarray(np.asarray(inp["L0_ffn_weight"]).T)
    d["wft"] = wft
    d.update(mkhost)
    return d


def kernel(**inp):
    nc, mk, tape_np = build_program(inp)
    in_maps = [make_core_inputs(inp, c, tape_np, mk.host) for c in range(8)]
    res = run_bass_kernel_spmd(nc, in_maps, core_ids=list(range(8)))
    out = np.zeros((4, 2048, D), np.float32)
    for c in range(8):
        b, half = c // 2, c % 2
        out[b, half * 1024:(half + 1) * 1024, :] = res.results[c]["out"].T
    return out


# version 10 (final)
# speedup vs baseline: 1.1761x; 1.0151x over previous
import numpy as np
import ml_dtypes
import concourse.bass as bass
import concourse.mybir as mybir
from concourse.bass_utils import run_bass_kernel_spmd

F32 = mybir.dt.float32
BF16 = mybir.dt.bfloat16
FP8 = mybir.dt.float8e4
AF = mybir.ActivationFunctionType
ALU = mybir.AluOpType
AX = mybir.AxisListType
BF = ml_dtypes.bfloat16

T = 1024
TF = T + 3
NCH = 8
D = 2048
KT = 16
NH = 32
DS = 64
HD = 64
FIN = 65
TAPE_G = 3072
NRING = 2
EPS_LN = 1e-5
EPS_RMS = 1e-5
EPS_FP8 = 1e-6
MSKNEG = -1e30

def _mm(out, l, r, st, sp):
    return lambda t: t.matmul(out, l, r, start=st, stop=sp)

def _tp(out, in_, idm):
    return lambda t: t.transpose(out, in_, idm)

def _cp(dst, src):
    return lambda e: e.tensor_copy(dst, src)

def _acp(dst, src):
    return lambda s: s.copy(dst, src)

def _act(dst, src, f, bias=0.0, scale=1.0):
    return lambda s: s.activation(dst, src, f, bias=bias, scale=scale)

def _tt(dst, a, b, op):
    return lambda v: v.tensor_tensor(dst, a, b, op)

def _ts(dst, a, s1, s2, op0, op1=ALU.bypass, acc=None):
    if s2 is None:
        return lambda v: v.tensor_scalar(dst, a, s1, None, op0, accum_out=acc)
    return lambda v: v.tensor_scalar(dst, a, s1, s2, op0, op1, accum_out=acc)

def _stt(dst, a, sc, b, op0, op1):
    return lambda v: v.scalar_tensor_tensor(dst, a, sc, b, op0, op1)

def _red(dst, src, op, absval=False):
    return lambda v: v.tensor_reduce(dst, src, AX.X, op,
                                     apply_absolute_value=absval)

def _bn(dst, src):
    return lambda v: v.bn_stats(dst, src)

def _bna(dst, src):
    return lambda v: v.bn_aggr(dst, src)

def _memset(dst, c):
    return lambda v: v.memset(dst, c)

def _recip(dst, src):
    return lambda v: v.reciprocal(dst, src)


class Prog:
    ENGS = ("sync", "scalar", "vector", "tensor", "gpsimd")

    def __init__(self, nc):
        self.nc = nc
        self.steps = {e: [] for e in self.ENGS}
        self.semctx = []
        self.sem = {}
        for k in self.ENGS + ("dsync", "dscal", "dgp"):
            cm = nc.semaphore("sem_" + k)
            self.sem[k] = cm.__enter__()
            self.semctx.append(cm)
        self.cnt = {k: 0 for k in self.sem}
        self.last_wait = {e: {k: 0 for k in self.sem} for e in self.ENGS}

    def wait(self, e, ev):
        if ev is None:
            return
        sem_name, val = ev
        if val <= 0:
            return
        if self.last_wait[e][sem_name] >= val:
            return
        self.last_wait[e][sem_name] = val
        s = self.sem[sem_name]
        self.steps[e].append(lambda eng, s=s, val=val: eng.wait_ge(s, val))

    SELFSYNC = ("vector", "scalar")

    def run(self, e, fn):
        if e in self.SELFSYNC:
            self.inc(e, fn)
        else:
            self.steps[e].append(fn)

    def inc(self, e, fn, n=1):
        if e in self.SELFSYNC:
            self.wait(e, (e, self.cnt[e]))
        self.cnt[e] += n
        s, v = self.sem[e], self.cnt[e]
        self.steps[e].append(lambda eng, fn=fn, s=s, n=n: fn(eng).then_inc(s, n))
        return (e, v)

    def nop_inc(self, e, n=1):
        if e in self.SELFSYNC:
            self.wait(e, (e, self.cnt[e]))
        self.cnt[e] += n
        s, v = self.sem[e], self.cnt[e]
        self.steps[e].append(lambda eng, s=s, n=n: eng.nop().then_inc(s, n))
        return (e, v)

    def get_sem(self, name):
        if name not in self.sem:
            cm = self.nc.semaphore("sem_" + name)
            self.sem[name] = cm.__enter__()
            self.semctx.append(cm)
            self.cnt[name] = 0
            for e in self.ENGS:
                self.last_wait[e][name] = 0
        return self.sem[name]

    def dma(self, q, out, in_, ev=None, sem=None, **kw):
        qs = sem or {"sync": "dsync", "scalar": "dscal", "gpsimd": "dgp"}[q]
        s = self.get_sem(qs)
        if ev is not None:
            self.wait(q, ev)
        self.cnt[qs] += 16
        v = self.cnt[qs]
        def fn(eng, out=out, in_=in_, s=s, kw=kw):
            eng.dma_start(out=out, in_=in_, **kw).then_inc(s, 16)
        self.steps[q].append(fn)
        return (qs, v)

    def ev(self, e):
        return (e, self.cnt.get(e, 0))

    def phase_start(self):
        self.barrier(("scalar", "vector"), ("tensor",))

    def full_barrier(self, waiters=("tensor", "vector", "scalar", "sync")):
        prods = [k for k in self.cnt if self.cnt[k] > 0]
        for w in waiters:
            for p in prods:
                if p != w:
                    self.wait(w, (p, self.cnt[p]))

    def barrier(self, producers, waiters):
        evs = [self.ev(p) for p in producers if p in self.cnt]
        for w in waiters:
            for ev in evs:
                self.wait(w, ev)

    def build(self, block):
        for e in self.ENGS:
            steps = self.steps[e]
            def runner(eng, steps=steps):
                for fn in steps:
                    fn(eng)
            getattr(block, e)(runner)


class Tape:
    def __init__(self):
        self.blocks = []
        self.groups = 0
        self.used = 0
        self.pend = []

    def add(self, blk):
        w = blk.shape[1]
        assert blk.shape[0] == 128 and w <= TAPE_G, blk.shape
        if self.used + w > TAPE_G:
            self._close()
        g, off = self.groups, self.used
        self.pend.append(np.ascontiguousarray(blk, dtype=np.float32))
        self.used += w
        return g, off, w

    def _close(self):
        if self.used < TAPE_G:
            self.pend.append(np.zeros((128, TAPE_G - self.used), np.float32))
        self.blocks.extend(self.pend)
        self.pend = []
        self.groups += 1
        self.used = 0

    def finish(self):
        if self.used or not self.blocks:
            self._close()
        return np.concatenate(self.blocks, axis=1)


class MK:

    def __init__(self, nc, P):
        self.nc = nc
        self.P = P
        self.tape = Tape()
        self.tape_cons = {}
        self.cur_grp = -1
        self.host = {}
        self.t = {}
        self._stat_evac = None
        self.pb6_free = None
        self.pb7_free = None
        self.wsteps = []
        self.wsteps_i = 0

    def S(self, name, shape, dt=F32, side=None):
        self._scnt = getattr(self, "_scnt", {})
        n = self._scnt.get(name, 0)
        self._scnt[name] = n + 1
        bname = name if n == 0 else "%s_%d" % (name, n)
        cm = (self.nc.sbuf_tensor(bname, list(shape), dt, side=side)
              if side else self.nc.sbuf_tensor(bname, list(shape), dt))
        h = cm.__enter__()
        self.t[name] = h
        self._cms = getattr(self, "_cms", {})
        self._cms[name] = cm
        return h

    def free(self, *names):
        for n in names:
            self._cms[n].__exit__(None, None, None)
            del self._cms[n]
            del self.t[n]

    def tt_tile(self, blk):
        g, off, w = self.tape.add(blk)
        P = self.P
        if g != self.cur_grp:
            cnow = P.cnt.get("tensor", 0)
            if self.cur_grp >= 0:
                self.tape_cons[self.cur_grp] = ("tensor", cnow + 1)
                assert g < NRING or cnow > self._sw_cnt, \
                    "no tensor inc between tape group switches"
            self._sw_cnt = cnow
            self.cur_grp = g
            P.get_sem("tp%d" % (g % NRING)) and None
            P.wait("tensor", ("tp%d" % (g % NRING), 16 * (g // NRING + 1)))
        ring = self.t["RING%d" % (g % NRING)]
        return ring[:, off:off + w]

    def tape_finalize(self, tape_dram_ap):
        P = self.P
        if self.cur_grp >= 0 and self.cur_grp not in self.tape_cons:
            self.tape_cons[self.cur_grp] = P.ev("tensor")
        tape_np = self.tape.finish()
        ngrp = self.tape.groups
        for g in range(ngrp):
            if g >= NRING:
                P.wait("gpsimd", self.tape_cons[g - NRING])
            ring = self.t["RING%d" % (g % NRING)]
            P.dma("gpsimd", ring[:], tape_dram_ap[:, g * TAPE_G:(g + 1) * TAPE_G],
                  sem="tp%d" % (g % NRING))
        return tape_np

    def setup_consts(self, ins, inp):
        P = self.P
        iv, jv = np.meshgrid(np.arange(128), np.arange(128), indexing="ij")
        ones_r = np.ones((1, 128), np.float32)
        cdefs = {
            "c_ones_row": (np.ones((1, 128)), BF16, (1, 128)),
            "c_ones_col": (np.ones((128, 1)), BF16, (128, 1)),
            "c_oz_row": (np.full((1, 128), 1.0 / D), BF16, (1, 128)),
            "c_onesf": (np.ones((128, 128)), BF16, (128, 128)),
            "c_negf": (-np.ones((128, 128)), BF16, (128, 128)),
            "c_ut": ((iv <= jv).astype(np.float32), BF16, (128, 128)),
            "c_sln": (-(iv > jv).astype(np.float32), BF16, (128, 128)),
            "c_id": (np.eye(128), BF16, (128, 128)),
            "c_utx4n": (np.tile(-(iv <= jv).astype(np.float32), (1, 4)), BF16, (128, 512)),
            "c_msk4": (np.tile(np.where(jv >= iv, 0.0, MSKNEG), (1, 4)), BF16, (128, 512)),
        }
        n1w = np.asarray(inp["L0_attn_norm1_w"], np.float32).reshape(16, 128).T
        n2w = np.asarray(inp["L0_attn_norm2_w"], np.float32).reshape(16, 128).T
        zbv = np.asarray(inp["L0_attn_z_bias"], np.float32).reshape(16, 128).T
        cbv = np.asarray(inp["L0_attn_conv_b"], np.float32).reshape(48, 128).T
        dcol = np.broadcast_to(np.asarray(inp["L0_attn_D"], np.float32)[None, :], (128, 32))
        bias = np.asarray(inp["L0_ffn_bias"]).astype(np.float32).reshape(16, 128).T
        cdefs.update({
            "n1w": (n1w, F32, (128, 16)), "n2w": (n2w, F32, (128, 16)),
            "zbv": (zbv, F32, (128, 16)), "cbv": (cbv, F32, (128, 48)),
            "dcol": (dcol, F32, (128, 32)), "biasf": (bias, F32, (128, 16)),
        })
        cvals = [0.0, 1.0, 1e-5, 1.0 / D, -0.5, -1.0, 0.5]
        self.host["cvals"] = np.tile(np.asarray([cvals], np.float32), (128, 1))
        self.S("cvals_s", (128, 8), F32)
        self.const_ev = {}
        for nm, (arr, dt, shape) in cdefs.items():
            npdt = BF if dt == BF16 else np.float32
            self.host[nm] = np.ascontiguousarray(arr, dtype=npdt).reshape(shape)
            self.S(nm + "_s", shape, dt)
            self.t[nm] = self.t[nm + "_s"]
            self.const_ev[nm] = P.dma("sync", self.t[nm][:], ins[nm], sem="cbulk")
        cv_ev = P.dma("sync", self.t["cvals_s"][:, 0:7], ins["cvals"], sem="cbulk")
        for i, v in enumerate(cvals):
            self.nc.const_aps.register(F32, v, self.t["cvals_s"][:, i:i + 1])
        self.S("selc_s", (64, 1), F32)
        self.t["selc"] = self.t["selc_s"]
        self.const_ev["selc"] = P.dma("sync", self.t["selc"][:], ins["selc"], sem="cbulk")
        self.cev = P.ev("cbulk")
        for nm in list(self.const_ev):
            self.const_ev[nm] = self.cev
        P.wait("scalar", self.cev)
        P.wait("vector", self.cev)

    def stats_grp(self, src_fn, dep_fn, c0, w, s1pair, s2row, scratch):
        P = self.P
        Eb = scratch
        pp = self.pp
        want_sum = s1pair is not None
        mm_prev = None
        for k in range(KT):
            srcap = src_fn(k)
            P.wait("scalar", dep_fn(k))
            P.wait("scalar", mm_prev)
            if want_sum:
                P.run("scalar", _acp(Eb[:, 0:w], srcap))
            ev_sq = P.inc("scalar", _act(Eb[:, 512:512 + w], srcap, AF.Square))
            P.wait("tensor", ev_sq)
            P.wait("tensor", self._stat_evac)
            if want_sum:
                P.inc("tensor", _mm(pp[0][0:1, 0:w], self.t["c_ones_col"][:, 0:1],
                                    Eb[:, 0:w], k == 0, k == KT - 1))
            P.inc("tensor", _mm(pp[1][0:1, 0:w], self.t["c_ones_col"][:, 0:1],
                                Eb[:, 512:512 + w], k == 0, k == KT - 1))
            mm_prev = P.ev("tensor")
        P.wait("vector", mm_prev)
        if want_sum:
            P.run("vector", _cp(s1pair[0][0:1, 0:w], pp[0][0:1, 0:w]))
            P.run("vector", _tt(s1pair[1][0:1, 0:w], pp[0][0:1, 0:w],
                                s1pair[0][0:1, 0:w], ALU.subtract))
        ev = P.inc("vector", _cp(s2row[0:1, 0:w], pp[1][0:1, 0:w]))
        self._stat_evac = ev
        return ev

    def row_hilo(self, row, hi, lo, wtot, dep):
        P = self.P
        P.wait("vector", dep)
        P.run("vector", _cp(hi, row))
        return P.inc("vector", _tt(lo, row, hi, ALU.subtract))

    def bcast2(self, lhs_row, hi, lo, psum, w, dep, start=True):
        P = self.P
        P.wait("tensor", dep)
        P.run("tensor", _mm(psum[:, 0:w], lhs_row[0:1, :], hi, start, False))
        return P.inc("tensor", _mm(psum[:, 0:w], lhs_row[0:1, :], lo, False, True))

    def ln_rms(self, x_dram, tag, gate_ev=None):
        P = self.P
        XN, H1T = self.t["XN"], self.t["H1T"]
        RW, RWB = self.t["RW"], self.t["RWB"]
        S1s, XE0 = self.t["S1s"], self.t["XE0"]
        pp = self.pp
        r = lambda i: RW[0:1, i * 512:(i + 1) * 512]
        b = lambda i: RWB[0:1, i * 512:(i + 1) * 512]
        P.phase_start()
        xev = []
        for k in range(KT):
            xev.append(P.dma("sync", XN[:, k * TF:(k + 1) * TF],
                             x_dram[k * 128:(k + 1) * 128, :],
                             ev=(gate_ev if k == 0 else None), sem="xld"))
        gl = [(0, 512), (512, 512), (1024, 3)]
        xall = xev[-1]
        xn_ev = [None] * KT
        prev_v = None
        for (c0, w) in gl:
            sev = self.stats_grp(
                lambda k: XN[:, k * TF + c0:k * TF + c0 + w],
                lambda k: xall, c0, w, (b(0), b(1)), r(0), XE0)
            P.run("vector", _tt(r(1)[0:1, 0:w], b(0)[0:1, 0:w], b(1)[0:1, 0:w],
                                ALU.add))
            P.run("vector", _tt(r(1)[0:1, 0:w], r(1)[0:1, 0:w], r(1)[0:1, 0:w],
                                ALU.mult))
            vev = P.inc("vector", _stt(r(1)[0:1, 0:w], r(1)[0:1, 0:w],
                                       -1.0 / D, r(0)[0:1, 0:w],
                                       ALU.mult, ALU.add))
            P.wait("scalar", vev)
            P.run("scalar", _act(r(1)[0:1, 0:w], r(1)[0:1, 0:w], AF.Ln,
                                 bias=EPS_LN, scale=1.0 / D))
            aev = P.inc("scalar", _act(r(1)[0:1, 0:w], r(1)[0:1, 0:w],
                                       AF.Exp, scale=-0.5))
            hev2 = self.row_hilo(r(1)[0:1, 0:w], b(2)[0:1, 0:w],
                                 b(3)[0:1, 0:w], w, aev)
            P.wait("tensor", hev2)
            P.wait("tensor", prev_v)
            P.run("tensor", _mm(pp[2][:, 0:w], self.t["c_oz_row"][0:1, :],
                                b(0)[0:1, 0:w], True, False))
            P.inc("tensor", _mm(pp[2][:, 0:w], self.t["c_oz_row"][0:1, :],
                                b(1)[0:1, 0:w], False, True))
            P.run("tensor", _mm(pp[3][:, 0:w], self.t["c_ones_row"][0:1, :],
                                b(2)[0:1, 0:w], True, False))
            P.inc("tensor", _mm(pp[3][:, 0:w], self.t["c_ones_row"][0:1, :],
                                b(3)[0:1, 0:w], False, True))
            bev = P.ev("tensor")
            P.wait("vector", bev)
            for k in range(KT):
                xs = XN[:, k * TF + c0:k * TF + c0 + w]
                P.run("vector", _stt(S1s[:, 0:w], xs, 1.0, pp[2][:, 0:w],
                                     ALU.mult, ALU.subtract))
                ev = P.inc("vector", _tt(xs, S1s[:, 0:w], pp[3][:, 0:w],
                                         ALU.mult))
            prev_v = P.ev("vector")
        for k in range(KT):
            xn_ev[k] = prev_v
        h1_ev = [None] * KT
        prev_v2 = None
        for (c0, w) in gl:
            sev1 = self.stats_grp(
                lambda k: XN[:, k * TF + c0:k * TF + c0 + w],
                lambda k: xn_ev[k], c0, w, None, r(0), XE0)
            P.wait("scalar", sev1)
            P.run("scalar", _act(r(1)[0:1, 0:w], r(0)[0:1, 0:w], AF.Ln,
                                 bias=EPS_RMS, scale=1.0 / D))
            aev1 = P.inc("scalar", _act(r(1)[0:1, 0:w], r(1)[0:1, 0:w],
                                        AF.Exp, scale=-0.5))
            hev3 = self.row_hilo(r(1)[0:1, 0:w], b(2)[0:1, 0:w],
                                 b(3)[0:1, 0:w], w, aev1)
            P.wait("tensor", hev3)
            P.wait("tensor", prev_v2)
            P.run("tensor", _mm(pp[3][:, 0:w], self.t["c_ones_row"][0:1, :],
                                b(2)[0:1, 0:w], True, False))
            bev = P.inc("tensor", _mm(pp[3][:, 0:w], self.t["c_ones_row"][0:1, :],
                                      b(3)[0:1, 0:w], False, True))
            P.wait("vector", bev)
            for k in range(KT):
                ev = P.inc("vector", _stt(
                    H1T[:, k * TF + c0:k * TF + c0 + w],
                    XN[:, k * TF + c0:k * TF + c0 + w],
                    self.t["n1w"][:, k:k + 1], pp[3][:, 0:w],
                    ALU.mult, ALU.mult))
            prev_v2 = P.ev("vector")
        for k in range(KT):
            h1_ev[k] = prev_v2
        return xn_ev, h1_ev

    def scan_prepare(self, xfull_dram):
        P = self.P
        self.scan_dmas = []
        XSF = self.t["XSF"]
        for i in range(64):
            for hf in range(2):
                self.scan_dmas.append(
                    (XSF[:, hf * 1024:(hf + 1) * 1024],
                     xfull_dram[i * 128:(i + 1) * 128,
                                hf * 1024:(hf + 1) * 1024]))
        self.scan_i = 0
        self.scan_loaded = [P.dma("scalar", *self.scan_dmas[0], sem="scn"),
                            P.dma("scalar", *self.scan_dmas[1], sem="scn")]
        P.run("vector", _memset(self.t["STT8"][:, 7:8], 0.0))

    def scan_step(self):
        if getattr(self, "noscan", False):
            return
        if self.scan_i >= 128:
            return
        P = self.P
        i = self.scan_i
        self.scan_i += 1
        hf = i % 2
        slot = self.t["XSF"][:, hf * 1024:(hf + 1) * 1024]
        BNO, S8 = self.t["BNO"], self.t["STT8"]
        P.wait("vector", self.scan_loaded[i])
        for j in range(2):
            P.run("vector", _bn(BNO[:, hf * 12 + j * 6:hf * 12 + (j + 1) * 6],
                                slot[:, j * 512:(j + 1) * 512]))
        dst = S8[:, 2 + hf:3 + hf]
        P.run("vector", _red(dst, slot[:], ALU.max))
        ev = P.inc("vector", _red(S8[:, 4 + hf:5 + hf], slot[:], ALU.min))
        if i + 2 < 128:
            self.scan_loaded.append(
                P.dma("scalar", *self.scan_dmas[i + 2], ev=ev, sem="scn"))
        if hf == 0:
            return
        P.run("vector", _bna(S8[:, 0:2], BNO[:, 0:24]))
        P.run("vector", _tt(S8[:, 2:3], S8[:, 2:3], S8[:, 3:4], ALU.max))
        P.run("vector", _tt(S8[:, 4:5], S8[:, 4:5], S8[:, 5:6], ALU.min))
        P.run("vector", _tt(S8[:, 3:4], S8[:, 2:3], S8[:, 0:1], ALU.subtract))
        P.run("vector", _tt(S8[:, 5:6], S8[:, 0:1], S8[:, 4:5], ALU.subtract))
        P.run("vector", _tt(S8[:, 3:4], S8[:, 3:4], S8[:, 5:6], ALU.max))
        P.run("vector", _tt(S8[:, 3:4], S8[:, 3:4], S8[:, 3:4], ALU.mult))
        P.run("vector", _ts(S8[:, 6:7], S8[:, 1:2], EPS_LN, None, ALU.add))
        P.run("vector", _recip(S8[:, 6:7], S8[:, 6:7]))
        P.run("vector", _tt(S8[:, 3:4], S8[:, 3:4], S8[:, 6:7], ALU.mult))
        P.run("vector", _tt(S8[:, 7:8], S8[:, 7:8], S8[:, 3:4], ALU.max))

    def scan_flush(self):
        while self.scan_i < 128:
            self.scan_step()

    def inproj(self, ftiles, h1_ev, WT, conv_diag, scr_dram, scr_base, tag):
        P = self.P
        pp = self.pp
        H1T = self.t["H1T"]
        P.phase_start()
        gl = [(0, 512), (512, 512), (1024, 3)]
        prev_cons = {}
        prev_conv = None
        prev_cvo = None
        prev_xe = [None, None]
        na_done = None
        for fi, f in enumerate(ftiles):
            self.scan_step()
            self.scan_step()
            if getattr(self, "psetlo", False):
                pset = (pp[0], pp[1], pp[2])
            else:
                pset = (pp[4], pp[5], pp[6]) if fi % 2 == 0 else (pp[7], pp[0], pp[1])
            wrows = 32 if f == 64 else 128
            P.wait("tensor", prev_cons.get(fi % 2))
            for k in range(KT):
                wt = self.tt_tile(WT[k * 128:(k + 1) * 128,
                                     f * 128:f * 128 + wrows])
                if k == 0:
                    for gi, (c0, w) in enumerate(gl):
                        P.wait("tensor", h1_ev[k])
                for gi, (c0, w) in enumerate(gl):
                    f1 = _mm(pset[gi][0:wrows, 0:w], wt,
                             H1T[:, k * TF + c0:k * TF + c0 + w],
                             k == 0, k == KT - 1)
                    (P.inc if k == KT - 1 else P.run)("tensor", f1)
            mm_ev = P.ev("tensor")
            if f == 64:
                na_done = self.na_chain(pset, mm_ev, tag)
                prev_cons[fi % 2] = na_done
            elif f < 16:
                P.wait("scalar", mm_ev)
                SZ = self.t["SZ"]
                zmap = [(0, 3, 509), (1, 0, 512), (2, 0, 3)]
                o = 0
                for gi, p0, w in zmap:
                    ev = P.inc("scalar", _act(
                        SZ[:, f * 1024 + o:f * 1024 + o + w],
                        pset[gi][0:128, p0:p0 + w], AF.Silu,
                        bias=self.t["zbv"][:, f:f + 1]))
                    o += w
                prev_cons[fi % 2] = ev
            else:
                par = 0
                XE = self.t["XE0"]
                CVO = self.t["CVO0"]
                P.wait("scalar", mm_ev)
                P.wait("scalar", prev_xe[par])
                for gi, (c0, w) in enumerate(gl):
                    P.run("scalar", _acp(XE[:, c0:c0 + w], pset[gi][0:128, 0:w]))
                xe_ev = P.ev("scalar")
                P.wait("tensor", xe_ev)
                P.wait("tensor", prev_conv)
                for tap in range(4):
                    wd = self.tt_tile(conv_diag(f)[tap])
                    f1 = _mm(pp[2][:, 0:512], wd,
                             XE[:, tap:tap + 512], tap == 0, tap == 3)
                    f2 = _mm(pp[3][:, 0:512], wd,
                             XE[:, 512 + tap:512 + tap + 512],
                             tap == 0, tap == 3)
                    (P.inc if tap == 3 else P.run)("tensor", f1)
                    (P.inc if tap == 3 else P.run)("tensor", f2)
                cv_ev = P.ev("tensor")
                P.wait("scalar", cv_ev)
                P.wait("scalar", prev_cvo)
                ci = f - scr_base
                P.run("scalar", _act(CVO[:, 0:512], pp[2][:, 0:512], AF.Identity,
                                     bias=self.t["cbv"][:, ci:ci + 1]))
                ev2 = P.inc("scalar", _act(CVO[:, 512:1024], pp[3][:, 0:512],
                                           AF.Identity,
                                           bias=self.t["cbv"][:, ci:ci + 1]))
                prev_conv = ev2
                dma_ev = P.dma("sync", scr_dram[ci * 128:(ci + 1) * 128, :],
                               CVO[:, 0:1024], ev=ev2, sem="cvw")
                prev_cvo = dma_ev
                prev_xe[par] = cv_ev
                prev_cons[fi % 2] = xe_ev
        return na_done, prev_cvo

    def na_chain(self, pset, mm_ev, tag):
        P = self.P
        NAH, NAL = self.t["NAH"], self.t["NAL"]
        S2s = self.t["S2s"]
        S1s = self.t["S1s"]
        gl = [(0, 512), (512, 512), (1024, 3)]
        na_stop = getattr(self, "na_stop", None)
        if na_stop == "a":
            return mm_ev
        P.wait("scalar", mm_ev)
        for gi, (c0, w) in enumerate(gl):
            P.run("scalar", _acp(S2s[:, c0:c0 + w], pset[gi][0:128, 0:w]))
        evac_ev = P.ev("scalar")
        if na_stop == "b":
            return evac_ev
        P.wait("vector", evac_ev)
        rel_ev = P.inc("vector", _ts(S1s[0:32, 0:TF], S2s[0:32, 0:TF], 0.0,
                                     None, ALU.max))
        P.wait("scalar", rel_ev)
        P.run("scalar", _act(S2s[0:32, 0:TF], S2s[0:32, 0:TF], AF.Abs))
        P.run("scalar", _act(S2s[0:32, 0:TF], S2s[0:32, 0:TF], AF.Exp, scale=-1.0))
        aev2 = P.inc("scalar", _act(S2s[0:32, 0:TF], S2s[0:32, 0:TF], AF.Ln,
                                    bias=1.0))
        if na_stop == "c":
            return aev2
        P.wait("vector", aev2)
        P.run("vector", _tt(S2s[0:32, 3:TF], S1s[0:32, 3:TF],
                            S2s[0:32, 3:TF], ALU.add))
        P.run("vector", _cp(NAH[0:32, 0:T], S2s[0:32, 3:TF]))
        ev = P.inc("vector", _tt(NAL[0:32, 0:T], S2s[0:32, 3:TF],
                                 NAH[0:32, 0:T], ALU.subtract))
        return ev

    def ssd_chunk_common(self, c, scr, ntiles, dep_scr):
        P = self.P
        XBCC = self.t["XBCC"]
        pb6 = self.pp[6].bitcast(BF16)
        pb7 = self.pp[7].bitcast(BF16)
        ev = {}
        scr_r = scr.rearrange("(j p) t -> p j t", p=128)
        dmas = []
        for q in range(2):
            dmas.append(P.dma(
                "sync",
                XBCC[:, q * 16 * 128:(q + 1) * 16 * 128].rearrange(
                    "p (j t) -> p j t", j=16),
                scr_r[:, q * 16:(q + 1) * 16, c * 128:(c + 1) * 128],
                ev=dep_scr, sem="rbk"))
        if ntiles > 32:
            dmas.append(P.dma(
                "sync",
                self.t["XBCC2"][:].rearrange("p (j t) -> p j t", j=16),
                scr_r[:, 32:48, c * 128:(c + 1) * 128],
                ev=dep_scr, sem="rbk"))
        ev["xbcc"] = dmas[-1]
        P.wait("tensor", dmas[-1])
        P.wait("tensor", self.const_ev["c_id"])
        cid = self.t["c_id"][:]
        pb_free = [self.pb6_free, self.pb7_free]
        for dst_name, base in (("XTOK", 0), ("BTOK", 16)):
            dst = self.t[dst_name]
            for half in range(2):
                pb = pb6 if half == 0 else pb7
                P.wait("tensor", pb_free[half])
                for j in range(8):
                    jt = base + half * 8 + j
                    P.inc("tensor", _tp(pb[:, j * 128:(j + 1) * 128],
                                        XBCC[:, jt * 128:(jt + 1) * 128],
                                        cid))
                tev = P.ev("tensor")
                P.wait("scalar", tev)
                eev = P.inc("scalar", _acp(
                    dst[:, half * 1024:(half + 1) * 1024], pb[:]))
                ev["tp_evac_%s%d" % (dst_name, half)] = eev
                pb_free[half] = eev
        self.pb6_free, self.pb7_free = pb_free
        NAT = self.t["NAT"]
        P.wait("tensor", self.pb6_free)
        P.wait("tensor", self.na_ev)
        P.inc("tensor", _tp(pb6[:, 0:32], self.t["NAH"][0:32, c * 128:(c + 1) * 128],
                            cid[0:32, 0:32]))
        P.inc("tensor", _tp(pb6[:, 32:64],
                            self.t["NAL"][0:32, c * 128:(c + 1) * 128],
                            cid[0:32, 0:32]))
        tev = P.ev("tensor")
        P.wait("scalar", tev)
        nat_ev = P.inc("scalar", _acp(NAT[:], pb6[:, 0:64]))
        ev["nat"] = nat_ev
        self.pb6_free = nat_ev
        pp2 = self.pp[2]
        P.wait("tensor", nat_ev)
        P.run("tensor", _mm(pp2[:, 0:32], self.t["c_sln"][:], NAT[:, 0:32],
                            True, False))
        P.inc("tensor", _mm(pp2[:, 0:32], self.t["c_sln"][:], NAT[:, 32:64],
                            False, True))
        P.run("tensor", _mm(pp2[0:64, 32:64], self.t["c_negf"][:][:, 0:64],
                            NAT[:, 0:32], True, False))
        P.inc("tensor", _mm(pp2[0:64, 32:64], self.t["c_negf"][:][:, 0:64],
                            NAT[:, 32:64], False, True))
        dte_mm = P.ev("tensor")
        P.wait("scalar", dte_mm)
        P.run("scalar", _act(self.t["DTE"][:], pp2[:, 0:32], AF.Exp))
        ev["dteA"] = P.inc("scalar", _act(self.t["ABC"][:], pp2[0:64, 32:64],
                                          AF.Exp))
        return ev

    def ssd_sa_heads(self, ev, upd_prev):
        P = self.P
        XTOK, BTOK = self.t["XTOK"], self.t["BTOK"]
        HALL = self.t["HALL"]
        pb = (self.pp[0], self.pp[1], self.pp[4], self.pp[5])
        if upd_prev is None:
            self._sa_cons = {}
        vev = None
        for g in range(4):
            P.wait("vector", ev["dteA"])
            P.wait("vector", ev["tp_evac_BTOK1"])
            P.wait("vector", self._sa_cons.get(("bd", g % 2)))
            bdg = self.t["BDB"][:, (g % 2) * 512:(g % 2) * 512 + 512]
            bdev = P.inc("vector", _tt(
                bdg.rearrange("p (a b) -> p a b", a=8),
                BTOK[:, g * 512:(g + 1) * 512].rearrange("p (a b) -> p a b",
                                                         a=8),
                self.t["DTE"][:, g * 8:(g + 1) * 8].unsqueeze(2)
                    .to_broadcast([128, 8, 64]),
                ALU.mult))
            P.wait("tensor", bdev)
            P.wait("tensor", self._sa_cons.get(g, ev["dteA"]))
            P.wait("tensor", ev["tp_evac_XTOK1"])
            for hh in range(8):
                h = g * 8 + hh
                (P.inc if hh == 7 else P.run)("tensor", _mm(
                    pb[g][0:64, hh * 64:(hh + 1) * 64],
                    bdg[:, hh * 64:(hh + 1) * 64],
                    XTOK[:, h * 64:(h + 1) * 64], True, True))
            sev = P.ev("tensor")
            hv = HALL[0:64, g * 512:(g + 1) * 512]
            P.run("vector", _tt(
                hv.rearrange("p (a b) -> p a b", a=8),
                hv.rearrange("p (a b) -> p a b", a=8),
                self.t["ABC"][:, g * 8:(g + 1) * 8].unsqueeze(2)
                    .to_broadcast([64, 8, 64]),
                ALU.mult))
            P.wait("vector", sev)
            vev = P.inc("vector", _tt(hv, hv, pb[g][0:64, 0:512], ALU.add))
            self._sa_cons[g] = vev
            self._sa_cons[("bd", g % 2)] = sev
        return vev

    def ssd_main_heads(self, c, ev, hprev_ev):
        P = self.P
        XTOK, BTOK, XBCC = self.t["XTOK"], self.t["BTOK"], self.t["XBCC"]
        HALL, HPREV, SCN = self.t["HALL"], self.t["HPREV"], self.t["SCN"]
        SEG, SCO, ECUM, CSB = self.t["SEG"], self.t["SCO"], self.t["ECUM"], self.t["CSB"]
        AUT, NAT, YT = self.t["AUT"], self.t["NAT"], self.t["YT"]
        SZ, YG = self.t["SZ"], self.t["YG"]
        pdif = (self.pp[0], self.pp[1])
        psm = (self.pp[2], self.pp[3])
        py = (self.pp[4], self.pp[5])
        cons = {"seg": {}, "sco": {}, "cs": {}, "ecum": {}, "psm": {},
                "py": {}, "aut": {}, "dif": {}}
        scan_prev = None
        for g in range(8):
            gp = g % 2
            P.wait("vector", ev["nat"])
            P.wait("vector", cons["aut"].get(0))
            auth = AUT[:, 0:512]
            autl = AUT[:, 512:1024]
            c4n = self.t["c_utx4n"][:].rearrange("p (a b) -> p a b", a=4)
            nah = NAT[:, g * 4:(g + 1) * 4].unsqueeze(2).to_broadcast([128, 4, 128])
            nal = NAT[:, 32 + g * 4:32 + (g + 1) * 4].unsqueeze(2).to_broadcast([128, 4, 128])
            P.run("vector", _tt(auth.rearrange("p (a b) -> p a b", a=4), c4n,
                                nah, ALU.mult))
            aut_ev = P.inc("vector", _tt(autl.rearrange("p (a b) -> p a b", a=4),
                                         c4n, nal, ALU.mult))
            P.wait("tensor", aut_ev)
            P.wait("tensor", cons["dif"].get(gp))
            P.run("tensor", _mm(pdif[gp][:], self.t["c_ut"][:], nah, True, False))
            P.run("tensor", _mm(pdif[gp][:], self.t["c_ut"][:], nal, False, False))
            P.run("tensor", _mm(pdif[gp][:], self.t["c_onesf"][:], auth, False, False))
            P.run("tensor", _mm(pdif[gp][:], self.t["c_onesf"][:], autl, False, False))
            dif_ev = P.inc("tensor", _mm(pdif[gp][:], self.t["c_id"][:],
                                         self.t["c_msk4"][:], False, True))
            P.wait("scalar", dif_ev)
            P.wait("scalar", cons["seg"].get(gp))
            seg_ev = P.inc("scalar", _act(SEG[:, gp * 512:(gp + 1) * 512],
                                          pdif[gp][:], AF.Exp))
            cons["dif"][gp] = seg_ev
            for hh in range(4):
                h = g * 4 + hh
                par = h % 2
                self.wstep()
                bh = XBCC[(h % 2) * 64:(h % 2) * 64 + 64,
                          (16 + h // 2) * 128:(17 + h // 2) * 128]
                ch = self.t["XBCC2"][(h % 2) * 64:(h % 2) * 64 + 64,
                                     (h // 2) * 128:(h // 2 + 1) * 128]
                xh = XBCC[(h % 2) * 64:(h % 2) * 64 + 64,
                          (h // 2) * 128:(h // 2 + 1) * 128]
                P.wait("tensor", cons["psm"].get(par, ev["dteA"]))
                mm_cb = P.inc("tensor", _mm(psm[par][0:128, 0:128], bh, ch,
                                            True, True))
                P.run("tensor", _mm(psm[par][0:64, 128:256],
                                    self.t["c_onesf"][:][:, 0:64],
                                    auth[:, hh * 128:(hh + 1) * 128], True, False))
                mm_e = P.inc("tensor", _mm(psm[par][0:64, 128:256],
                                           self.t["c_onesf"][:][:, 0:64],
                                           autl[:, hh * 128:(hh + 1) * 128],
                                           False, True))
                cons["aut"][0] = mm_e
                P.wait("scalar", mm_e)
                P.wait("scalar", cons["ecum"].get(par))
                rb = (h % 2) * 64
                ec_ev = P.inc("scalar", _act(
                    ECUM[rb:rb + 64, par * 128:(par + 1) * 128],
                    psm[par][0:64, 128:256], AF.Exp))
                P.wait("vector", seg_ev)
                P.wait("vector", mm_cb)
                P.wait("vector", cons["sco"].get(par))
                sco_ev = P.inc("vector", _tt(
                    SCO[:, par * 128:(par + 1) * 128],
                    SEG[:, gp * 512 + hh * 128:gp * 512 + (hh + 1) * 128],
                    psm[par][0:128, 0:128], ALU.mult))
                cons["seg"][gp] = sco_ev
                P.wait("vector", ec_ev)
                P.wait("vector", cons["cs"].get(par))
                cs_ev = P.inc("vector", _tt(
                    CSB[rb:rb + 64, par * 128:(par + 1) * 128],
                    ch, ECUM[rb:rb + 64, par * 128:(par + 1) * 128],
                    ALU.mult))
                cons["ecum"][par] = cs_ev
                P.wait("tensor", sco_ev)
                P.wait("tensor", cs_ev)
                P.wait("tensor", hprev_ev)
                P.wait("tensor", cons["py"].get(par))
                P.run("tensor", _mm(py[par][0:64, 0:128],
                                    XTOK[:, h * 64:(h + 1) * 64],
                                    SCO[:, par * 128:(par + 1) * 128], True, False))
                y_mm = P.inc("tensor", _mm(py[par][0:64, 0:128],
                                           HPREV[rb:rb + 64, h * 64:(h + 1) * 64],
                                           CSB[rb:rb + 64, par * 128:(par + 1) * 128],
                                           False, True))
                cons["sco"][par] = y_mm
                cons["cs"][par] = y_mm
                bd = self.t["BDB"][:, par * 64:(par + 1) * 64]
                P.wait("vector", ev["dteA"])
                bd_ev = P.inc("vector", _ts(bd, BTOK[:, h * 64:(h + 1) * 64],
                                            self.t["DTE"][:, h:h + 1], None,
                                            ALU.mult))
                P.wait("tensor", bd_ev)
                s_mm = P.inc("tensor", _mm(psm[par][0:64, 256:320], bd,
                                           XTOK[:, h * 64:(h + 1) * 64],
                                           True, True))
                P.wait("vector", y_mm)
                P.run("vector", _stt(YT[rb:rb + 64, par * 128:(par + 1) * 128],
                                     xh, self.t["dcol"][rb:rb + 64, h:h + 1],
                                     py[par][0:64, 0:128], ALU.mult, ALU.add))
                szsl = SZ[(h % 2) * 64:(h % 2) * 64 + 64,
                          (h // 2) * 1024 + c * 128:(h // 2) * 1024 + (c + 1) * 128]
                yg_ev = P.inc("vector", _tt(
                    YG[rb:rb + 64,
                       (h // 2) * 1024 + c * 128:(h // 2) * 1024 + (c + 1) * 128],
                    YT[rb:rb + 64, par * 128:(par + 1) * 128], szsl, ALU.mult))
                cons["py"][par] = yg_ev
                P.wait("vector", s_mm)
                P.run("vector", _ts(SCN[0:64, 0:64],
                                    HALL[0:64, h * 64:(h + 1) * 64],
                                    self.t["ABC"][:, h:h + 1], None, ALU.mult))
                scan_prev = P.inc("vector", _tt(HALL[0:64, h * 64:(h + 1) * 64],
                                                SCN[0:64, 0:64],
                                                psm[par][0:64, 256:320], ALU.add))
                cons["psm"][par] = scan_prev
        return scan_prev

    def ssd_pass(self, scr_ap, ntiles, main, dep_scr):
        P = self.P
        P.phase_start()
        upd = None
        for c in range(NCH):
            self.scan_step()
            dep = dep_scr[c] if isinstance(dep_scr, list) else dep_scr
            if upd is not None:
                P.barrier(("tensor", "vector", "scalar", "dsync", "dscal"), ("sync",))
            ev = self.ssd_chunk_common(c, scr_ap, ntiles, dep)
            if main:
                P.wait("vector", upd)
                P.run("vector", _cp(self.t["HPREV"][0:64, :], self.t["HALL"][:]))
                hprev_ev = P.inc("vector", _cp(self.t["HPREV"][64:128, :],
                                               self.t["HALL"][:]))
                upd = self.ssd_main_heads(c, ev, hprev_ev)
            else:
                upd = self.ssd_sa_heads(ev, upd)
        return upd

    def outproj(self, WOT, ins, upd):
        P = self.P
        YG = self.t["YG"]
        S1s, S2s, OH = self.t["S1s"], self.t["S2s"], self.t["OH"]
        E5 = self.t["E5"]
        pp = self.pp
        ppst = (pp[2], pp[3])
        st_mm = {}
        st_pend = None
        prev = {}
        prev_dma = {}
        h2_ev = [None] * KT

        def emit_stats():
            sq_ev, sdo = st_pend
            P.wait("tensor", sq_ev)
            for half in range(2):
                fm = _mm(ppst[half][0:1, 0:512], self.t["c_ones_col"][:, 0:1],
                         E5[:, (sdo % 2) * 1024 + half * 512:
                            (sdo % 2) * 1024 + (half + 1) * 512],
                         sdo == 0, sdo == KT - 1)
                (P.inc if sdo == KT - 1 else P.run)("tensor", fm)
            st_mm[sdo] = P.ev("tensor")

        P.phase_start()
        P.wait("tensor", upd)
        for do in range(KT):
            par = do % 2
            ps = (pp[0], pp[1]) if par == 0 else (pp[4], pp[5])
            Sb = S1s if par == 0 else S2s
            xrd = P.dma("sync", Sb[:, 0:1024],
                        ins["xn_scr"][do * 128:(do + 1) * 128, :],
                        ev=prev.get(par), sem="xr%d" % par)
            P.wait("tensor", prev.get(par))
            for k in range(KT):
                wt = self.tt_tile(WOT[k * 128:(k + 1) * 128,
                                      do * 128:(do + 1) * 128])
                for half in range(2):
                    f1 = _mm(ps[half][:], wt,
                             YG[:, k * 1024 + half * 512:k * 1024 + (half + 1) * 512],
                             k == 0, k == KT - 1)
                    (P.inc if k == KT - 1 else P.run)("tensor", f1)
            mm_ev = P.ev("tensor")
            if st_pend is not None:
                emit_stats()
                st_pend = None
            P.wait("vector", mm_ev)
            P.wait("vector", xrd)
            P.wait("vector", prev_dma.get(par))
            for half in range(2):
                ev = P.inc("vector", _tt(
                    OH[:, par * 1024 + half * 512:par * 1024 + (half + 1) * 512],
                    ps[half][:], Sb[:, half * 512:(half + 1) * 512], ALU.add))
            prev[par] = ev
            P.wait("scalar", ev)
            P.wait("scalar", st_mm.get(do - 2))
            eslot = E5[:, (do % 2) * 1024:(do % 2) * 1024 + 1024]
            P.run("scalar", _act(eslot[:, 0:512],
                                 OH[:, par * 1024:par * 1024 + 512], AF.Square))
            sq_ev = P.inc("scalar", _act(
                eslot[:, 512:1024],
                OH[:, par * 1024 + 512:par * 1024 + 1024], AF.Square))
            st_pend = (sq_ev, do)
            h2_ev[do] = P.dma("sync", ins["h2_scr"][do * 128:(do + 1) * 128, :],
                              OH[:, par * 1024:(par + 1) * 1024], ev=ev,
                              sem="h2w%d" % par)
            prev_dma[par] = h2_ev[do]
        emit_stats()
        return h2_ev, st_mm[KT - 1]

    def rms2_h3(self, ins, h2_ev, stats_ev):
        P = self.P
        RW, RWB = self.t["RW"], self.t["RWB"]
        S1s, S2s = self.t["S1s"], self.t["S2s"]
        H3 = self.t["H3"]
        pp = self.pp
        r = lambda i: RW[0:1, i * 512:(i + 1) * 512]
        b = lambda i: RWB[0:1, i * 512:(i + 1) * 512]
        P.phase_start()
        gl = [(0, 512), (512, 512)]
        h3_ev = [None] * KT
        rstd_done = []
        for gi, (c0, w) in enumerate(gl):
            ppsrc = pp[2] if gi == 0 else pp[3]
            P.wait("vector", stats_ev)
            vev = P.inc("vector", _cp(r(0)[0:1, 0:w], ppsrc[0:1, 0:w]))
            P.wait("scalar", vev)
            P.run("scalar", _act(r(1)[0:1, 0:w], r(0)[0:1, 0:w], AF.Ln,
                                 bias=EPS_RMS, scale=1.0 / D))
            aev = P.inc("scalar", _act(r(1)[0:1, 0:w], r(1)[0:1, 0:w],
                                       AF.Exp, scale=-0.5))
            hev = self.row_hilo(r(1)[0:1, 0:w], b(2)[0:1, 0:w],
                                b(3)[0:1, 0:w], w, aev)
            ppb = pp[6] if gi == 0 else pp[7]
            P.wait("tensor", hev)
            P.run("tensor", _mm(ppb[:, 0:w], self.t["c_ones_row"][0:1, :],
                                b(2)[0:1, 0:w], True, False))
            rstd_done.append(P.inc("tensor", _mm(
                ppb[:, 0:w], self.t["c_ones_row"][0:1, :],
                b(3)[0:1, 0:w], False, True)))
        P.wait("vector", rstd_done[0])
        P.wait("vector", rstd_done[1])
        prevc = {}
        for k in range(KT):
            par = k % 2
            Sb = S1s if par == 0 else S2s
            P.wait("sync", h2_ev[k])
            ld = P.dma("sync", Sb[:, 0:1024],
                       ins["h2_scr"][k * 128:(k + 1) * 128, :],
                       ev=prevc.get(par, rstd_done[1]), sem="r2%d" % par)
            P.wait("vector", ld)
            for gi, (c0, w) in enumerate(gl):
                ppb = pp[6] if gi == 0 else pp[7]
                ev = P.inc("vector", _stt(
                    H3[:, k * 1024 + c0:k * 1024 + c0 + w],
                    Sb[:, c0:c0 + w],
                    self.t["n2w"][:, k:k + 1], ppb[:, 0:w],
                    ALU.mult, ALU.mult))
            prevc[par] = ev
            h3_ev[k] = ev
        return h3_ev

    def ffn_gateup(self, WGT, WUT, h3_ev, sgu_gate_ev):
        P = self.P
        H1T, SGU = self.t["H3"], self.t["SGU"]
        S1s, S2s = self.t["S1s"], self.t["S2s"]
        pp = self.pp
        P.phase_start()
        prev = {}
        first = True
        ev = None
        for ff in range(64):
            self.scan_step()
            par = ff % 2
            ps = pp[0:4] if par == 0 else pp[4:8]
            P.wait("tensor", prev.get(par))
            if first:
                P.wait("vector", sgu_gate_ev)
            for k in range(KT):
                if first:
                    P.wait("tensor", h3_ev[k])
                wg = self.tt_tile(WGT[k * 128:(k + 1) * 128,
                                      ff * 128:(ff + 1) * 128])
                for half in range(2):
                    f1 = _mm(ps[half][:], wg,
                             H1T[:, k * 1024 + half * 512:k * 1024 + (half + 1) * 512],
                             k == 0, k == KT - 1)
                    P.inc("tensor", f1)
                wu = self.tt_tile(WUT[k * 128:(k + 1) * 128,
                                      ff * 128:(ff + 1) * 128])
                for half in range(2):
                    f1 = _mm(ps[2 + half][:], wu,
                             H1T[:, k * 1024 + half * 512:k * 1024 + (half + 1) * 512],
                             k == 0, k == KT - 1)
                    P.inc("tensor", f1)
            mm_ev = P.ev("tensor")
            first = False
            Sb = S1s if par == 0 else S2s
            P.wait("scalar", mm_ev)
            P.run("scalar", _act(Sb[:, 0:512], ps[0][:], AF.Silu))
            sev = P.inc("scalar", _act(Sb[:, 512:1024], ps[1][:], AF.Silu))
            P.wait("vector", sev)
            for half in range(2):
                ev = P.inc("vector", _tt(
                    SGU[:, ff * 1024 + half * 512:ff * 1024 + (half + 1) * 512],
                    Sb[:, half * 512:(half + 1) * 512], ps[2 + half][:], ALU.mult))
            prev[par] = ev
        return ev

    def down_final(self, WDT, ins, sgu_ev, h2st_ev, wq_ev, scal_ev, out_ap):
        P = self.P
        SGU, XQ = self.t["SGU"], self.t["XQ"]
        S1s, S2s, E8 = self.t["S1s"], self.t["S2s"], self.t["E8"]
        OUTB, WQB = self.t["OUTB"], self.t["WQB"]
        COLS = self.t["COLS"]
        pp = self.pp
        prev = {}
        prev_dma = {}
        prev_wq = None
        out_dmas = []
        P.phase_start()
        P.wait("tensor", sgu_ev)
        for do in range(KT):
            par = do % 2
            ps = pp[0:4] if par == 0 else pp[4:8]
            wq_rd = P.dma("sync", WQB[:].rearrange("p (j t) -> p j t", j=KT),
                          ins["wq_scr"].rearrange("(j p) t -> p j t", p=128)[
                              :, :, do * 128:(do + 1) * 128],
                          ev=(prev_wq if do else wq_ev), sem="wqr")
            Sb = S1s if par == 0 else S2s
            xrd = P.dma("sync", Sb[:, 0:1024],
                        ins["xmT"][do * 128:(do + 1) * 128, 3:TF],
                        ev=prev.get(par), sem="xr%d" % par)
            P.wait("tensor", prev.get(par))
            for fk in range(64):
                wt = self.tt_tile(WDT[fk * 128:(fk + 1) * 128,
                                      do * 128:(do + 1) * 128])
                for half in range(2):
                    f1 = _mm(ps[half][:], wt,
                             SGU[:, fk * 1024 + half * 512:fk * 1024 + (half + 1) * 512],
                             fk == 0, fk == 63)
                    P.inc("tensor", f1)
            P.wait("tensor", wq_rd)
            P.wait("tensor", scal_ev)
            for k in range(KT):
                for half in range(2):
                    f1 = _mm(ps[2 + half][:], WQB[:, k * 128:(k + 1) * 128],
                             XQ[:, k * 1024 + half * 512:k * 1024 + (half + 1) * 512],
                             k == 0, k == KT - 1)
                    (P.inc if k == KT - 1 else P.run)("tensor", f1)
            mm_ev = P.ev("tensor")
            prev_wq = mm_ev
            h2rd = P.dma("sync", OUTB[:, (1 - par) * 1024:(2 - par) * 1024],
                         ins["h2_scr"][do * 128:(do + 1) * 128, :],
                         ev=prev_dma.get(1 - par), sem="r2%d" % (1 - par))
            P.wait("vector", mm_ev)
            P.wait("vector", xrd)
            P.wait("vector", h2rd)
            P.wait("vector", prev_dma.get(par))
            for half in range(2):
                eb = E8[:, half * 512:(half + 1) * 512]
                P.run("vector", _ts(eb, ps[2 + half][:], COLS[:, 1:2], None,
                                    ALU.mult))
                P.run("vector", _ts(eb, eb, self.t["biasf"][:, do:do + 1],
                                    None, ALU.add))
                ob = OUTB[:, par * 1024 + half * 512:par * 1024 + (half + 1) * 512]
                hb = OUTB[:, (1 - par) * 1024 + half * 512:
                          (1 - par) * 1024 + (half + 1) * 512]
                P.run("vector", _tt(hb, ps[half][:], hb, ALU.add))
                P.run("vector", _tt(hb, hb, Sb[:, half * 512:(half + 1) * 512],
                                    ALU.add))
                ev = P.inc("vector", _tt(ob, hb, eb, ALU.add))
            prev[par] = ev
            out_dmas.append(P.dma(
                "sync", out_ap[do * 128:(do + 1) * 128, :],
                OUTB[:, par * 1024:(par + 1) * 1024], ev=ev,
                sem="out%d" % par))
            prev_dma[par] = out_dmas[-1]
        P.wait("sync", out_dmas[-1])
        return out_dmas

    def bcast_col(self, src_scalar, dst_col, bnc_row, dep):
        P = self.P
        e1 = P.dma("sync", bnc_row[0:1, 0:1], src_scalar, ev=dep, sem="bnc")
        return P.dma("sync", dst_col,
                     bnc_row[0:1, 0:1].to_broadcast([128, 1]), ev=e1, sem="bnc")

    def part_max(self, col, dst_scalar, bnc_row, dep, tag):
        P = self.P
        M128 = self.t["M128"]
        e1 = P.dma("sync", bnc_row[:, 0:1], col, ev=dep, sem="bnc")
        e2 = P.dma("sync", M128[0:1, :], bnc_row.rearrange("p o -> o p"), ev=e1,
                   sem="bnc")
        P.wait("vector", e2)
        return P.inc("vector", _red(dst_scalar, M128[0:1, :], ALU.max))

    def fp8_wsteps_init(self, ins):
        P = self.P
        WFB, WQ8 = self.t["WFB"], self.t["WQ8"]
        WS8, XSC, COLS = self.t["WS8"], self.t["XSC"], self.t["COLS"]
        st = {}
        steps = []

        def p1(k):
            def f():
                slot = WFB[:, (k % 2) * 2048:(k % 2) * 2048 + 2048]
                ld = P.dma("sync", slot, ins["wft"][k * 128:(k + 1) * 128, :],
                           ev=st.get(("c", k % 2)), sem="wfl")
                P.wait("vector", ld)
                if k == 0:
                    P.run("vector", _memset(WS8[:, 0:1], 0.0))
                P.run("vector", _red(WS8[:, 1:2], slot, ALU.max, absval=True))
                st[("c", k % 2)] = P.inc("vector", _tt(
                    WS8[:, 0:1], WS8[:, 0:1], WS8[:, 1:2], ALU.max))
            return f

        def mid():
            mx = self.part_max(WS8[:, 0:1], XSC[0:1, 1:2], self.t["bncap3"],
                               st[("c", 1)], "ws")
            P.wait("vector", mx)
            P.run("vector", _ts(XSC[0:1, 1:2], XSC[0:1, 1:2], EPS_FP8, None,
                                ALU.max))
            P.run("vector", _recip(XSC[0:1, 4:5], XSC[0:1, 1:2]))
            wsi = P.inc("vector", _tt(XSC[0:1, 5:6], XSC[0:1, 0:1],
                                      XSC[0:1, 1:2], ALU.mult))
            ce0 = self.bcast_col(XSC[0:1, 5:6], COLS[:, 1:2],
                                 self.t["bncap3"], wsi)
            st["bc"] = self.bcast_col(XSC[0:1, 4:5], COLS[:, 2:3],
                                      self.t["bncap4"], ce0)

        def p2(k):
            def f():
                slot = WFB[:, (k % 2) * 2048:(k % 2) * 2048 + 2048]
                q = WQ8[:, (k % 2) * 2048:(k % 2) * 2048 + 2048]
                ld = P.dma("sync", slot, ins["wft"][k * 128:(k + 1) * 128, :],
                           ev=(st.get(("q", k % 2)) or st.get(("c", k % 2))),
                           sem="wfl")
                P.wait("vector", ld)
                if k == 0:
                    P.wait("vector", st["bc"])
                qev = P.inc("vector", _ts(q, slot, COLS[:, 2:3], None,
                                          ALU.mult))
                st[("q", k % 2)] = P.dma(
                    "sync", ins["wq_scr"][k * 128:(k + 1) * 128, :], q,
                    ev=qev, sem="wqw")
                if k == KT - 1:
                    self._wq_ev = st[("q", k % 2)]
            return f

        for k in range(KT):
            steps.append(p1(k))
        steps.append(mid)
        for k in range(KT):
            steps.append(p2(k))
        self.wsteps = steps
        self.wsteps_i = 0

    def wstep(self):
        if self.wsteps_i < len(self.wsteps):
            f = self.wsteps[self.wsteps_i]
            self.wsteps_i += 1
            f()

    def fp8_weight_pass(self, ins):
        P = self.P
        WFB, WQ8 = self.t["WFB"], self.t["WQ8"]
        XSC, COLS, STT8 = self.t["XSC"], self.t["COLS"], self.t["STT8"]
        P.run("vector", _memset(STT8[:, 0:1], 0.0))
        lds = []
        prev = None
        for k in range(KT):
            ld = P.dma("sync", WFB[:], ins["wft"][k * 128:(k + 1) * 128, :],
                       ev=prev, sem="wfl")
            P.wait("vector", ld)
            P.run("vector", _red(STT8[:, 1:2], WFB[:], ALU.max, absval=True))
            prev = P.inc("vector", _tt(STT8[:, 0:1], STT8[:, 0:1], STT8[:, 1:2],
                                       ALU.max))
            lds.append((ld, prev))
        ws_ev = self.part_max(STT8[:, 0:1], XSC[0:1, 1:2], self.t["bncap"],
                              lds[-1][1], "ws")
        P.wait("vector", ws_ev)
        P.run("vector", _ts(XSC[0:1, 1:2], XSC[0:1, 1:2], EPS_FP8, None, ALU.max))
        P.run("vector", _recip(XSC[0:1, 4:5], XSC[0:1, 1:2]))
        wsi_ev = P.inc("vector", _tt(XSC[0:1, 5:6], XSC[0:1, 0:1],
                                     XSC[0:1, 1:2], ALU.mult))
        ce0 = self.bcast_col(XSC[0:1, 5:6], COLS[:, 1:2], self.t["bncap"], wsi_ev)
        ce = self.bcast_col(XSC[0:1, 4:5], COLS[:, 2:3], self.t["bncap2"], ce0)
        P.wait("vector", ce)
        prev_q = None
        for k in range(KT):
            ld = P.dma("sync", WFB[:], ins["wft"][k * 128:(k + 1) * 128, :],
                       ev=prev_q, sem="wfl")
            P.wait("vector", ld)
            qev = P.inc("vector", _ts(WQ8[:], WFB[:], COLS[:, 2:3], None,
                                      ALU.mult))
            prev_q = P.dma("sync", ins["wq_scr"][k * 128:(k + 1) * 128, :],
                           WQ8[:], ev=qev, sem="wqw")
        return prev_q

    def xs_finalize(self, xn_ev):
        P = self.P
        XSC, COLS, STT8 = self.t["XSC"], self.t["COLS"], self.t["STT8"]
        self.scan_flush()
        qev = P.ev("vector")
        qmax_ev = self.part_max(STT8[:, 7:8], XSC[0:1, 2:3], self.t["bncap"],
                                qev, "xs")
        P.wait("scalar", qmax_ev)
        P.run("scalar", _act(XSC[0:1, 0:1], XSC[0:1, 2:3], AF.Ln))
        xs_ev = P.inc("scalar", _act(XSC[0:1, 0:1], XSC[0:1, 0:1], AF.Exp,
                                     scale=0.5))
        P.wait("vector", xs_ev)
        P.run("vector", _ts(XSC[0:1, 0:1], XSC[0:1, 0:1], EPS_FP8, None, ALU.max))
        xi_ev = P.inc("vector", _recip(XSC[0:1, 3:4], XSC[0:1, 0:1]))
        c1 = self.bcast_col(XSC[0:1, 3:4], COLS[:, 0:1], self.t["bncap"], xi_ev)
        return c1

    def xq_quant(self, xn_ev, c2):
        P = self.P
        COLS = self.t["COLS"]
        XN, XQ = self.t["XN"], self.t["XQ"]
        P.wait("vector", c2)
        for k in range(KT):
            P.wait("vector", xn_ev[k])
            ev = P.inc("vector", _ts(XQ[:, k * 1024:(k + 1) * 1024],
                                     XN[:, k * TF + 3:(k + 1) * TF],
                                     COLS[:, 0:1], None, ALU.mult))
        return ev

    def build_all(self, ins, inp, out_ap, stop_after=None, dbg=None):
        nc, P = self.nc, self.P
        self.pp = [nc.alloc_psum_tensor("pp%d" % i, [128, 512], F32)
                   for i in range(8)]
        self.S("RING0", (128, TAPE_G), BF16)
        self.S("RING1", (128, TAPE_G), BF16)
        self.S("S1s", (128, TF), F32)
        self.S("S2s", (128, TF), F32)
        self.S("BNO", (128, 24), F32)
        self.S("STT8", (128, 8), F32)
        self.S("XSC", (1, 8), F32)
        self.S("M128", (1, 128), F32)
        self.S("COLS", (128, 8), F32)
        self.S("WS8", (128, 2), F32)
        self.t["bncap"] = ins["bnc1"]
        self.t["bncap2"] = ins["bnc2"]
        self.t["bncap3"] = ins["bnc3"]
        self.t["bncap4"] = ins["bnc4"]
        self.setup_consts(ins, inp)
        self.S("HALL", (64, D), F32)
        self.S("SCN", (64, 64), F32)
        self.S("XBCC", (128, 4096), BF16)
        self.S("XTOK", (128, 2048), BF16)
        self.S("BTOK", (128, 2048), BF16)
        self.S("NAT", (128, 64), BF16)
        self.S("DTE", (128, 32), F32)
        self.S("ABC", (64, 32), F32)
        self.S("BDB", (128, 1024), BF16)
        self.S("NAH", (32, T), BF16)
        self.S("NAL", (32, T), BF16)
        self.S("SZ", (128, KT * 1024), BF16)
        self.S("XN", (128, KT * TF), F32)
        self.S("H1T", (128, KT * TF), BF16)
        self.S("RW", (1, 1024), F32)
        self.S("RWB", (1, 2048), BF16)
        self.S("XE0", (128, TF), BF16)
        self.S("CVO0", (128, 1024), BF16)
        self.S("XSF", (128, 2048), F32, side="right")

        P.run("vector", _memset(self.t["HALL"][:], 0.0))
        hall0 = P.nop_inc("vector")
        self.scan_prepare(ins["xfull"])
        if stop_after == "consts":
            return self.dbg_dump(dbg, out_ap)

        WT = np.ascontiguousarray(np.asarray(inp["L0_attn_in_proj_w"], np.float32).T)
        cw = np.asarray(inp["L0_attn_conv_w"], np.float32)

        def conv_diag(f):
            ch0 = (f - 16) * 128
            out = []
            for tap in range(4):
                d = np.zeros((128, 128), np.float32)
                np.fill_diagonal(d, cw[ch0:ch0 + 128, tap])
                out.append(d)
            return out

        _, h1p_ev = self.ln_rms(ins["xpT"], "pf", gate_ev=self.cev)
        if stop_after == "pfln":
            return self.dbg_dump(dbg, out_ap)
        if stop_after and stop_after.startswith("pfj"):
            parts = stop_after.split("_")
            n = int(parts[1])
            self.noscan = True
            for tok in parts[2:]:
                if tok in ("a", "b", "c", "d"):
                    self.na_stop = tok
                if tok == "lo":
                    self.psetlo = True
            tiles = [64] if n == 0 else list(range(16, 16 + n))
            self.na_ev, _ = self.inproj(tiles, [None] * KT, WT, conv_diag,
                                        ins["pbc_scr"], 16, "pf")
            return self.dbg_dump(dbg, out_ap)
        if stop_after and stop_after.startswith("pfi"):
            parts = stop_after.split("_")
            n = int(parts[1])
            if "ns" in parts[2:]:
                self.noscan = True
            for tok in parts[2:]:
                if tok in ("a", "b", "c", "d"):
                    self.na_stop = tok
            tiles = [64] if n == 0 else list(range(16, 16 + n))
            self.na_ev, _ = self.inproj(tiles, h1p_ev, WT, conv_diag,
                                        ins["pbc_scr"], 16, "pf")
            return self.dbg_dump(dbg, out_ap)
        pf_tiles = list(range(16, 48)) + [64]
        self.na_ev, cvw_ev = self.inproj(pf_tiles, h1p_ev, WT, conv_diag,
                                         ins["pbc_scr"], 16, "pf")
        if stop_after == "pfinproj":
            return self.dbg_dump(dbg, out_ap)
        P.wait("vector", hall0)
        upd = self.ssd_pass(ins["pbc_scr"], 32, False, cvw_ev)
        P.wait("vector", upd)
        if stop_after == "pfssd":
            return self.dbg_dump(dbg, out_ap)
        P.wait("vector", self.const_ev["selc"])
        selev = P.inc("vector", _ts(self.t["HALL"][:], self.t["HALL"][:],
                                    self.t["selc"][:, 0:1], None, ALU.mult))
        xn_ev, h1_ev = self.ln_rms(ins["xmT"], "mn")
        if stop_after == "ln":
            return self.dbg_dump(dbg, out_ap)
        mn_tiles = list(range(16, 64)) + [64] + list(range(0, 16))
        self.na_ev, cvw_ev = self.inproj(mn_tiles, h1_ev, WT, conv_diag,
                                         ins["xbc_scr"], 16, "mn")
        if stop_after == "inproj":
            return self.dbg_dump(dbg, out_ap)
        c2 = self.xs_finalize(xn_ev)
        P.full_barrier()
        self.free("XSF", "CVO0", "XE0", "RWB", "RW")
        self.S("XQ", (128, KT * 1024), FP8, side="right")
        xq_ev = self.xq_quant(xn_ev, c2)
        xn_st = None
        for k in range(KT):
            P.wait("sync", xq_ev)
            xn_st = P.dma("sync", ins["xn_scr"][k * 128:(k + 1) * 128, :],
                          self.t["XN"][:, k * TF + 3:(k + 1) * TF], sem="xns")
        P.full_barrier()
        self.free("H1T", "XN")
        if stop_after == "xq":
            return self.dbg_dump(dbg, out_ap)
        self.S("WFB", (128, 4096), BF16)
        self.S("WQ8", (128, 4096), FP8)
        self.fp8_wsteps_init(ins)
        self.S("YG", (128, KT * 1024), BF16)
        self.S("SEG", (128, 1024), BF16)
        self.S("SCO", (128, 256), BF16)
        self.S("ECUM", (128, 256), BF16)
        self.S("CSB", (128, 256), BF16)
        self.S("AUT", (128, 1024), BF16)
        self.S("YT", (128, 256), F32)
        self.S("HPREV", (128, D), BF16)
        self.S("XBCC2", (128, 2048), BF16)
        upd = self.ssd_pass(ins["xbc_scr"], 48, True, cvw_ev)
        if stop_after == "ssd":
            return self.dbg_dump(dbg, out_ap)
        P.full_barrier()
        self.free("XBCC2", "HPREV", "YT", "AUT", "CSB", "ECUM", "SCO", "SEG")
        self.S("OH", (128, 2048), F32)
        self.S("E5", (128, 2048), BF16)
        WOT = np.ascontiguousarray(np.asarray(inp["L0_attn_out_proj_w"], np.float32).T)
        h2_ev, st_ev = self.outproj(WOT, ins, upd)
        if stop_after == "h2":
            return self.dbg_dump(dbg, out_ap)
        P.full_barrier()
        self.free("E5", "OH", "YG", "WQ8", "WFB", "SZ", "NAL", "NAH",
                  "BDB", "ABC", "DTE", "NAT", "BTOK", "XTOK", "XBCC", "SCN",
                  "HALL")
        self.S("H3", (128, KT * 1024), BF16)
        self.S("RW", (1, 1024), F32)
        self.S("RWB", (1, 2048), BF16)
        h3_ev = self.rms2_h3(ins, h2_ev, st_ev)
        P.full_barrier()
        self.free("RWB", "RW")
        if stop_after == "h3":
            return self.dbg_dump(dbg, out_ap)
        self.S("SGU", (128, 64 * 1024), BF16, side="right")
        WGT = np.ascontiguousarray(np.asarray(inp["L0_attn_gate_w"], np.float32).T)
        WUT = np.ascontiguousarray(np.asarray(inp["L0_attn_up_w"], np.float32).T)
        sgu_ev = self.ffn_gateup(WGT, WUT, h3_ev, None)
        if stop_after == "sgu":
            return self.dbg_dump(dbg, out_ap)
        P.full_barrier()
        self.free("H3")
        self.S("OUTB", (128, 2048), F32)
        self.S("WQB", (128, 2048), FP8)
        self.S("E8", (128, 1024), BF16)
        WDT = np.ascontiguousarray(np.asarray(inp["L0_attn_down_w"], np.float32).T)
        self.down_final(WDT, ins, sgu_ev, None, self._wq_ev, xq_ev, out_ap)
        return None

    def dbg_dump(self, dbg, out_ap):
        P = self.P
        if dbg is None:
            return None
        name, col0 = dbg
        P.barrier(("tensor", "vector", "scalar", "dsync", "dscal", "cvw",
                   "h2w0", "h2w1", "wqw", "xns", "xld"), ("sync",))
        if name.startswith("@"):
            dram = self.dram_by_name[name[1:]]
            if dram.dtype == F32:
                e = P.dma("sync", out_ap[0:128, :],
                          dram[col0:col0 + 128, 0:1024], sem="out0")
                P.wait("sync", e)
                return None
            eb = self.t["S2s"][:].bitcast(BF16)
            dbs = self.t["S1s"]
            e1 = P.dma("sync", eb[:, 0:1024], dram[col0:col0 + 128, 0:1024],
                       sem="out0")
            P.wait("scalar", e1)
            e2 = P.inc("scalar", _acp(dbs[:, 0:1024], eb[:, 0:1024]))
            e3 = P.dma("sync", out_ap[0:128, :], dbs[:, 0:1024],
                       ev=e2, sem="out0")
            P.wait("sync", e3)
            return None
        srct = self.t[name]
        if srct.ap().dtype != F32:
            dbs = self.t["S1s"]
            e2 = P.inc("vector", _cp(dbs[:, 0:1024],
                                     srct[:, col0:col0 + 1024]))
            e3 = P.dma("sync", out_ap[0:128, :], dbs[:, 0:1024],
                       ev=e2, sem="out0")
        else:
            np_ = srct.ap().shape[0]
            e3 = P.dma("sync", out_ap[0:np_, :], srct[:, col0:col0 + 1024],
                       sem="out0")
        P.wait("sync", e3)
        return None


def build_program(inp, stop_after=None, dbg=None):
    nc = bass.Bass("TRN2", target_bir_lowering=False)
    P = Prog(nc)
    mk = MK(nc, P)
    ins = {}
    mk_dram = {}
    def din(name, shape, dt, kind="ExternalInput"):
        ins[name] = nc.dram_tensor(name, list(shape), dt, kind=kind).ap()
        mk_dram[name] = ins[name]
    din("xmT", (D, TF), F32)
    din("xpT", (D, TF), F32)
    din("xfull", (8192, D), F32)
    din("wft", (D, D), BF16)
    din("selc", (64, 1), F32)
    for nm in ("c_ones_row", "c_oz_row"):
        din(nm, (1, 128), BF16)
    din("c_ones_col", (128, 1), BF16)
    for nm in ("c_onesf", "c_negf", "c_ut", "c_sln", "c_id"):
        din(nm, (128, 128), BF16)
    for nm in ("c_utx4n", "c_msk4"):
        din(nm, (128, 512), BF16)
    for nm in ("n1w", "n2w", "zbv"):
        din(nm, (128, 16), F32)
    din("cbv", (128, 48), F32)
    din("dcol", (128, 32), F32)
    din("cvals", (128, 7), F32)
    din("biasf", (128, 16), F32)
    din("xbc_scr", (6144, 1024), BF16, "Internal")
    din("pbc_scr", (4096, 1024), BF16, "Internal")
    din("h2_scr", (D, 1024), F32, "Internal")
    din("xn_scr", (D, 1024), F32, "Internal")
    din("wq_scr", (D, D), FP8, "Internal")
    din("bnc1", (128, 1), F32, "Internal")
    din("bnc2", (128, 1), F32, "Internal")
    din("bnc3", (128, 1), F32, "Internal")
    din("bnc4", (128, 1), F32, "Internal")
    out_ap = nc.dram_tensor("out", [D, T], F32, kind="ExternalOutput").ap()
    mk.dram_by_name = mk_dram
    with nc.Block() as block:
        mk.build_all(ins, inp, out_ap, stop_after=stop_after, dbg=dbg)
        ngrp_cols = None
        mk.tape._close() if mk.tape.used else None
        ncols = max(mk.tape.groups, 1) * TAPE_G
        tape_dram = nc.dram_tensor("tape", [128, ncols], F32,
                                   kind="ExternalInput").ap()
        tape_np = mk.tape_finalize(tape_dram)
        P.build(block)
    return nc, mk, tape_np


def make_core_inputs(inp, core, tape_np, mkhost):
    x = np.asarray(inp["L0_attn_hidden_states"], np.float32).reshape(8192, D)
    b, half = core // 2, core % 2
    base = b * 2048 + half * 1024
    xm = np.zeros((TF, D), np.float32)
    xm[3:, :] = x[base:base + T]
    if half == 1:
        xm[0:3, :] = x[base - 3:base]
    xmT = np.ascontiguousarray(xm.T)
    xp = np.zeros((TF, D), np.float32)
    if half == 1:
        xp[3:, :] = x[b * 2048:b * 2048 + T]
    xpT = np.ascontiguousarray(xp.T)
    sel = np.full((64, 1), float(half), np.float32)
    d = {"xmT": xmT, "xpT": xpT, "xfull": x, "selc": sel, "tape": tape_np}
    wft = np.ascontiguousarray(np.asarray(inp["L0_ffn_weight"]).T)
    d["wft"] = wft
    d.update(mkhost)
    return d


def kernel(**inp):
    nc, mk, tape_np = build_program(inp)
    in_maps = [make_core_inputs(inp, c, tape_np, mk.host) for c in range(8)]
    res = run_bass_kernel_spmd(nc, in_maps, core_ids=list(range(8)))
    out = np.zeros((4, 2048, D), np.float32)
    for c in range(8):
        b, half = c // 2, c % 2
        out[b, half * 1024:(half + 1) * 1024, :] = res.results[c]["out"].T
    return out
